# version 17
# baseline (speedup 1.0000x reference)
import numpy as np
import concourse.bass as bass
import concourse.mybir as mybir
from concourse.bass_utils import run_bass_kernel_spmd

F32 = mybir.dt.float32
BF16 = mybir.dt.bfloat16
AF = mybir.ActivationFunctionType
ALU = mybir.AluOpType

P = 128
NT = 1024
LH = 512
D = 4096
NCH = 32
DK = 1024
NHEAD = 8
NQH = 32
NE = 8
N_CORES = 8

SBUF_BYTES = 229376


class S:
    __slots__ = ("h", "n", "name")

    def __init__(self, h, name):
        self.h = h
        self.n = 0
        self.name = name


class Prog:
    def __init__(self):
        self.ops = []

    def op(self, eng, fn, waits=(), inc=None, by=1):
        for (s, v) in waits:
            assert v <= s.n, f"wait on future value {s.name} {v} > {s.n}"
        self.ops.append((eng, fn, [(s, v) for (s, v) in waits if v > 0], inc, by))
        if inc is not None:
            inc.n += (by if by is not None else 1)
            return inc.n
        return None


def build_program():
    nc = bass.Bass("TRN2", target_bir_lowering=False)

    d_xT = nc.dram_tensor("xT", [P, NCH * NT], F32, kind="ExternalInput").ap()
    d_wq = nc.dram_tensor("wq", [NCH, P, D], F32, kind="ExternalInput").ap()
    d_wo = nc.dram_tensor("wo", [NCH, P, D], F32, kind="ExternalInput").ap()
    d_wb = nc.dram_tensor("wb", [NCH, P, D], F32, kind="ExternalInput").ap()
    d_wk = nc.dram_tensor("wk", [4, P, 32 * 256], F32, kind="ExternalInput").ap()
    d_wv = nc.dram_tensor("wv", [4, P, 32 * 256], F32, kind="ExternalInput").ap()
    d_gw = nc.dram_tensor("gw", [P, NCH * NE], F32, kind="ExternalInput").ap()
    d_ics = nc.dram_tensor("ics", [NE, D], F32, kind="ExternalInput").ap()
    d_ocs = nc.dram_tensor("ocs", [NE, D], F32, kind="ExternalInput").ap()
    d_bqc = nc.dram_tensor("bqc", [P, NCH], F32, kind="ExternalInput").ap()
    d_boc = nc.dram_tensor("boc", [P, NCH], F32, kind="ExternalInput").ap()
    d_bfc = nc.dram_tensor("bfc", [P, NCH], F32, kind="ExternalInput").ap()
    d_bk = nc.dram_tensor("bk", [1, DK], F32, kind="ExternalInput").ap()
    d_bv = nc.dram_tensor("bv", [1, DK], F32, kind="ExternalInput").ap()
    d_out = nc.dram_tensor("outT", [NCH, P, NT], F32, kind="ExternalOutput").ap()
    cc_in = nc.dram_tensor("cc_in", [DK + NE, P], F32, kind="Internal")
    cc_out = nc.dram_tensor("cc_out", [DK + NE, P], F32, kind="Internal")

    resv = nc.alloc_sbuf_tensor("resv", [P, 212096], mybir.dt.uint8)
    BASE = (nc.lookup_mloc(resv).addr + 127) & ~127

    def at(name, shape, dtype, rel):
        off = BASE + rel
        end = off + shape[1] * (2 if dtype == BF16 else 4)
        assert end <= SBUF_BYTES, (name, off, end)
        return nc.alloc_sbuf_tensor_at(name, shape, dtype, offset=off)

    xn = at("xn", [P, NCH * NT], BF16, 0)
    qb = at("qb", [P, NCH * NT], BF16, 65536)
    arena = at("arena", [P, 16384], BF16, 131072)
    ktm = at("ktm", [P, 2 * 8 * 256], BF16, 163840)
    vtm = at("vtm", [P, 2 * 8 * 256], BF16, 172032)
    residb = at("residb", [P, 2 * NT], BF16, 163840)
    outst = at("outst", [P, 2 * NT], F32, 167936)
    scr = at("scr", [P, 6 * LH], F32, 176128)
    kvtb = at("kvtb", [P, NHEAD * P], BF16, 163840)
    rstd_sb = at("rstd_sb", [P, NT], BF16, 176128)
    vA = at("vA", [P, NT], BF16, 178176)
    vB = at("vB", [P, NT], BF16, 180224)
    pA = at("pA", [P, NT], BF16, 182272)
    pB = at("pB", [P, NT], BF16, 184320)
    lnAb = at("lnAb", [P, LH], BF16, 186368)
    qsc = at("qsc", [P, 4 * LH], BF16, 176128)
    r_mean = at("r_mean", [1, NT], F32, 188416)
    r_ex2 = at("r_ex2", [1, NT], F32, 192512)
    r_scr = at("r_scr", [1, NT], F32, 196608)
    routb = at("routb", [NE, NT], BF16, 200704)
    bkb = at("bkb", [1, DK], BF16, 202752)
    bvb = at("bvb", [1, DK], BF16, 204800)
    gwT = at("gwT", [P, NCH * NE], BF16, 206848)
    kvst = at("kvst", [P, 2 * P], F32, 207360)
    kvscr = at("kvscr", [P, 256], F32, 208384)
    xsqb = at("xsqb", [P, NT], BF16, 207360)
    mrsb = at("mrsb", [P, NT], BF16, 207360)
    expL = at("expL", [NE, NT], BF16, 207360)
    xsqb2 = at("xsqb2", [P, NT], BF16, 176128)
    ones_r = at("ones_r", [1, P], BF16, 209408)
    onesf = at("onesf", [1, 129], F32, 209664)
    ones_c = at("ones_c", [P, 1], BF16, 210208)
    bqcol = at("bqcol", [P, NCH], F32, 210240)
    bocol = at("bocol", [P, NCH], F32, 210368)
    bfcol = at("bfcol", [P, NCH], F32, 210496)
    ksb = at("ksb", [P, NHEAD], F32, 210624)
    ksb_r = at("ksb_r", [P, NHEAD], F32, 210656)
    epsb = at("epsb", [1, 1], F32, 210688)
    icst = at("icst", [NE, 2 * P], BF16, 210720)
    ocst = at("ocst", [NE, 2 * P], BF16, 211232)

    ps = [nc.alloc_psum_tensor(f"ps{i}", [P, LH], F32) for i in range(8)]

    pr = Prog()
    sems = []

    def mk(name):
        s = S(None, name)
        sems.append(s)
        return s

    sP = mk("sP")
    sX = mk("sX")
    sX1 = mk("sX1")
    sX2 = mk("sX2")
    sX3 = mk("sX3")
    sX4 = mk("sX4")
    sX5 = mk("sX5")
    sX6 = mk("sX6")
    sX7 = mk("sX7")
    aP = mk("aP")
    sWa = mk("sWa")
    sWb = mk("sWb")
    sBa = mk("sBa")
    sBb = mk("sBb")
    sIa = mk("sIa")
    sIb = mk("sIb")
    sOa = mk("sOa")
    sOb = mk("sOb")
    sKa = mk("sKa")
    sKb = mk("sKb")
    sKs = mk("sKs")
    sKR = mk("sKR")
    sRa = mk("sRa")
    sRb = mk("sRb")
    sSa = mk("sSa")
    sSb = mk("sSb")
    sQb = mk("sQb")
    tP = mk("tP")
    aS = mk("aS")
    aV = mk("aV")
    cC = mk("cC")

    ve_ones = pr.op("VE", lambda e: e.memset(ones_r[:, :], 1.0), inc=aV)
    pr.op("VE", lambda e: e.memset(onesf[:, :], 1.0), inc=aV)
    pr.op("VE", lambda e: e.memset(ones_c[:, :], 1.0), inc=aV)
    ve_ones = pr.op("VE", lambda e: e.memset(epsb[:, :], 1e-5), inc=aV)

    sXq = [sX, sX1, sX2, sX3, sX4, sX5, sX6, sX7]
    QW = 4 * NT
    g_x4 = []
    xh_done = []
    for qq in range(8):
        for h in range(2):
            k = 2 * qq + h
            w = [xh_done[k - 1]] if k >= 1 else []
            v = pr.op("GP", lambda e, qq=qq, h=h: e.dma_start(
                out=xn[h * 64:(h + 1) * 64, qq * QW:(qq + 1) * QW],
                in_=d_xT[h * 64:(h + 1) * 64, qq * QW:(qq + 1) * QW]),
                waits=w, inc=sXq[qq], by=16)
            xh_done.append((sXq[qq], v))
        g_x4.append(xh_done[2 * qq + 1][1])

    pr.op("GP", lambda e: e.dma_start(out=gwT[:, :], in_=d_gw), inc=sP, by=16)
    pr.op("GP", lambda e: e.dma_start(out=bkb[:, :], in_=d_bk), inc=sP, by=16)
    pr.op("GP", lambda e: e.dma_start(out=bvb[:, :], in_=d_bv), inc=sP, by=16)
    pr.op("GP", lambda e: e.dma_start(out=bqcol[:, :], in_=d_bqc), inc=sP, by=16)
    pr.op("GP", lambda e: e.dma_start(out=bocol[:, :], in_=d_boc), inc=sP, by=16)
    s_pre = pr.op("GP", lambda e: e.dma_start(out=bfcol[:, :], in_=d_bfc),
                  inc=sP, by=16)

    t_stats = []
    a_sq = []
    for c in range(NCH):
        xq = xsqb if c % 2 == 0 else xsqb2
        waits = [(sXq[c // 4], g_x4[c // 4])]
        if c >= 2:
            waits.append((tP, t_stats[c - 2]))
        a_sq.append(pr.op("SC", lambda e, c=c, xq=xq: e.activation(
            xq[:, :], xn[:, c * NT:(c + 1) * NT], AF.Square), waits=waits, inc=aS))
        for h in range(2):
            pr.op("TE", lambda e, c=c, h=h: e.matmul(
                ps[h][0:1, :], ones_c[:, :],
                xn[:, c * NT + h * LH: c * NT + (h + 1) * LH],
                start=(c == 0), stop=(c == NCH - 1)),
                waits=[(sXq[c // 4], g_x4[c // 4]), (aV, ve_ones)])
        pr.op("TE", lambda e, c=c, xq=xq: e.matmul(
            ps[2][0:1, :], ones_c[:, :], xq[:, 0:LH],
            start=(c == 0), stop=(c == NCH - 1)),
            waits=[(aS, a_sq[c])])
        t_stats.append(pr.op("TE", lambda e, c=c, xq=xq: e.matmul(
            ps[3][0:1, :], ones_c[:, :], xq[:, LH:NT],
            start=(c == 0), stop=(c == NCH - 1)), inc=tP))

    a_mean = None
    for h in range(2):
        a_mean = pr.op("SC", lambda e, h=h: e.activation(
            r_mean[0:1, h * LH:(h + 1) * LH], ps[h][0:1, :], AF.Copy,
            scale=1.0 / D), waits=[(tP, t_stats[-1])], inc=aS)
        a_mean = pr.op("SC", lambda e, h=h: e.activation(
            r_ex2[0:1, h * LH:(h + 1) * LH], ps[2 + h][0:1, :], AF.Copy,
            scale=1.0 / D), inc=aS)
    v_ms = pr.op("VE", lambda e: e.tensor_tensor(
        r_scr[0:1, :], r_mean[0:1, :], r_mean[0:1, :], ALU.mult),
        waits=[(aS, a_mean)], inc=aV)
    v_var = pr.op("VE", lambda e: e.tensor_tensor(
        r_scr[0:1, :], r_ex2[0:1, :], r_scr[0:1, :], ALU.subtract),
        waits=[(aV, v_ms)], inc=aV)
    a_lnv = pr.op("SC", lambda e: e.activation(
        r_ex2[0:1, :], r_scr[0:1, :], AF.Ln, bias=epsb[0:1, :]),
        waits=[(aV, v_var), (aV, ve_ones)], inc=aS)
    a_rstd = pr.op("SC", lambda e: e.activation(
        r_scr[0:1, :], r_ex2[0:1, :], AF.Exp, scale=-0.5),
        waits=[(aS, a_lnv)], inc=aS)
    v_mr = pr.op("VE", lambda e: e.tensor_tensor(
        r_mean[0:1, :], r_mean[0:1, :], r_scr[0:1, :], ALU.mult),
        waits=[(aS, a_rstd)], inc=aV)
    t_bc = None
    for h in range(2):
        pr.op("TE", lambda e, h=h: e.matmul(
            ps[4 + h][:, :], onesf[:, 1:129], r_scr[0:1, h * LH:(h + 1) * LH],
            start=True, stop=True), waits=[(aV, v_mr)])
        t_bc = pr.op("TE", lambda e, h=h: e.matmul(
            ps[6 + h][:, :], onesf[:, 1:129], r_mean[0:1, h * LH:(h + 1) * LH],
            start=True, stop=True), inc=tP)
    a_rsb = None
    for h in range(2):
        a_rsb = pr.op("SC", lambda e, h=h: e.activation(
            rstd_sb[:, h * LH:(h + 1) * LH], ps[4 + h][:, :], AF.Copy),
            waits=[(tP, t_bc)], inc=aS)
    a_mrb = None
    for h in range(2):
        a_mrb = pr.op("SC", lambda e, h=h: e.activation(
            mrsb[:, h * LH:(h + 1) * LH], ps[6 + h][:, :], AF.Copy),
            waits=[(tP, t_bc)], inc=aS)

    arena_h = [arena[:, 0:8192], arena[:, 8192:16384]]
    g_wkb = [pr.op("GP", lambda e: e.dma_start(
        out=arena_h[0], in_=d_wk[0]), inc=sWa, by=16)]
    g_wvb = [pr.op("GP", lambda e: e.dma_start(
        out=arena_h[1], in_=d_wv[0]), inc=sWb, by=16)]

    p_xn = []
    eng_cnt = {"VE": 0, "GP": 0}
    last_buf = {}
    for c in range(NCH):
        on_ve = (c % 3 != 0)
        eng, sem = ("VE", aV) if on_ve else ("GP", aP)
        k = eng_cnt[eng]
        eng_cnt[eng] += 1
        lsl = ([vA, vB] if on_ve else [pA, pB])[k % 2]
        cs = slice(c * NT, (c + 1) * NT)
        mw = [(aS, a_mrb)]
        if (eng, k % 2) in last_buf:
            mw.append((sem, last_buf[(eng, k % 2)]))
        vmul = pr.op(eng, lambda e, cs=cs, lsl=lsl: e.tensor_tensor(
            lsl[:, :], xn[:, cs], rstd_sb[:, :], ALU.mult),
            waits=mw, inc=sem)
        lastp = pr.op(eng, lambda e, cs=cs, lsl=lsl: e.tensor_tensor(
            xn[:, cs], lsl[:, :], mrsb[:, :], ALU.subtract),
            waits=[(sem, vmul)], inc=sem)
        last_buf[(eng, k % 2)] = lastp
        p_xn.append((sem, lastp))
    p_xn_all = [(aV, max(v for (s, v) in p_xn if s is aV)),
                (aP, max(v for (s, v) in p_xn if s is aP))]
    rel45 = [(aS, a_rsb)]

    kts = [ktm[:, 0:2048], ktm[:, 2048:4096]]
    vts = [vtm[:, 0:2048], vtm[:, 2048:4096]]
    prev_vphi = None
    t_kmm = []
    t_vmm = []
    t_kvh = []
    t_ksum = []
    s_pack = []
    evA = {}
    vphi_blk = []
    acp_blk = []
    for b in range(4):
        if b >= 1:
            g_wkb.append(pr.op("GP", lambda e, b=b: e.dma_start(
                out=arena_h[0], in_=d_wk[b]),
                waits=[(tP, t_kmm[b - 1])], inc=sWa, by=16))
            g_wvb.append(pr.op("GP", lambda e, b=b: e.dma_start(
                out=arena_h[1], in_=d_wv[b]),
                waits=[(tP, t_vmm[b - 1])], inc=sWb, by=16))
        for is_k in (True, False):
            gld = g_wkb[b] if is_k else g_wvb[b]
            sld = sWa if is_k else sWb
            ah = arena_h[0] if is_k else arena_h[1]
            dstT = (kts if is_k else vts)[b % 2]
            for lt in range(8):
                bank = lt % 4
                bw = [(sld, gld), (aV, ve_ones), (sP, s_pre)]
                bw += evA.get(bank, [])
                if b >= 2:
                    bw.append((tP, t_ksum[b - 2]))
                for c in range(NCH):
                    cw = (bw if c == 0 else []) + [p_xn[c]]
                    pr.op("TE", lambda e, c=c, lt=lt, ah=ah, bank=bank: e.matmul(
                        ps[bank][:, 0:256],
                        xn[:, c * NT + lt * P: c * NT + (lt + 1) * P],
                        ah[:, c * 256:(c + 1) * 256],
                        start=(c == 0), stop=False),
                        waits=cw)
                brow = (bkb if is_k else bvb)[0:1, b * 256:(b + 1) * 256]
                t_this = pr.op("TE", lambda e, brow=brow, bank=bank: e.matmul(
                    ps[bank][:, 0:256], ones_r[:, :], brow,
                    start=False, stop=True), inc=tP)
                dst = dstT[:, lt * 256:(lt + 1) * 256]
                if is_k:
                    ew = [(tP, t_this)]
                    if prev_vphi is not None:
                        ew.append((aV, prev_vphi))
                    pr.op("SC", lambda e, bank=bank: e.activation(
                        lnAb[:, 0:256], ps[bank][:, 0:256], AF.Exp),
                        waits=ew, inc=aS)
                    a_r = pr.op("SC", lambda e, bank=bank: e.activation(
                        lnAb[:, 256:512], ps[bank][:, 0:256], AF.Relu),
                        inc=aS)
                    prev_vphi = pr.op("VE", lambda e, dst=dst:
                                      e.scalar_tensor_tensor(
                                          dst, lnAb[:, 0:256], 1.0,
                                          lnAb[:, 256:512], ALU.min, ALU.add),
                                      waits=[(aS, a_r)], inc=aV)
                    evA[bank] = [(aS, a_r)]
                else:
                    a_cp = pr.op("SC", lambda e, dst=dst, bank=bank: e.activation(
                        dst, ps[bank][:, 0:256], AF.Copy),
                        waits=[(tP, t_this)], inc=aS)
                    evA[bank] = [(aS, a_cp)]
            if is_k:
                t_kmm.append(t_this)
                vphi_blk.append(prev_vphi)
            else:
                t_vmm.append(t_this)
                acp_blk.append(aS.n)
        t_last = None
        for sub in range(2):
            hh = 2 * b + sub
            bank = 4 + sub
            bw = [(aV, vphi_blk[b]), (aS, acp_blk[b])]
            if b == 0:
                bw += rel45
            if b >= 1:
                bw.append((aV, t_kvh[b - 1][1][sub]))
            for lt in range(8):
                t_last = pr.op(
                    "TE", lambda e, lt=lt, sub=sub, b=b, bank=bank: e.matmul(
                        ps[bank][:, 0:P],
                        kts[b % 2][:, lt * 256 + sub * P: lt * 256 + (sub + 1) * P],
                        vts[b % 2][:, lt * 256 + sub * P: lt * 256 + (sub + 1) * P],
                        start=(lt == 0), stop=(lt == 7)),
                    waits=bw if lt == 0 else [],
                    inc=(tP if (sub == 1 and lt == 7) else None))
        v_evs = []
        for sub in range(2):
            hh = 2 * b + sub
            sK = sKa if sub == 0 else sKb
            vw = [(tP, t_last)]
            if b >= 1:
                vw.append((sK, s_pack[2 * (b - 1) + sub]))
            v_ev = pr.op("VE", lambda e, sub=sub, bank=4 + sub: e.tensor_copy(
                kvst[:, sub * P:(sub + 1) * P], ps[bank][:, 0:P]),
                waits=vw, inc=aV)
            v_evs.append(v_ev)
            s_pack.append(pr.op("SY", lambda e, hh=hh, sub=sub: e.dma_start(
                out=cc_in.ap()[hh * P:(hh + 1) * P, :],
                in_=kvst[:, sub * P:(sub + 1) * P]),
                waits=[(aV, v_ev)], inc=sK, by=16))
        t_kvh.append((t_last, v_evs))
        ksbank = 6 + b // 2
        kcol = (b % 2) * 256
        t_kb = None
        for lt in range(8):
            kw = [(aV, vphi_blk[b])]
            if b == 0:
                kw.append((aS, a_mrb))
            t_kb = pr.op("TE", lambda e, lt=lt, b=b, ksbank=ksbank, kcol=kcol:
                         e.matmul(
                             ps[ksbank][0:1, kcol:kcol + 256], ones_c[:, :],
                             kts[b % 2][:, lt * 256:(lt + 1) * 256],
                             start=(lt == 0), stop=(lt == 7)),
                         waits=kw if lt == 0 else [],
                         inc=(tP if lt == 7 else None))
        t_ksum.append(t_kb)
    t_ks = t_ksum[-1]
    a_ks = None
    for h in range(2):
        a_ks = pr.op("SC", lambda e, h=h: e.activation(
            r_scr[0:1, h * LH:(h + 1) * LH], ps[6 + h][0:1, :], AF.Copy),
            waits=[(tP, t_ks)], inc=aS)
    s_ks = None
    for hh in range(NHEAD):
        s_ks = pr.op("SY", lambda e, hh=hh: e.dma_start(
            out=cc_in.ap()[DK + hh:DK + hh + 1, :],
            in_=r_scr[0:1, hh * P:(hh + 1) * P]),
            waits=[(aS, a_ks)], inc=sKs, by=16)

    cc_waits = [(sKs, s_ks), (sKa, sKa.n), (sKb, sKb.n)]

    colq4 = [arena[:, 4096 * i:4096 * (i + 1)] for i in range(4)]
    qsems = [sWa, sQb, sBa, sBb]
    t_qcol = []
    ev_q = {}
    g_qcol = []
    v_phi_q = []
    for j in range(NCH):
        sW = qsems[j % 4]
        if j < 2:
            waits = [(tP, t_kmm[3])]
        elif j < 4:
            waits = [(tP, t_vmm[3])]
        else:
            waits = [(tP, t_qcol[j - 4])]
        g_qcol.append(pr.op("GP", lambda e, j=j: e.dma_start(
            out=colq4[j % 4], in_=d_wq[j]), waits=waits, inc=sW, by=16))
        bank0, bank1 = (j % 2) * 2, (j % 2) * 2 + 1
        bw = [(sW, g_qcol[j])]
        for k in (bank0, bank1):
            bw += ev_q.get(k, [])
        tq = None
        for c in range(NCH):
            pr.op("TE", lambda e, j=j, c=c, bank0=bank0: e.matmul(
                ps[bank0][:, :], colq4[j % 4][:, c * P:(c + 1) * P],
                xn[:, c * NT: c * NT + LH],
                start=(c == 0), stop=(c == NCH - 1)),
                waits=bw if c == 0 else [])
            tq = pr.op("TE", lambda e, j=j, c=c, bank1=bank1: e.matmul(
                ps[bank1][:, :], colq4[j % 4][:, c * P:(c + 1) * P],
                xn[:, c * NT + LH: c * NT + NT],
                start=(c == 0), stop=(c == NCH - 1)),
                inc=(tP if c == NCH - 1 else None))
        t_qcol.append(tq)
        for h in range(2):
            bank = bank0 if h == 0 else bank1
            u = h
            s_e = qsc[:, (2 * u) * LH:(2 * u + 1) * LH]
            s_r = qsc[:, (2 * u + 1) * LH:(2 * u + 2) * LH]
            ew = [(tP, t_qcol[j]), (sP, s_pre)]
            if j >= 1:
                ew.append((aV, v_phi_q[2 * (j - 1) + h]))
            a_e = pr.op("SC", lambda e, bank=bank, j=j, s_e=s_e: e.activation(
                s_e, ps[bank][:, :], AF.Exp, bias=bqcol[:, j:j + 1]),
                waits=ew, inc=aS)
            a_r = pr.op("SC", lambda e, bank=bank, j=j, s_r=s_r: e.activation(
                s_r, ps[bank][:, :], AF.Relu, bias=bqcol[:, j:j + 1]), inc=aS)
            dst = qb[:, j * NT + h * LH: j * NT + (h + 1) * LH]
            vp = pr.op("VE", lambda e, dst=dst, s_e=s_e, s_r=s_r:
                       e.scalar_tensor_tensor(
                           dst, s_e, 1.0, s_r, ALU.min, ALU.add),
                       waits=[(aS, a_r)], inc=aV)
            v_phi_q.append(vp)
            ev_q[bank] = [(aS, a_r)]

        if j == 3:
            c_cc = pr.op("GP", lambda e: e.collective_compute(
                "AllReduce", ALU.add,
                replica_groups=[list(range(N_CORES))],
                ins=[cc_in.ap()], outs=[cc_out.ap()]),
                waits=cc_waits, inc=cC, by=1)

        if j == 4:
            g_kvtb = pr.op("GP", lambda e: e.dma_start(
                out=kvtb[:, :],
                in_=cc_out.ap()[0:DK, :].rearrange("(h d) v -> d h v", d=P)),
                waits=[(cC, c_cc)], inc=sX, by=16)
            s_ksb = pr.op("SY", lambda e: e.dma_start(
                out=ksb[:, :],
                in_=cc_out.ap()[DK:DK + NE, :].rearrange("h v -> v h"),
                allow_slow_non_contiguous=True),
                waits=[(cC, c_cc)], inc=sKR, by=16)
            v_k1 = pr.op("VE", lambda e: e.tensor_single_scalar(
                ksb_r[:, :], ksb[:, :], 1e-10, ALU.add),
                waits=[(sKR, s_ksb)], inc=aV)
            pr.op("VE", lambda e: e.reciprocal(ksb_r[:, :], ksb_r[:, :]),
                  waits=[(aV, v_k1)], inc=aV)
            v_ksr = aV.n

        if j == 20:
            t_rt = []
            for h in range(2):
                tr = None
                for c in range(NCH):
                    tr = pr.op("TE", lambda e, h=h, c=c: e.matmul(
                        ps[6][0:8, :] if h == 0 else ps[7][0:8, :],
                        gwT[:, c * NE:(c + 1) * NE],
                        xn[:, c * NT + h * LH: c * NT + h * LH + LH],
                        start=(c == 0), stop=(c == NCH - 1)),
                        waits=([(sP, s_pre), (aS, a_ks), (cC, c_cc)]
                               if c == 0 else []),
                        inc=(tP if c == NCH - 1 else None))
                t_rt.append(tr)
        if j == 21:
            a_exp = []
            for h in range(2):
                a_exp.append(pr.op("SC", lambda e, h=h: e.activation(
                    expL[0:8, h * LH:(h + 1) * LH], ps[6 + h][0:8, :], AF.Exp),
                    waits=[(tP, t_rt[h])], inc=aS))
        if j == 22:
            t_z = []
            for h in range(2):
                t_z.append(pr.op("TE", lambda e, h=h: e.matmul(
                    ps[6 + h][0:1, :], ones_c[0:8, :],
                    expL[0:8, h * LH:(h + 1) * LH],
                    start=True, stop=True), waits=[(aS, a_exp[h])], inc=tP))
        if j == 23:
            a_z = None
            for h in range(2):
                a_z = pr.op("SC", lambda e, h=h: e.activation(
                    r_ex2[0:1, h * LH:(h + 1) * LH], ps[6 + h][0:1, :],
                    AF.Copy), waits=[(tP, t_z[h])], inc=aS)
            a_zl = pr.op("SC", lambda e: e.activation(
                r_mean[0:1, :], r_ex2[0:1, :], AF.Ln),
                waits=[(aS, a_z)], inc=aS)
            a_zi = pr.op("SC", lambda e: e.activation(
                r_mean[0:1, :], r_mean[0:1, :], AF.Exp, scale=-1.0),
                waits=[(aS, a_zl)], inc=aS)
        if j == 24:
            t_zb = []
            for h in range(2):
                t_zb.append(pr.op("TE", lambda e, h=h: e.matmul(
                    ps[6 + h][0:8, :], onesf[:, 1:9],
                    r_mean[0:1, h * LH:(h + 1) * LH],
                    start=True, stop=True), waits=[(aS, a_zi)], inc=tP))
            v_rout = None
            for h in range(2):
                v_rout = pr.op("VE", lambda e, h=h: e.tensor_tensor(
                    routb[0:8, h * LH:(h + 1) * LH],
                    expL[0:8, h * LH:(h + 1) * LH],
                    ps[6 + h][0:8, :], ALU.mult),
                    waits=[(tP, t_zb[h])], inc=aV)
    v_q_done = aV.n

    bank_ev = {b: list(ev_q.get(b, [])) for b in range(6)}
    bank_ev[6] = [(aV, v_rout)]
    bank_ev[7] = [(aV, v_rout)]

    t_attn_done = None
    for qh in range(NQH):
        kvh = qh // 4
        for h in range(2):
            u = 2 * qh + h
            bank = u % 6
            bw = [(sX, g_kvtb), (aV, v_phi_q[u])] + bank_ev.get(bank, [])
            t_a = pr.op("TE", lambda e, qh=qh, h=h, kvh=kvh, bank=bank: e.matmul(
                ps[bank][:, :], kvtb[:, kvh * P:(kvh + 1) * P],
                qb[:, qh * NT + h * LH: qh * NT + (h + 1) * LH],
                start=True, stop=True), waits=bw, inc=tP)
            t_attn_done = t_a
            dst = qb[:, qh * NT + h * LH: qh * NT + (h + 1) * LH]
            if u % 2 == 0:
                v_a = pr.op("VE", lambda e, dst=dst, bank=bank, kvh=kvh:
                            e.tensor_single_scalar(
                                dst, ps[bank][:, :], ksb_r[:, kvh:kvh + 1],
                                ALU.mult),
                            waits=[(tP, t_a), (aV, v_ksr)], inc=aV)
                bank_ev[bank] = [(aV, v_a)]
            else:
                s_a = pr.op("SC", lambda e, dst=dst, bank=bank, kvh=kvh:
                            e.activation(
                                dst, ps[bank][:, :], AF.Copy,
                                scale=ksb_r[:, kvh:kvh + 1]),
                            waits=[(tP, t_a), (aV, v_ksr)], inc=aS)
                bank_ev[bank] = [(aS, s_a)]
    attn_v_done = aV.n
    attn_s_done = aS.n

    g_ict = []
    for j in range(2):
        g_ict.append(pr.op("GP", lambda e, j=j: e.dma_start(
            out=icst[:, (j % 2) * P:(j % 2 + 1) * P],
            in_=d_ics[:, j * P:(j + 1) * P]),
            inc=(sIa if j % 2 == 0 else sIb), by=16))

    colw = [arena[:, 0:4096], arena[:, 4096:8192]]
    colb = [arena[:, 8192:12288], arena[:, 12288:16384]]
    g_wo = []
    g_wb = []
    g_oct = []
    g_res = []
    a_sgn = []
    for j in range(2):
        sW = sWa if j % 2 == 0 else sWb
        sB = sBa if j % 2 == 0 else sBb
        sR = sRa if j % 2 == 0 else sRb
        sO = sOa if j % 2 == 0 else sOb
        g_wo.append(pr.op("GP", lambda e, j=j: e.dma_start(
            out=colw[j % 2], in_=d_wo[j]),
            waits=[(tP, t_qcol[-1])], inc=sW, by=16))
        g_wb.append(pr.op("GP", lambda e, j=j: e.dma_start(
            out=colb[j % 2], in_=d_wb[j]),
            waits=[(tP, t_qcol[-1])], inc=sB, by=16))
        a_sgn.append(pr.op("VE", lambda e, j=j: e.tensor_scalar(
            colb[j % 2][:, :], colb[j % 2][:, :], 0.0, 0.5,
            ALU.is_ge, ALU.subtract),
            waits=[(sB, g_wb[j])], inc=aV))
        g_res.append(pr.op("GP", lambda e, j=j: e.dma_start(
            out=residb[:, (j % 2) * NT:(j % 2 + 1) * NT],
            in_=d_xT[:, j * NT:(j + 1) * NT]),
            waits=[(tP, t_attn_done)], inc=sR, by=16))
        g_oct.append(pr.op("GP", lambda e, j=j: e.dma_start(
            out=ocst[:, (j % 2) * P:(j % 2 + 1) * P],
            in_=d_ocs[:, j * P:(j + 1) * P]), inc=sO, by=16))

    pre_thunks = []
    for h in range(2):
        u = h % 2
        bS = 2 if u == 0 else 5
        bO = 4 if u == 0 else 7
        bw0 = [(sWa, g_wo[0]), (aV, attn_v_done), (aS, attn_s_done)]
        bw0 += bank_ev[bS]
        for c in range(NCH):
            pre_thunks.append((
                "S", h,
                lambda e, h=h, c=c, bS=bS: e.matmul(
                    ps[bS][:, :], colw[0][:, c * P:(c + 1) * P],
                    qb[:, c * NT + h * LH: c * NT + (h + 1) * LH],
                    start=(c == 0), stop=(c == NCH - 1)),
                bw0 if c == 0 else [], c == NCH - 1))
        bwo = [(sOa, g_oct[0]), (aV, v_rout)] + bank_ev[bO]
        pre_thunks.append((
            "O", h,
            lambda e, h=h, bO=bO: e.matmul(
                ps[bO][:, :], ocst[0:8, 0:P],
                routb[0:8, h * LH:(h + 1) * LH],
                start=True, stop=True), bwo, True))

    pre_tS = {}
    pre_tO = {}
    pre_iter = iter(pre_thunks)

    def drain_pre(n):
        for _ in range(n):
            item = next(pre_iter, None)
            if item is None:
                return
            kind, h, fn, w, is_end = item
            v = pr.op("TE", fn, waits=w, inc=(tP if is_end else None))
            if is_end:
                (pre_tS if kind == "S" else pre_tO)[h] = v

    xs_banks = [6, 0, 1, 3]
    t_ic = []
    for j in range(NCH):
        sI = sIa if j % 2 == 0 else sIb
        if j >= 2:
            g_ict.append(pr.op("GP", lambda e, j=j: e.dma_start(
                out=icst[:, (j % 2) * P:(j % 2 + 1) * P],
                in_=d_ics[:, j * P:(j + 1) * P]),
                waits=[(tP, t_ic[j - 2])], inc=sI, by=16))
        for h in range(2):
            bank = xs_banks[(2 * j + h) % 4]
            bw = [(sI, g_ict[j]), (aV, v_rout), (tP, t_qcol[-1])]
            bw += bank_ev.get(bank, [])
            t_i = pr.op("TE", lambda e, j=j, h=h, bank=bank: e.matmul(
                ps[bank][:, :], icst[0:8, (j % 2) * P:(j % 2 + 1) * P],
                routb[0:8, h * LH:(h + 1) * LH],
                start=True, stop=True), waits=bw, inc=tP)
            sl = slice(j * NT + h * LH, j * NT + (h + 1) * LH)
            vx = pr.op("VE", lambda e, sl=sl, bank=bank: e.tensor_tensor(
                xn[:, sl], xn[:, sl], ps[bank][:, :], ALU.mult),
                waits=[(tP, t_i)], inc=aV)
            bank_ev[bank] = [(aV, vx)]
            if j >= 6:
                drain_pre(2)
        t_ic.append(t_i)
    drain_pre(len(pre_thunks))
    xs_v_done = aV.n

    t_scol = []
    t_mcol = []
    t_oc = []
    s_store = []
    ev_fin = {
        ("M", 0): bank_ev[3][0],
        ("M", 1): bank_ev[6][0],
    }
    v_out = {}
    for j in range(NCH):
        sW = sWa if j % 2 == 0 else sWb
        sB = sBa if j % 2 == 0 else sBb
        sR = sRa if j % 2 == 0 else sRb
        sO = sOa if j % 2 == 0 else sOb
        sS = sSa if j % 2 == 0 else sSb
        if j >= 2:
            g_wo.append(pr.op("GP", lambda e, j=j: e.dma_start(
                out=colw[j % 2], in_=d_wo[j]),
                waits=[(tP, t_scol[j - 2])], inc=sW, by=16))
            g_wb.append(pr.op("GP", lambda e, j=j: e.dma_start(
                out=colb[j % 2], in_=d_wb[j]),
                waits=[(tP, t_mcol[j - 2])], inc=sB, by=16))
            a_sgn.append(pr.op("VE", lambda e, j=j: e.tensor_scalar(
                colb[j % 2][:, :], colb[j % 2][:, :], 0.0, 0.5,
                ALU.is_ge, ALU.subtract),
                waits=[(sB, g_wb[j])], inc=aV))
            g_res.append(pr.op("GP", lambda e, j=j: e.dma_start(
                out=residb[:, (j % 2) * NT:(j % 2 + 1) * NT],
                in_=d_xT[:, j * NT:(j + 1) * NT]),
                waits=[(aV, v_out[(j - 2, 1)])], inc=sR, by=16))
            g_oct.append(pr.op("GP", lambda e, j=j: e.dma_start(
                out=ocst[:, (j % 2) * P:(j % 2 + 1) * P],
                in_=d_ocs[:, j * P:(j + 1) * P]),
                waits=[(tP, t_oc[j - 2])], inc=sO, by=16))
        for h in range(2):
            u = (2 * j + h) % 2
            bS, bM, bO = (2, 3, 4) if u == 0 else (5, 6, 7)
            s_os = scr[:, (3 * u) * LH:(3 * u + 1) * LH]
            s_P = scr[:, (3 * u + 1) * LH:(3 * u + 2) * LH]
            s_g = scr[:, (3 * u + 2) * LH:(3 * u + 3) * LH]
            if j == 0:
                tS = pre_tS[h]
                tO = pre_tO[h]
            else:
                bw = [(sW, g_wo[j]), (aV, attn_v_done), (aS, attn_s_done)]
                if ("S", u) in ev_fin:
                    bw.append(ev_fin[("S", u)])
                tS = None
                for c in range(NCH):
                    tS = pr.op("TE", lambda e, j=j, h=h, c=c, bS=bS: e.matmul(
                        ps[bS][:, :], colw[j % 2][:, c * P:(c + 1) * P],
                        qb[:, c * NT + h * LH: c * NT + (h + 1) * LH],
                        start=(c == 0), stop=(c == NCH - 1)),
                        waits=bw if c == 0 else [],
                        inc=(tP if c == NCH - 1 else None))
                bw = [(sO, g_oct[j]), (aV, v_rout)]
                if ("O", u) in ev_fin:
                    bw.append(ev_fin[("O", u)])
                tO = pr.op("TE", lambda e, j=j, h=h, bO=bO: e.matmul(
                    ps[bO][:, :], ocst[0:8, (j % 2) * P:(j % 2 + 1) * P],
                    routb[0:8, h * LH:(h + 1) * LH],
                    start=True, stop=True), waits=bw, inc=tP)
            if h == 1:
                t_scol.append(tS)
                t_oc.append(tO)
            bw = [(aV, a_sgn[j]), (aV, xs_v_done)]
            if ("M", u) in ev_fin:
                bw.append(ev_fin[("M", u)])
            tM = None
            for c in range(NCH):
                tM = pr.op("TE", lambda e, j=j, h=h, c=c, bM=bM: e.matmul(
                    ps[bM][:, :], colb[j % 2][:, c * P:(c + 1) * P],
                    xn[:, c * NT + h * LH: c * NT + (h + 1) * LH],
                    start=(c == 0), stop=(c == NCH - 1)),
                    waits=bw if c == 0 else [],
                    inc=(tP if c == NCH - 1 else None))
            if h == 1:
                t_mcol.append(tM)
            a_os = pr.op("SC", lambda e, bO=bO, s_os=s_os: e.activation(
                s_os, ps[bO][:, :], AF.Copy), waits=[(tP, tO)], inc=aS)
            v_P = pr.op("VE", lambda e, bM=bM, s_P=s_P, s_os=s_os: e.tensor_tensor(
                s_P, ps[bM][:, :], s_os, ALU.mult),
                waits=[(tP, tM), (aS, a_os)], inc=aV)
            a_g = pr.op("SC", lambda e, j=j, s_P=s_P, s_g=s_g: e.activation(
                s_g, s_P, AF.Silu, bias=bfcol[:, j:j + 1], scale=2.0),
                waits=[(aV, v_P), (sP, s_pre)], inc=aS)
            v_sg = pr.op("VE", lambda e, j=j, bS=bS, s_os=s_os, s_g=s_g:
                         e.scalar_tensor_tensor(
                             s_os, ps[bS][:, :], bocol[:, j:j + 1], s_g,
                             ALU.add, ALU.mult),
                         waits=[(tP, tS), (aS, a_g), (sP, s_pre)], inc=aV)
            ow = [(aV, v_sg), (sR, g_res[j])]
            if j >= 2:
                ow.append((sS, s_store[j - 2]))
            vo = pr.op("VE", lambda e, j=j, h=h, s_os=s_os: e.tensor_tensor(
                outst[:, (j % 2) * NT + h * LH:(j % 2) * NT + (h + 1) * LH],
                s_os, residb[:, (j % 2) * NT + h * LH:(j % 2) * NT + (h + 1) * LH],
                ALU.add), waits=ow, inc=aV)
            v_out[(j, h)] = vo
            ev_fin[("S", u)] = (aV, v_sg)
            ev_fin[("M", u)] = (aV, v_P)
            ev_fin[("O", u)] = (aS, a_os)
        s_store.append(pr.op("SY", lambda e, j=j: e.dma_start(
            out=d_out[j], in_=outst[:, (j % 2) * NT:(j % 2 + 1) * NT]),
            waits=[(aV, v_out[(j, 1)])], inc=sS, by=16))

    pr.op("SY", lambda e: e.nop(), waits=[(sSa, sSa.n), (sSb, sSb.n)])

    from contextlib import ExitStack
    with ExitStack() as ctx:
        for s in sems:
            s.h = ctx.enter_context(nc.semaphore(s.name))
        block = ctx.enter_context(nc.Block())

        def runner(key):
            def cb(eng):
                hw = {}
                for (k, fn, waits, inc, by) in pr.ops:
                    if k != key:
                        continue
                    for (s, v) in waits:
                        if hw.get(s.name, 0) < v:
                            eng.wait_ge(s.h, v)
                            hw[s.name] = v
                    inst = fn(eng)
                    if inc is not None and inst is not None:
                        if by is None:
                            inst.then_inc(inc.h)
                        else:
                            inst.then_inc(inc.h, by)
            return cb

        block.sync(runner("SY"))
        block.scalar(runner("SC"))
        block.vector(runner("VE"))
        block.tensor(runner("TE"))
        block.gpsimd(runner("GP"))

    return nc


_NC_CACHE = []


def kernel(**inputs):
    f = np.float32

    def g(name):
        return np.asarray(inputs[name], dtype=f)

    x = g("L0_attn_hidden_states")[0]
    Wq, bq = g("L0_attn_Wq"), g("L0_attn_bq")
    Wk, bk = g("L0_attn_Wk"), g("L0_attn_bk")
    Wv, bv = g("L0_attn_Wv"), g("L0_attn_bv")
    Wo, bo = g("L0_attn_Wo"), g("L0_attn_bo")
    Wf, bf = g("L0_ffn_weight"), g("L0_ffn_bias")
    gw = g("L0_ffn_gate_w")
    ics = g("L0_ffn_in_channel_scale")
    ocs = g("L0_ffn_out_channel_scale")

    def colt(W):
        O = W.shape[0]
        return np.ascontiguousarray(
            W.T.reshape(NCH, P, O // P, P).transpose(2, 1, 0, 3)
        ).reshape(O // P, P, NCH * P)

    wq_t = colt(Wq)
    wo_t = colt(Wo)
    wb_t = colt(Wf)

    def blkt(W):
        return np.ascontiguousarray(
            W.T.reshape(NCH, P, 4, 256).transpose(2, 1, 0, 3)
        ).reshape(4, P, NCH * 256)

    wk_t = blkt(Wk)
    wv_t = blkt(Wv)
    gw_t = np.ascontiguousarray(
        gw.T.reshape(NCH, P, NE).transpose(1, 0, 2)).reshape(P, NCH * NE)
    bqc = np.ascontiguousarray(bq.reshape(NCH, P).T)
    boc = np.ascontiguousarray(bo.reshape(NCH, P).T)
    bfc = np.ascontiguousarray(bf.reshape(NCH, P).T)
    bk2 = np.ascontiguousarray(bk.reshape(1, DK))
    bv2 = np.ascontiguousarray(bv.reshape(1, DK))
    ics = np.ascontiguousarray(ics)
    ocs = np.ascontiguousarray(ocs)

    xT = np.ascontiguousarray(x.T)
    in_maps = []
    for c in range(N_CORES):
        xc = np.ascontiguousarray(
            xT[:, c * NT:(c + 1) * NT].reshape(NCH, P, NT)
            .transpose(1, 0, 2)).reshape(P, NCH * NT)
        in_maps.append({
            "xT": xc, "wq": wq_t, "wo": wo_t, "wb": wb_t,
            "wk": wk_t, "wv": wv_t, "gw": gw_t, "ics": ics, "ocs": ocs,
            "bqc": bqc, "boc": boc, "bfc": bfc, "bk": bk2, "bv": bv2,
        })

    if not _NC_CACHE:
        _NC_CACHE.append(build_program())
    nc = _NC_CACHE[0]
    res = run_bass_kernel_spmd(nc, in_maps, core_ids=list(range(N_CORES)))

    out = np.empty((8192, D), dtype=f)
    for c in range(N_CORES):
        oc = np.asarray(res.results[c]["outT"], dtype=f)
        out[c * NT:(c + 1) * NT, :] = oc.reshape(D, NT).T
    return out.reshape(1, 8192, D)


# version 18
# speedup vs baseline: 1.0475x; 1.0475x over previous
import numpy as np
import concourse.bass as bass
import concourse.mybir as mybir
from concourse.bass_utils import run_bass_kernel_spmd

F32 = mybir.dt.float32
BF16 = mybir.dt.bfloat16
AF = mybir.ActivationFunctionType
ALU = mybir.AluOpType

P = 128
NT = 1024
LH = 512
D = 4096
NCH = 32
DK = 1024
NHEAD = 8
NQH = 32
NE = 8
N_CORES = 8

SBUF_BYTES = 229376


class S:
    __slots__ = ("h", "n", "name")

    def __init__(self, h, name):
        self.h = h
        self.n = 0
        self.name = name


class Prog:
    def __init__(self):
        self.ops = []

    def op(self, eng, fn, waits=(), inc=None, by=1):
        for (s, v) in waits:
            assert v <= s.n, f"wait on future value {s.name} {v} > {s.n}"
        self.ops.append((eng, fn, [(s, v) for (s, v) in waits if v > 0], inc, by))
        if inc is not None:
            inc.n += (by if by is not None else 1)
            return inc.n
        return None


def build_program():
    nc = bass.Bass("TRN2", target_bir_lowering=False)

    d_xT = nc.dram_tensor("xT", [P, NCH * NT], F32, kind="ExternalInput").ap()
    d_wq = nc.dram_tensor("wq", [NCH, P, D], F32, kind="ExternalInput").ap()
    d_wo = nc.dram_tensor("wo", [NCH, P, D], F32, kind="ExternalInput").ap()
    d_wb = nc.dram_tensor("wb", [NCH, P, D], F32, kind="ExternalInput").ap()
    d_wk = nc.dram_tensor("wk", [4, P, 32 * 256], F32, kind="ExternalInput").ap()
    d_wv = nc.dram_tensor("wv", [4, P, 32 * 256], F32, kind="ExternalInput").ap()
    d_gw = nc.dram_tensor("gw", [P, NCH * NE], F32, kind="ExternalInput").ap()
    d_ics = nc.dram_tensor("ics", [NE, D], F32, kind="ExternalInput").ap()
    d_ocs = nc.dram_tensor("ocs", [NE, D], F32, kind="ExternalInput").ap()
    d_bqc = nc.dram_tensor("bqc", [P, NCH], F32, kind="ExternalInput").ap()
    d_boc = nc.dram_tensor("boc", [P, NCH], F32, kind="ExternalInput").ap()
    d_bfc = nc.dram_tensor("bfc", [P, NCH], F32, kind="ExternalInput").ap()
    d_bk = nc.dram_tensor("bk", [1, DK], F32, kind="ExternalInput").ap()
    d_bv = nc.dram_tensor("bv", [1, DK], F32, kind="ExternalInput").ap()
    d_out = nc.dram_tensor("outT", [NCH, P, NT], F32, kind="ExternalOutput").ap()
    cc_in = nc.dram_tensor("cc_in", [DK + NE, P], F32, kind="Internal")
    cc_out = nc.dram_tensor("cc_out", [DK + NE, P], F32, kind="Internal")

    resv = nc.alloc_sbuf_tensor("resv", [P, 212096], mybir.dt.uint8)
    BASE = (nc.lookup_mloc(resv).addr + 127) & ~127

    def at(name, shape, dtype, rel):
        off = BASE + rel
        end = off + shape[1] * (2 if dtype == BF16 else 4)
        assert end <= SBUF_BYTES, (name, off, end)
        return nc.alloc_sbuf_tensor_at(name, shape, dtype, offset=off)

    xn = at("xn", [P, NCH * NT], BF16, 0)
    qb = at("qb", [P, NCH * NT], BF16, 65536)
    arena = at("arena", [P, 16384], BF16, 131072)
    ktm = at("ktm", [P, 2 * 8 * 256], BF16, 163840)
    vtm = at("vtm", [P, 2 * 8 * 256], BF16, 172032)
    residb = at("residb", [P, 2 * NT], BF16, 163840)
    outst = at("outst", [P, 2 * NT], F32, 167936)
    scr = at("scr", [P, 6 * LH], F32, 176128)
    qex = at("qex", [P, 4096], BF16, 163840)
    kvtb = at("kvtb", [P, NHEAD * P], BF16, 182272)
    rstd_sb = at("rstd_sb", [P, NT], BF16, 176128)
    vA = at("vA", [P, NT], BF16, 178176)
    vB = at("vB", [P, NT], BF16, 180224)
    pA = at("pA", [P, NT], BF16, 182272)
    pB = at("pB", [P, NT], BF16, 184320)
    lnAb = at("lnAb", [P, LH], BF16, 186368)
    qsc = at("qsc", [P, 4 * LH], BF16, 176128)
    r_mean = at("r_mean", [1, NT], F32, 188416)
    r_ex2 = at("r_ex2", [1, NT], F32, 192512)
    r_scr = at("r_scr", [1, NT], F32, 196608)
    routb = at("routb", [NE, NT], BF16, 200704)
    bkb = at("bkb", [1, DK], BF16, 202752)
    bvb = at("bvb", [1, DK], BF16, 204800)
    gwT = at("gwT", [P, NCH * NE], BF16, 206848)
    kvst = at("kvst", [P, 2 * P], F32, 207360)
    kvscr = at("kvscr", [P, 256], F32, 208384)
    xsqb = at("xsqb", [P, NT], BF16, 207360)
    mrsb = at("mrsb", [P, NT], BF16, 207360)
    expL = at("expL", [NE, NT], BF16, 207360)
    xsqb2 = at("xsqb2", [P, NT], BF16, 176128)
    ones_r = at("ones_r", [1, P], BF16, 209408)
    onesf = at("onesf", [1, 129], F32, 209664)
    ones_c = at("ones_c", [P, 1], BF16, 210208)
    bqcol = at("bqcol", [P, NCH], F32, 210240)
    bocol = at("bocol", [P, NCH], F32, 210368)
    bfcol = at("bfcol", [P, NCH], F32, 210496)
    ksb = at("ksb", [P, NHEAD], F32, 210624)
    ksb_r = at("ksb_r", [P, NHEAD], F32, 210656)
    epsb = at("epsb", [1, 1], F32, 210688)
    icst = at("icst", [NE, 2 * P], BF16, 210720)
    ocst = at("ocst", [NE, 2 * P], BF16, 211232)

    ps = [nc.alloc_psum_tensor(f"ps{i}", [P, LH], F32) for i in range(8)]

    pr = Prog()
    sems = []

    def mk(name):
        s = S(None, name)
        sems.append(s)
        return s

    sP = mk("sP")
    sX = mk("sX")
    sX1 = mk("sX1")
    sX2 = mk("sX2")
    sX3 = mk("sX3")
    sX4 = mk("sX4")
    sX5 = mk("sX5")
    sX6 = mk("sX6")
    sX7 = mk("sX7")
    aP = mk("aP")
    sWa = mk("sWa")
    sWb = mk("sWb")
    sBa = mk("sBa")
    sBb = mk("sBb")
    sIa = mk("sIa")
    sIb = mk("sIb")
    sOa = mk("sOa")
    sOb = mk("sOb")
    sKa = mk("sKa")
    sKb = mk("sKb")
    sKs = mk("sKs")
    sKR = mk("sKR")
    sRa = mk("sRa")
    sRb = mk("sRb")
    sSa = mk("sSa")
    sSb = mk("sSb")
    sQb = mk("sQb")
    sQc = mk("sQc")
    tP = mk("tP")
    aS = mk("aS")
    aV = mk("aV")
    cC = mk("cC")

    ve_ones = pr.op("VE", lambda e: e.memset(ones_r[:, :], 1.0), inc=aV)
    pr.op("VE", lambda e: e.memset(onesf[:, :], 1.0), inc=aV)
    pr.op("VE", lambda e: e.memset(ones_c[:, :], 1.0), inc=aV)
    ve_ones = pr.op("VE", lambda e: e.memset(epsb[:, :], 1e-5), inc=aV)

    sXq = [sX, sX1, sX2, sX3, sX4, sX5, sX6, sX7]
    QW = 4 * NT
    g_x4 = []
    for qq in range(8):
        g_x4.append(pr.op("GP", lambda e, qq=qq: e.dma_start(
            out=xn[:, qq * QW:(qq + 1) * QW],
            in_=d_xT[:, qq * QW:(qq + 1) * QW]), inc=sXq[qq], by=16))

    pr.op("GP", lambda e: e.dma_start(out=gwT[:, :], in_=d_gw), inc=sP, by=16)
    pr.op("GP", lambda e: e.dma_start(out=bkb[:, :], in_=d_bk), inc=sP, by=16)
    pr.op("GP", lambda e: e.dma_start(out=bvb[:, :], in_=d_bv), inc=sP, by=16)
    pr.op("GP", lambda e: e.dma_start(out=bqcol[:, :], in_=d_bqc), inc=sP, by=16)
    pr.op("GP", lambda e: e.dma_start(out=bocol[:, :], in_=d_boc), inc=sP, by=16)
    s_pre = pr.op("GP", lambda e: e.dma_start(out=bfcol[:, :], in_=d_bfc),
                  inc=sP, by=16)

    t_stats = []
    a_sq = []
    for c in range(NCH):
        xq = xsqb if c % 2 == 0 else xsqb2
        waits = [(sXq[c // 4], g_x4[c // 4])]
        if c >= 2:
            waits.append((tP, t_stats[c - 2]))
        a_sq.append(pr.op("SC", lambda e, c=c, xq=xq: e.activation(
            xq[:, :], xn[:, c * NT:(c + 1) * NT], AF.Square), waits=waits, inc=aS))
        for h in range(2):
            pr.op("TE", lambda e, c=c, h=h: e.matmul(
                ps[h][0:1, :], ones_c[:, :],
                xn[:, c * NT + h * LH: c * NT + (h + 1) * LH],
                start=(c == 0), stop=(c == NCH - 1)),
                waits=[(sXq[c // 4], g_x4[c // 4]), (aV, ve_ones)])
        pr.op("TE", lambda e, c=c, xq=xq: e.matmul(
            ps[2][0:1, :], ones_c[:, :], xq[:, 0:LH],
            start=(c == 0), stop=(c == NCH - 1)),
            waits=[(aS, a_sq[c])])
        t_stats.append(pr.op("TE", lambda e, c=c, xq=xq: e.matmul(
            ps[3][0:1, :], ones_c[:, :], xq[:, LH:NT],
            start=(c == 0), stop=(c == NCH - 1)), inc=tP))

    a_mean = None
    for h in range(2):
        a_mean = pr.op("SC", lambda e, h=h: e.activation(
            r_mean[0:1, h * LH:(h + 1) * LH], ps[h][0:1, :], AF.Copy,
            scale=1.0 / D), waits=[(tP, t_stats[-1])], inc=aS)
        a_mean = pr.op("SC", lambda e, h=h: e.activation(
            r_ex2[0:1, h * LH:(h + 1) * LH], ps[2 + h][0:1, :], AF.Copy,
            scale=1.0 / D), inc=aS)
    v_ms = pr.op("VE", lambda e: e.tensor_tensor(
        r_scr[0:1, :], r_mean[0:1, :], r_mean[0:1, :], ALU.mult),
        waits=[(aS, a_mean)], inc=aV)
    v_var = pr.op("VE", lambda e: e.tensor_tensor(
        r_scr[0:1, :], r_ex2[0:1, :], r_scr[0:1, :], ALU.subtract),
        waits=[(aV, v_ms)], inc=aV)
    a_lnv = pr.op("SC", lambda e: e.activation(
        r_ex2[0:1, :], r_scr[0:1, :], AF.Ln, bias=epsb[0:1, :]),
        waits=[(aV, v_var), (aV, ve_ones)], inc=aS)
    a_rstd = pr.op("SC", lambda e: e.activation(
        r_scr[0:1, :], r_ex2[0:1, :], AF.Exp, scale=-0.5),
        waits=[(aS, a_lnv)], inc=aS)
    v_mr = pr.op("VE", lambda e: e.tensor_tensor(
        r_mean[0:1, :], r_mean[0:1, :], r_scr[0:1, :], ALU.mult),
        waits=[(aS, a_rstd)], inc=aV)
    t_bc = None
    for h in range(2):
        pr.op("TE", lambda e, h=h: e.matmul(
            ps[4 + h][:, :], onesf[:, 1:129], r_scr[0:1, h * LH:(h + 1) * LH],
            start=True, stop=True), waits=[(aV, v_mr)])
        t_bc = pr.op("TE", lambda e, h=h: e.matmul(
            ps[6 + h][:, :], onesf[:, 1:129], r_mean[0:1, h * LH:(h + 1) * LH],
            start=True, stop=True), inc=tP)
    a_rsb = None
    for h in range(2):
        a_rsb = pr.op("SC", lambda e, h=h: e.activation(
            rstd_sb[:, h * LH:(h + 1) * LH], ps[4 + h][:, :], AF.Copy),
            waits=[(tP, t_bc)], inc=aS)
    a_mrb = None
    for h in range(2):
        a_mrb = pr.op("SC", lambda e, h=h: e.activation(
            mrsb[:, h * LH:(h + 1) * LH], ps[6 + h][:, :], AF.Copy),
            waits=[(tP, t_bc)], inc=aS)

    arena_h = [arena[:, 0:8192], arena[:, 8192:16384]]
    g_wkb = [pr.op("GP", lambda e: e.dma_start(
        out=arena_h[0], in_=d_wk[0]), inc=sWa, by=16)]
    g_wvb = [pr.op("GP", lambda e: e.dma_start(
        out=arena_h[1], in_=d_wv[0]), inc=sWb, by=16)]

    p_xn = []
    eng_cnt = {"VE": 0, "GP": 0}
    last_buf = {}
    for c in range(NCH):
        on_ve = True
        eng, sem = ("VE", aV) if on_ve else ("GP", aP)
        k = eng_cnt[eng]
        eng_cnt[eng] += 1
        lsl = ([vA, vB] if on_ve else [pA, pB])[k % 2]
        cs = slice(c * NT, (c + 1) * NT)
        mw = [(aS, a_mrb)]
        if (eng, k % 2) in last_buf:
            mw.append((sem, last_buf[(eng, k % 2)]))
        vmul = pr.op(eng, lambda e, cs=cs, lsl=lsl: e.tensor_tensor(
            lsl[:, :], xn[:, cs], rstd_sb[:, :], ALU.mult),
            waits=mw, inc=sem)
        lastp = pr.op(eng, lambda e, cs=cs, lsl=lsl: e.tensor_tensor(
            xn[:, cs], lsl[:, :], mrsb[:, :], ALU.subtract),
            waits=[(sem, vmul)], inc=sem)
        last_buf[(eng, k % 2)] = lastp
        p_xn.append((sem, lastp))
    rel45 = [(aS, a_rsb)]

    kts = [ktm[:, 0:2048], ktm[:, 2048:4096]]
    vts = [vtm[:, 0:2048], vtm[:, 2048:4096]]
    prev_vphi = None
    t_kmm = []
    t_vmm = []
    t_kvh = []
    t_ksum = []
    s_pack = []
    evA = {}
    vphi_blk = []
    acp_blk = []
    for b in range(4):
        if b >= 1:
            g_wkb.append(pr.op("GP", lambda e, b=b: e.dma_start(
                out=arena_h[0], in_=d_wk[b]),
                waits=[(tP, t_kmm[b - 1])], inc=sWa, by=16))
            g_wvb.append(pr.op("GP", lambda e, b=b: e.dma_start(
                out=arena_h[1], in_=d_wv[b]),
                waits=[(tP, t_vmm[b - 1])], inc=sWb, by=16))
        for is_k in (True, False):
            gld = g_wkb[b] if is_k else g_wvb[b]
            sld = sWa if is_k else sWb
            ah = arena_h[0] if is_k else arena_h[1]
            dstT = (kts if is_k else vts)[b % 2]
            for lt in range(8):
                bank = lt % 4
                bw = [(sld, gld), (aV, ve_ones), (sP, s_pre)]
                bw += evA.get(bank, [])
                if b >= 2:
                    bw.append((tP, t_ksum[b - 2]))
                for c in range(NCH):
                    cw = (bw if c == 0 else []) + [p_xn[c]]
                    pr.op("TE", lambda e, c=c, lt=lt, ah=ah, bank=bank: e.matmul(
                        ps[bank][:, 0:256],
                        xn[:, c * NT + lt * P: c * NT + (lt + 1) * P],
                        ah[:, c * 256:(c + 1) * 256],
                        start=(c == 0), stop=False),
                        waits=cw)
                brow = (bkb if is_k else bvb)[0:1, b * 256:(b + 1) * 256]
                t_this = pr.op("TE", lambda e, brow=brow, bank=bank: e.matmul(
                    ps[bank][:, 0:256], ones_r[:, :], brow,
                    start=False, stop=True), inc=tP)
                dst = dstT[:, lt * 256:(lt + 1) * 256]
                if is_k:
                    ew = [(tP, t_this)]
                    if prev_vphi is not None:
                        ew.append((aV, prev_vphi))
                    pr.op("SC", lambda e, bank=bank: e.activation(
                        lnAb[:, 0:256], ps[bank][:, 0:256], AF.Exp),
                        waits=ew, inc=aS)
                    a_r = pr.op("SC", lambda e, bank=bank: e.activation(
                        lnAb[:, 256:512], ps[bank][:, 0:256], AF.Relu),
                        inc=aS)
                    prev_vphi = pr.op("VE", lambda e, dst=dst:
                                      e.scalar_tensor_tensor(
                                          dst, lnAb[:, 0:256], 1.0,
                                          lnAb[:, 256:512], ALU.min, ALU.add),
                                      waits=[(aS, a_r)], inc=aV)
                    evA[bank] = [(aS, a_r)]
                else:
                    a_cp = pr.op("SC", lambda e, dst=dst, bank=bank: e.activation(
                        dst, ps[bank][:, 0:256], AF.Copy),
                        waits=[(tP, t_this)], inc=aS)
                    evA[bank] = [(aS, a_cp)]
            if is_k:
                t_kmm.append(t_this)
                vphi_blk.append(prev_vphi)
            else:
                t_vmm.append(t_this)
                acp_blk.append(aS.n)
        t_last = None
        for sub in range(2):
            hh = 2 * b + sub
            bank = 4 + sub
            bw = [(aV, vphi_blk[b]), (aS, acp_blk[b])]
            if b == 0:
                bw += rel45
            if b >= 1:
                bw.append((aV, t_kvh[b - 1][1][sub]))
            for lt in range(8):
                t_last = pr.op(
                    "TE", lambda e, lt=lt, sub=sub, b=b, bank=bank: e.matmul(
                        ps[bank][:, 0:P],
                        kts[b % 2][:, lt * 256 + sub * P: lt * 256 + (sub + 1) * P],
                        vts[b % 2][:, lt * 256 + sub * P: lt * 256 + (sub + 1) * P],
                        start=(lt == 0), stop=(lt == 7)),
                    waits=bw if lt == 0 else [],
                    inc=(tP if (sub == 1 and lt == 7) else None))
        v_evs = []
        for sub in range(2):
            hh = 2 * b + sub
            sK = sKa if sub == 0 else sKb
            vw = [(tP, t_last)]
            if b >= 1:
                vw.append((sK, s_pack[2 * (b - 1) + sub]))
            v_ev = pr.op("VE", lambda e, sub=sub, bank=4 + sub: e.tensor_copy(
                kvst[:, sub * P:(sub + 1) * P], ps[bank][:, 0:P]),
                waits=vw, inc=aV)
            v_evs.append(v_ev)
            s_pack.append(pr.op("SY", lambda e, hh=hh, sub=sub: e.dma_start(
                out=cc_in.ap()[hh * P:(hh + 1) * P, :],
                in_=kvst[:, sub * P:(sub + 1) * P]),
                waits=[(aV, v_ev)], inc=sK, by=16))
        t_kvh.append((t_last, v_evs))
        ksbank = 6 + b // 2
        kcol = (b % 2) * 256
        t_kb = None
        for lt in range(8):
            kw = [(aV, vphi_blk[b])]
            if b == 0:
                kw.append((aS, a_mrb))
            t_kb = pr.op("TE", lambda e, lt=lt, b=b, ksbank=ksbank, kcol=kcol:
                         e.matmul(
                             ps[ksbank][0:1, kcol:kcol + 256], ones_c[:, :],
                             kts[b % 2][:, lt * 256:(lt + 1) * 256],
                             start=(lt == 0), stop=(lt == 7)),
                         waits=kw if lt == 0 else [],
                         inc=(tP if lt == 7 else None))
        t_ksum.append(t_kb)
    t_ks = t_ksum[-1]
    a_ks = None
    for h in range(2):
        a_ks = pr.op("SC", lambda e, h=h: e.activation(
            r_scr[0:1, h * LH:(h + 1) * LH], ps[6 + h][0:1, :], AF.Copy),
            waits=[(tP, t_ks)], inc=aS)
    s_ks = None
    for hh in range(NHEAD):
        s_ks = pr.op("SY", lambda e, hh=hh: e.dma_start(
            out=cc_in.ap()[DK + hh:DK + hh + 1, :],
            in_=r_scr[0:1, hh * P:(hh + 1) * P]),
            waits=[(aS, a_ks)], inc=sKs, by=16)

    cc_waits = [(sKs, s_ks), (sKa, sKa.n), (sKb, sKb.n)]

    colq5 = [arena[:, 4096 * i:4096 * (i + 1)] for i in range(4)] + [qex[:, :]]
    qsems = [sWa, sQb, sBa, sBb, sQc]
    t_qcol = []
    ev_q = {}
    g_qcol = []
    v_phi_q = []
    for j in range(NCH):
        sW = qsems[j % 5]
        if j < 2:
            waits = [(tP, t_kmm[3])]
        elif j < 4:
            waits = [(tP, t_vmm[3])]
        elif j == 4:
            waits = [(tP, t_ks), (tP, t_kvh[3][0])]
        else:
            waits = [(tP, t_qcol[j - 5])]
        g_qcol.append(pr.op("GP", lambda e, j=j: e.dma_start(
            out=colq5[j % 5], in_=d_wq[j]), waits=waits, inc=sW, by=16))
        bank0, bank1 = (j % 2) * 2, (j % 2) * 2 + 1
        bw = [(sW, g_qcol[j])]
        for k in (bank0, bank1):
            bw += ev_q.get(k, [])
        tq = None
        for c in range(NCH):
            pr.op("TE", lambda e, j=j, c=c, bank0=bank0: e.matmul(
                ps[bank0][:, :], colq5[j % 5][:, c * P:(c + 1) * P],
                xn[:, c * NT: c * NT + LH],
                start=(c == 0), stop=(c == NCH - 1)),
                waits=bw if c == 0 else [])
            tq = pr.op("TE", lambda e, j=j, c=c, bank1=bank1: e.matmul(
                ps[bank1][:, :], colq5[j % 5][:, c * P:(c + 1) * P],
                xn[:, c * NT + LH: c * NT + NT],
                start=(c == 0), stop=(c == NCH - 1)),
                inc=(tP if c == NCH - 1 else None))
        t_qcol.append(tq)
        for h in range(2):
            bank = bank0 if h == 0 else bank1
            u = h
            s_e = qsc[:, (2 * u) * LH:(2 * u + 1) * LH]
            s_r = qsc[:, (2 * u + 1) * LH:(2 * u + 2) * LH]
            ew = [(tP, t_qcol[j]), (sP, s_pre)]
            if j >= 1:
                ew.append((aV, v_phi_q[2 * (j - 1) + h]))
            a_e = pr.op("SC", lambda e, bank=bank, j=j, s_e=s_e: e.activation(
                s_e, ps[bank][:, :], AF.Exp, bias=bqcol[:, j:j + 1]),
                waits=ew, inc=aS)
            a_r = pr.op("SC", lambda e, bank=bank, j=j, s_r=s_r: e.activation(
                s_r, ps[bank][:, :], AF.Relu, bias=bqcol[:, j:j + 1]), inc=aS)
            dst = qb[:, j * NT + h * LH: j * NT + (h + 1) * LH]
            vp = pr.op("VE", lambda e, dst=dst, s_e=s_e, s_r=s_r:
                       e.scalar_tensor_tensor(
                           dst, s_e, 1.0, s_r, ALU.min, ALU.add),
                       waits=[(aS, a_r)], inc=aV)
            v_phi_q.append(vp)
            ev_q[bank] = [(aS, a_r)]

        if j == 4:
            c_cc = pr.op("GP", lambda e: e.collective_compute(
                "AllReduce", ALU.add,
                replica_groups=[list(range(N_CORES))],
                ins=[cc_in.ap()], outs=[cc_out.ap()]),
                waits=cc_waits, inc=cC, by=1)

        if j == 16:
            g_kvtb = pr.op("GP", lambda e: e.dma_start(
                out=kvtb[:, :],
                in_=cc_out.ap()[0:DK, :].rearrange("(h d) v -> d h v", d=P)),
                waits=[(cC, c_cc)], inc=sX, by=16)
            s_ksb = pr.op("SY", lambda e: e.dma_start(
                out=ksb[:, :],
                in_=cc_out.ap()[DK:DK + NE, :].rearrange("h v -> v h"),
                allow_slow_non_contiguous=True),
                waits=[(cC, c_cc)], inc=sKR, by=16)
            v_k1 = pr.op("VE", lambda e: e.tensor_single_scalar(
                ksb_r[:, :], ksb[:, :], 1e-10, ALU.add),
                waits=[(sKR, s_ksb)], inc=aV)
            pr.op("VE", lambda e: e.reciprocal(ksb_r[:, :], ksb_r[:, :]),
                  waits=[(aV, v_k1)], inc=aV)
            v_ksr = aV.n

        if j == 20:
            t_rt = []
            for h in range(2):
                tr = None
                for c in range(NCH):
                    tr = pr.op("TE", lambda e, h=h, c=c: e.matmul(
                        ps[6][0:8, :] if h == 0 else ps[7][0:8, :],
                        gwT[:, c * NE:(c + 1) * NE],
                        xn[:, c * NT + h * LH: c * NT + h * LH + LH],
                        start=(c == 0), stop=(c == NCH - 1)),
                        waits=([(sP, s_pre), (aS, a_ks), (cC, c_cc)]
                               if c == 0 else []),
                        inc=(tP if c == NCH - 1 else None))
                t_rt.append(tr)
        if j == 21:
            a_exp = []
            for h in range(2):
                a_exp.append(pr.op("SC", lambda e, h=h: e.activation(
                    expL[0:8, h * LH:(h + 1) * LH], ps[6 + h][0:8, :], AF.Exp),
                    waits=[(tP, t_rt[h])], inc=aS))
        if j == 22:
            t_z = []
            for h in range(2):
                t_z.append(pr.op("TE", lambda e, h=h: e.matmul(
                    ps[6 + h][0:1, :], ones_c[0:8, :],
                    expL[0:8, h * LH:(h + 1) * LH],
                    start=True, stop=True), waits=[(aS, a_exp[h])], inc=tP))
        if j == 23:
            a_z = None
            for h in range(2):
                a_z = pr.op("SC", lambda e, h=h: e.activation(
                    r_ex2[0:1, h * LH:(h + 1) * LH], ps[6 + h][0:1, :],
                    AF.Copy), waits=[(tP, t_z[h])], inc=aS)
            a_zl = pr.op("SC", lambda e: e.activation(
                r_mean[0:1, :], r_ex2[0:1, :], AF.Ln),
                waits=[(aS, a_z)], inc=aS)
            a_zi = pr.op("SC", lambda e: e.activation(
                r_mean[0:1, :], r_mean[0:1, :], AF.Exp, scale=-1.0),
                waits=[(aS, a_zl)], inc=aS)
        if j == 24:
            t_zb = []
            for h in range(2):
                t_zb.append(pr.op("TE", lambda e, h=h: e.matmul(
                    ps[6 + h][0:8, :], onesf[:, 1:9],
                    r_mean[0:1, h * LH:(h + 1) * LH],
                    start=True, stop=True), waits=[(aS, a_zi)], inc=tP))
            v_rout = None
            for h in range(2):
                v_rout = pr.op("VE", lambda e, h=h: e.tensor_tensor(
                    routb[0:8, h * LH:(h + 1) * LH],
                    expL[0:8, h * LH:(h + 1) * LH],
                    ps[6 + h][0:8, :], ALU.mult),
                    waits=[(tP, t_zb[h])], inc=aV)
    v_q_done = aV.n

    bank_ev = {b: list(ev_q.get(b, [])) for b in range(6)}
    bank_ev[6] = [(aV, v_rout)]
    bank_ev[7] = [(aV, v_rout)]

    t_attn_done = None
    for qh in range(NQH):
        kvh = qh // 4
        for h in range(2):
            u = 2 * qh + h
            bank = u % 6
            bw = [(sX, g_kvtb), (aV, v_phi_q[u])] + bank_ev.get(bank, [])
            t_a = pr.op("TE", lambda e, qh=qh, h=h, kvh=kvh, bank=bank: e.matmul(
                ps[bank][:, :], kvtb[:, kvh * P:(kvh + 1) * P],
                qb[:, qh * NT + h * LH: qh * NT + (h + 1) * LH],
                start=True, stop=True), waits=bw, inc=tP)
            t_attn_done = t_a
            dst = qb[:, qh * NT + h * LH: qh * NT + (h + 1) * LH]
            if u % 2 == 0:
                v_a = pr.op("VE", lambda e, dst=dst, bank=bank, kvh=kvh:
                            e.tensor_single_scalar(
                                dst, ps[bank][:, :], ksb_r[:, kvh:kvh + 1],
                                ALU.mult),
                            waits=[(tP, t_a), (aV, v_ksr)], inc=aV)
                bank_ev[bank] = [(aV, v_a)]
            else:
                s_a = pr.op("SC", lambda e, dst=dst, bank=bank, kvh=kvh:
                            e.activation(
                                dst, ps[bank][:, :], AF.Copy,
                                scale=ksb_r[:, kvh:kvh + 1]),
                            waits=[(tP, t_a), (aV, v_ksr)], inc=aS)
                bank_ev[bank] = [(aS, s_a)]
    attn_v_done = aV.n
    attn_s_done = aS.n

    g_ict = []
    for j in range(2):
        g_ict.append(pr.op("GP", lambda e, j=j: e.dma_start(
            out=icst[:, (j % 2) * P:(j % 2 + 1) * P],
            in_=d_ics[:, j * P:(j + 1) * P]),
            inc=(sIa if j % 2 == 0 else sIb), by=16))

    colw = [arena[:, 0:4096], arena[:, 4096:8192]]
    colb = [arena[:, 8192:12288], arena[:, 12288:16384]]
    g_wo = []
    g_wb = []
    g_oct = []
    g_res = []
    a_sgn = []
    for j in range(2):
        sW = sWa if j % 2 == 0 else sWb
        sB = sBa if j % 2 == 0 else sBb
        sR = sRa if j % 2 == 0 else sRb
        sO = sOa if j % 2 == 0 else sOb
        g_wo.append(pr.op("GP", lambda e, j=j: e.dma_start(
            out=colw[j % 2], in_=d_wo[j]),
            waits=[(tP, t_qcol[-1])], inc=sW, by=16))
        g_wb.append(pr.op("GP", lambda e, j=j: e.dma_start(
            out=colb[j % 2], in_=d_wb[j]),
            waits=[(tP, t_qcol[-1])], inc=sB, by=16))
        a_sgn.append(pr.op("VE", lambda e, j=j: e.tensor_scalar(
            colb[j % 2][:, :], colb[j % 2][:, :], 0.0, 0.5,
            ALU.is_ge, ALU.subtract),
            waits=[(sB, g_wb[j])], inc=aV))
        g_res.append(pr.op("GP", lambda e, j=j: e.dma_start(
            out=residb[:, (j % 2) * NT:(j % 2 + 1) * NT],
            in_=d_xT[:, j * NT:(j + 1) * NT]),
            waits=[(tP, t_attn_done)], inc=sR, by=16))
        g_oct.append(pr.op("GP", lambda e, j=j: e.dma_start(
            out=ocst[:, (j % 2) * P:(j % 2 + 1) * P],
            in_=d_ocs[:, j * P:(j + 1) * P]), inc=sO, by=16))

    pre_thunks = []
    for h in range(2):
        u = h % 2
        bS = 2 if u == 0 else 5
        bO = 4 if u == 0 else 7
        bw0 = [(sWa, g_wo[0]), (aV, attn_v_done), (aS, attn_s_done)]
        bw0 += bank_ev[bS]
        for c in range(NCH):
            pre_thunks.append((
                "S", h,
                lambda e, h=h, c=c, bS=bS: e.matmul(
                    ps[bS][:, :], colw[0][:, c * P:(c + 1) * P],
                    qb[:, c * NT + h * LH: c * NT + (h + 1) * LH],
                    start=(c == 0), stop=(c == NCH - 1)),
                bw0 if c == 0 else [], c == NCH - 1))
        bwo = [(sOa, g_oct[0]), (aV, v_rout)] + bank_ev[bO]
        pre_thunks.append((
            "O", h,
            lambda e, h=h, bO=bO: e.matmul(
                ps[bO][:, :], ocst[0:8, 0:P],
                routb[0:8, h * LH:(h + 1) * LH],
                start=True, stop=True), bwo, True))

    pre_tS = {}
    pre_tO = {}
    pre_iter = iter(pre_thunks)

    def drain_pre(n):
        for _ in range(n):
            item = next(pre_iter, None)
            if item is None:
                return
            kind, h, fn, w, is_end = item
            v = pr.op("TE", fn, waits=w, inc=(tP if is_end else None))
            if is_end:
                (pre_tS if kind == "S" else pre_tO)[h] = v

    xs_banks = [6, 0, 1, 3]
    t_ic = []
    for j in range(NCH):
        sI = sIa if j % 2 == 0 else sIb
        if j >= 2:
            g_ict.append(pr.op("GP", lambda e, j=j: e.dma_start(
                out=icst[:, (j % 2) * P:(j % 2 + 1) * P],
                in_=d_ics[:, j * P:(j + 1) * P]),
                waits=[(tP, t_ic[j - 2])], inc=sI, by=16))
        for h in range(2):
            bank = xs_banks[(2 * j + h) % 4]
            bw = [(sI, g_ict[j]), (aV, v_rout), (tP, t_qcol[-1])]
            bw += bank_ev.get(bank, [])
            t_i = pr.op("TE", lambda e, j=j, h=h, bank=bank: e.matmul(
                ps[bank][:, :], icst[0:8, (j % 2) * P:(j % 2 + 1) * P],
                routb[0:8, h * LH:(h + 1) * LH],
                start=True, stop=True), waits=bw, inc=tP)
            sl = slice(j * NT + h * LH, j * NT + (h + 1) * LH)
            vx = pr.op("VE", lambda e, sl=sl, bank=bank: e.tensor_tensor(
                xn[:, sl], xn[:, sl], ps[bank][:, :], ALU.mult),
                waits=[(tP, t_i)], inc=aV)
            bank_ev[bank] = [(aV, vx)]
            if j >= 6:
                drain_pre(2)
        t_ic.append(t_i)
    drain_pre(len(pre_thunks))
    xs_v_done = aV.n

    t_scol = []
    t_mcol = []
    t_oc = []
    s_store = []
    ev_fin = {
        ("M", 0): bank_ev[3][0],
        ("M", 1): bank_ev[6][0],
    }
    v_out = {}
    for j in range(NCH):
        sW = sWa if j % 2 == 0 else sWb
        sB = sBa if j % 2 == 0 else sBb
        sR = sRa if j % 2 == 0 else sRb
        sO = sOa if j % 2 == 0 else sOb
        sS = sSa if j % 2 == 0 else sSb
        if j >= 2:
            g_wo.append(pr.op("GP", lambda e, j=j: e.dma_start(
                out=colw[j % 2], in_=d_wo[j]),
                waits=[(tP, t_scol[j - 2])], inc=sW, by=16))
            g_wb.append(pr.op("GP", lambda e, j=j: e.dma_start(
                out=colb[j % 2], in_=d_wb[j]),
                waits=[(tP, t_mcol[j - 2])], inc=sB, by=16))
            a_sgn.append(pr.op("VE", lambda e, j=j: e.tensor_scalar(
                colb[j % 2][:, :], colb[j % 2][:, :], 0.0, 0.5,
                ALU.is_ge, ALU.subtract),
                waits=[(sB, g_wb[j])], inc=aV))
            g_res.append(pr.op("GP", lambda e, j=j: e.dma_start(
                out=residb[:, (j % 2) * NT:(j % 2 + 1) * NT],
                in_=d_xT[:, j * NT:(j + 1) * NT]),
                waits=[(aV, v_out[(j - 2, 1)])], inc=sR, by=16))
            g_oct.append(pr.op("GP", lambda e, j=j: e.dma_start(
                out=ocst[:, (j % 2) * P:(j % 2 + 1) * P],
                in_=d_ocs[:, j * P:(j + 1) * P]),
                waits=[(tP, t_oc[j - 2])], inc=sO, by=16))
        for h in range(2):
            u = (2 * j + h) % 2
            bS, bM, bO = (2, 3, 4) if u == 0 else (5, 6, 7)
            s_os = scr[:, (3 * u) * LH:(3 * u + 1) * LH]
            s_P = scr[:, (3 * u + 1) * LH:(3 * u + 2) * LH]
            s_g = scr[:, (3 * u + 2) * LH:(3 * u + 3) * LH]
            if j == 0:
                tS = pre_tS[h]
                tO = pre_tO[h]
            else:
                bw = [(sW, g_wo[j]), (aV, attn_v_done), (aS, attn_s_done)]
                if ("S", u) in ev_fin:
                    bw.append(ev_fin[("S", u)])
                tS = None
                for c in range(NCH):
                    tS = pr.op("TE", lambda e, j=j, h=h, c=c, bS=bS: e.matmul(
                        ps[bS][:, :], colw[j % 2][:, c * P:(c + 1) * P],
                        qb[:, c * NT + h * LH: c * NT + (h + 1) * LH],
                        start=(c == 0), stop=(c == NCH - 1)),
                        waits=bw if c == 0 else [],
                        inc=(tP if c == NCH - 1 else None))
                bw = [(sO, g_oct[j]), (aV, v_rout)]
                if ("O", u) in ev_fin:
                    bw.append(ev_fin[("O", u)])
                tO = pr.op("TE", lambda e, j=j, h=h, bO=bO: e.matmul(
                    ps[bO][:, :], ocst[0:8, (j % 2) * P:(j % 2 + 1) * P],
                    routb[0:8, h * LH:(h + 1) * LH],
                    start=True, stop=True), waits=bw, inc=tP)
            if h == 1:
                t_scol.append(tS)
                t_oc.append(tO)
            bw = [(aV, a_sgn[j]), (aV, xs_v_done)]
            if ("M", u) in ev_fin:
                bw.append(ev_fin[("M", u)])
            tM = None
            for c in range(NCH):
                tM = pr.op("TE", lambda e, j=j, h=h, c=c, bM=bM: e.matmul(
                    ps[bM][:, :], colb[j % 2][:, c * P:(c + 1) * P],
                    xn[:, c * NT + h * LH: c * NT + (h + 1) * LH],
                    start=(c == 0), stop=(c == NCH - 1)),
                    waits=bw if c == 0 else [],
                    inc=(tP if c == NCH - 1 else None))
            if h == 1:
                t_mcol.append(tM)
            a_os = pr.op("SC", lambda e, bO=bO, s_os=s_os: e.activation(
                s_os, ps[bO][:, :], AF.Copy), waits=[(tP, tO)], inc=aS)
            v_P = pr.op("VE", lambda e, bM=bM, s_P=s_P, s_os=s_os: e.tensor_tensor(
                s_P, ps[bM][:, :], s_os, ALU.mult),
                waits=[(tP, tM), (aS, a_os)], inc=aV)
            a_g = pr.op("SC", lambda e, j=j, s_P=s_P, s_g=s_g: e.activation(
                s_g, s_P, AF.Silu, bias=bfcol[:, j:j + 1], scale=2.0),
                waits=[(aV, v_P), (sP, s_pre)], inc=aS)
            v_sg = pr.op("VE", lambda e, j=j, bS=bS, s_os=s_os, s_g=s_g:
                         e.scalar_tensor_tensor(
                             s_os, ps[bS][:, :], bocol[:, j:j + 1], s_g,
                             ALU.add, ALU.mult),
                         waits=[(tP, tS), (aS, a_g), (sP, s_pre)], inc=aV)
            ow = [(aV, v_sg), (sR, g_res[j])]
            if j >= 2:
                ow.append((sS, s_store[j - 2]))
            vo = pr.op("VE", lambda e, j=j, h=h, s_os=s_os: e.tensor_tensor(
                outst[:, (j % 2) * NT + h * LH:(j % 2) * NT + (h + 1) * LH],
                s_os, residb[:, (j % 2) * NT + h * LH:(j % 2) * NT + (h + 1) * LH],
                ALU.add), waits=ow, inc=aV)
            v_out[(j, h)] = vo
            ev_fin[("S", u)] = (aV, v_sg)
            ev_fin[("M", u)] = (aV, v_P)
            ev_fin[("O", u)] = (aS, a_os)
        s_store.append(pr.op("SY", lambda e, j=j: e.dma_start(
            out=d_out[j], in_=outst[:, (j % 2) * NT:(j % 2 + 1) * NT]),
            waits=[(aV, v_out[(j, 1)])], inc=sS, by=16))

    pr.op("SY", lambda e: e.nop(), waits=[(sSa, sSa.n), (sSb, sSb.n)])

    from contextlib import ExitStack
    with ExitStack() as ctx:
        for s in sems:
            s.h = ctx.enter_context(nc.semaphore(s.name))
        block = ctx.enter_context(nc.Block())

        def runner(key):
            def cb(eng):
                hw = {}
                for (k, fn, waits, inc, by) in pr.ops:
                    if k != key:
                        continue
                    for (s, v) in waits:
                        if hw.get(s.name, 0) < v:
                            eng.wait_ge(s.h, v)
                            hw[s.name] = v
                    inst = fn(eng)
                    if inc is not None and inst is not None:
                        if by is None:
                            inst.then_inc(inc.h)
                        else:
                            inst.then_inc(inc.h, by)
            return cb

        block.sync(runner("SY"))
        block.scalar(runner("SC"))
        block.vector(runner("VE"))
        block.tensor(runner("TE"))
        block.gpsimd(runner("GP"))

    return nc


_NC_CACHE = []


def kernel(**inputs):
    f = np.float32

    def g(name):
        return np.asarray(inputs[name], dtype=f)

    x = g("L0_attn_hidden_states")[0]
    Wq, bq = g("L0_attn_Wq"), g("L0_attn_bq")
    Wk, bk = g("L0_attn_Wk"), g("L0_attn_bk")
    Wv, bv = g("L0_attn_Wv"), g("L0_attn_bv")
    Wo, bo = g("L0_attn_Wo"), g("L0_attn_bo")
    Wf, bf = g("L0_ffn_weight"), g("L0_ffn_bias")
    gw = g("L0_ffn_gate_w")
    ics = g("L0_ffn_in_channel_scale")
    ocs = g("L0_ffn_out_channel_scale")

    def colt(W):
        O = W.shape[0]
        return np.ascontiguousarray(
            W.T.reshape(NCH, P, O // P, P).transpose(2, 1, 0, 3)
        ).reshape(O // P, P, NCH * P)

    wq_t = colt(Wq)
    wo_t = colt(Wo)
    wb_t = colt(Wf)

    def blkt(W):
        return np.ascontiguousarray(
            W.T.reshape(NCH, P, 4, 256).transpose(2, 1, 0, 3)
        ).reshape(4, P, NCH * 256)

    wk_t = blkt(Wk)
    wv_t = blkt(Wv)
    gw_t = np.ascontiguousarray(
        gw.T.reshape(NCH, P, NE).transpose(1, 0, 2)).reshape(P, NCH * NE)
    bqc = np.ascontiguousarray(bq.reshape(NCH, P).T)
    boc = np.ascontiguousarray(bo.reshape(NCH, P).T)
    bfc = np.ascontiguousarray(bf.reshape(NCH, P).T)
    bk2 = np.ascontiguousarray(bk.reshape(1, DK))
    bv2 = np.ascontiguousarray(bv.reshape(1, DK))
    ics = np.ascontiguousarray(ics)
    ocs = np.ascontiguousarray(ocs)

    xT = np.ascontiguousarray(x.T)
    in_maps = []
    for c in range(N_CORES):
        xc = np.ascontiguousarray(
            xT[:, c * NT:(c + 1) * NT].reshape(NCH, P, NT)
            .transpose(1, 0, 2)).reshape(P, NCH * NT)
        in_maps.append({
            "xT": xc, "wq": wq_t, "wo": wo_t, "wb": wb_t,
            "wk": wk_t, "wv": wv_t, "gw": gw_t, "ics": ics, "ocs": ocs,
            "bqc": bqc, "boc": boc, "bfc": bfc, "bk": bk2, "bv": bv2,
        })

    if not _NC_CACHE:
        _NC_CACHE.append(build_program())
    nc = _NC_CACHE[0]
    res = run_bass_kernel_spmd(nc, in_maps, core_ids=list(range(N_CORES)))

    out = np.empty((8192, D), dtype=f)
    for c in range(N_CORES):
        oc = np.asarray(res.results[c]["outT"], dtype=f)
        out[c * NT:(c + 1) * NT, :] = oc.reshape(D, NT).T
    return out.reshape(1, 8192, D)


# version 19
# speedup vs baseline: 1.0688x; 1.0203x over previous
import numpy as np
import concourse.bass as bass
import concourse.mybir as mybir
from concourse.bass_utils import run_bass_kernel_spmd

F32 = mybir.dt.float32
BF16 = mybir.dt.bfloat16
AF = mybir.ActivationFunctionType
ALU = mybir.AluOpType

P = 128
NT = 1024
LH = 512
D = 4096
NCH = 32
DK = 1024
NHEAD = 8
NQH = 32
NE = 8
N_CORES = 8

SBUF_BYTES = 229376


class S:
    __slots__ = ("h", "n", "name")

    def __init__(self, h, name):
        self.h = h
        self.n = 0
        self.name = name


class Prog:
    def __init__(self):
        self.ops = []

    def op(self, eng, fn, waits=(), inc=None, by=1):
        for (s, v) in waits:
            assert v <= s.n, f"wait on future value {s.name} {v} > {s.n}"
        self.ops.append((eng, fn, [(s, v) for (s, v) in waits if v > 0], inc, by))
        if inc is not None:
            inc.n += (by if by is not None else 1)
            return inc.n
        return None


def build_program():
    nc = bass.Bass("TRN2", target_bir_lowering=False)

    d_xT = nc.dram_tensor("xT", [P, NCH * NT], F32, kind="ExternalInput").ap()
    d_wq = nc.dram_tensor("wq", [NCH, P, D], F32, kind="ExternalInput").ap()
    d_wo = nc.dram_tensor("wo", [NCH, P, D], F32, kind="ExternalInput").ap()
    d_wb = nc.dram_tensor("wb", [NCH, P, D], F32, kind="ExternalInput").ap()
    d_wk = nc.dram_tensor("wk", [4, P, 32 * 256], F32, kind="ExternalInput").ap()
    d_wv = nc.dram_tensor("wv", [4, P, 32 * 256], F32, kind="ExternalInput").ap()
    d_gw = nc.dram_tensor("gw", [P, NCH * NE], F32, kind="ExternalInput").ap()
    d_ics = nc.dram_tensor("ics", [NE, D], F32, kind="ExternalInput").ap()
    d_ocs = nc.dram_tensor("ocs", [NE, D], F32, kind="ExternalInput").ap()
    d_bqc = nc.dram_tensor("bqc", [P, NCH], F32, kind="ExternalInput").ap()
    d_boc = nc.dram_tensor("boc", [P, NCH], F32, kind="ExternalInput").ap()
    d_bfc = nc.dram_tensor("bfc", [P, NCH], F32, kind="ExternalInput").ap()
    d_bk = nc.dram_tensor("bk", [1, DK], F32, kind="ExternalInput").ap()
    d_bv = nc.dram_tensor("bv", [1, DK], F32, kind="ExternalInput").ap()
    d_out = nc.dram_tensor("outT", [NCH, P, NT], F32, kind="ExternalOutput").ap()
    cc_in = nc.dram_tensor("cc_in", [DK + NE, P], F32, kind="Internal")
    cc_out = nc.dram_tensor("cc_out", [DK + NE, P], F32, kind="Internal")

    resv = nc.alloc_sbuf_tensor("resv", [P, 212096], mybir.dt.uint8)
    BASE = (nc.lookup_mloc(resv).addr + 127) & ~127

    def at(name, shape, dtype, rel):
        off = BASE + rel
        end = off + shape[1] * (2 if dtype == BF16 else 4)
        assert end <= SBUF_BYTES, (name, off, end)
        return nc.alloc_sbuf_tensor_at(name, shape, dtype, offset=off)

    xn = at("xn", [P, NCH * NT], BF16, 0)
    qb = at("qb", [P, NCH * NT], BF16, 65536)
    arena = at("arena", [P, 16384], BF16, 131072)
    ktm = at("ktm", [P, 2 * 8 * 256], BF16, 163840)
    vtm = at("vtm", [P, 2 * 8 * 256], BF16, 172032)
    residb = at("residb", [P, 2 * NT], BF16, 163840)
    outst = at("outst", [P, 2 * NT], F32, 167936)
    scr = at("scr", [P, 6 * LH], F32, 176128)
    qex = at("qex", [P, 4096], BF16, 163840)
    kvtb = at("kvtb", [P, NHEAD * P], BF16, 182272)
    rstd_sb = at("rstd_sb", [P, NT], BF16, 176128)
    vA = at("vA", [P, NT], BF16, 178176)
    vB = at("vB", [P, NT], BF16, 180224)
    pA = at("pA", [P, NT], BF16, 182272)
    pB = at("pB", [P, NT], BF16, 184320)
    lnAb = at("lnAb", [P, LH], BF16, 186368)
    qsc = at("qsc", [P, 4 * LH], BF16, 176128)
    r_mean = at("r_mean", [1, NT], F32, 188416)
    r_ex2 = at("r_ex2", [1, NT], F32, 192512)
    r_scr = at("r_scr", [1, NT], F32, 196608)
    routb = at("routb", [NE, NT], BF16, 200704)
    bkb = at("bkb", [1, DK], BF16, 202752)
    bvb = at("bvb", [1, DK], BF16, 204800)
    gwT = at("gwT", [P, NCH * NE], BF16, 206848)
    kvst = at("kvst", [P, 2 * P], F32, 207360)
    kvscr = at("kvscr", [P, 256], F32, 208384)
    xsqb = at("xsqb", [P, NT], BF16, 207360)
    mrsb = at("mrsb", [P, NT], BF16, 207360)
    expL = at("expL", [NE, NT], BF16, 207360)
    xsqb2 = at("xsqb2", [P, NT], BF16, 176128)
    ones_r = at("ones_r", [1, P], BF16, 209408)
    onesf = at("onesf", [1, 129], F32, 209664)
    ones_c = at("ones_c", [P, 1], BF16, 210208)
    bqcol = at("bqcol", [P, NCH], F32, 210240)
    bocol = at("bocol", [P, NCH], F32, 210368)
    bfcol = at("bfcol", [P, NCH], F32, 210496)
    ksb = at("ksb", [P, NHEAD], F32, 210624)
    ksb_r = at("ksb_r", [P, NHEAD], F32, 210656)
    epsb = at("epsb", [1, 1], F32, 210688)
    icst = at("icst", [NE, 8 * P], BF16, 210720)
    ocst = at("ocst", [NE, 2 * P], BF16, 208384)

    ps = [nc.alloc_psum_tensor(f"ps{i}", [P, LH], F32) for i in range(8)]

    pr = Prog()
    sems = []

    def mk(name):
        s = S(None, name)
        sems.append(s)
        return s

    sP = mk("sP")
    sX = mk("sX")
    sX1 = mk("sX1")
    sX2 = mk("sX2")
    sX3 = mk("sX3")
    sX4 = mk("sX4")
    sX5 = mk("sX5")
    sX6 = mk("sX6")
    sX7 = mk("sX7")
    aP = mk("aP")
    sWa = mk("sWa")
    sWb = mk("sWb")
    sBa = mk("sBa")
    sBb = mk("sBb")
    sIq = [mk(f"sI{k}") for k in range(8)]
    sOa = mk("sOa")
    sOb = mk("sOb")
    sKa = mk("sKa")
    sKb = mk("sKb")
    sKs = mk("sKs")
    sKR = mk("sKR")
    sRa = mk("sRa")
    sRb = mk("sRb")
    sSa = mk("sSa")
    sSb = mk("sSb")
    sQb = mk("sQb")
    sQc = mk("sQc")
    tP = mk("tP")
    aS = mk("aS")
    aV = mk("aV")
    cC = mk("cC")

    ve_ones = pr.op("VE", lambda e: e.memset(ones_r[:, :], 1.0), inc=aV)
    pr.op("VE", lambda e: e.memset(onesf[:, :], 1.0), inc=aV)
    pr.op("VE", lambda e: e.memset(ones_c[:, :], 1.0), inc=aV)
    ve_ones = pr.op("VE", lambda e: e.memset(epsb[:, :], 1e-5), inc=aV)

    sXq = [sX, sX1, sX2, sX3, sX4, sX5, sX6, sX7]
    QW = 4 * NT
    g_x4 = []
    for qq in range(8):
        g_x4.append(pr.op("GP", lambda e, qq=qq: e.dma_start(
            out=xn[:, qq * QW:(qq + 1) * QW],
            in_=d_xT[:, qq * QW:(qq + 1) * QW]), inc=sXq[qq], by=16))

    pr.op("GP", lambda e: e.dma_start(out=gwT[:, :], in_=d_gw), inc=sP, by=16)
    pr.op("GP", lambda e: e.dma_start(out=bkb[:, :], in_=d_bk), inc=sP, by=16)
    pr.op("GP", lambda e: e.dma_start(out=bvb[:, :], in_=d_bv), inc=sP, by=16)
    pr.op("GP", lambda e: e.dma_start(out=bqcol[:, :], in_=d_bqc), inc=sP, by=16)
    pr.op("GP", lambda e: e.dma_start(out=bocol[:, :], in_=d_boc), inc=sP, by=16)
    s_pre = pr.op("GP", lambda e: e.dma_start(out=bfcol[:, :], in_=d_bfc),
                  inc=sP, by=16)

    t_stats = []
    a_sq = []
    for c in range(NCH):
        xq = xsqb if c % 2 == 0 else xsqb2
        waits = [(sXq[c // 4], g_x4[c // 4])]
        if c >= 2:
            waits.append((tP, t_stats[c - 2]))
        a_sq.append(pr.op("SC", lambda e, c=c, xq=xq: e.activation(
            xq[:, :], xn[:, c * NT:(c + 1) * NT], AF.Square), waits=waits, inc=aS))
        for h in range(2):
            pr.op("TE", lambda e, c=c, h=h: e.matmul(
                ps[h][0:1, :], ones_c[:, :],
                xn[:, c * NT + h * LH: c * NT + (h + 1) * LH],
                start=(c == 0), stop=(c == NCH - 1)),
                waits=[(sXq[c // 4], g_x4[c // 4]), (aV, ve_ones)])
        pr.op("TE", lambda e, c=c, xq=xq: e.matmul(
            ps[2][0:1, :], ones_c[:, :], xq[:, 0:LH],
            start=(c == 0), stop=(c == NCH - 1)),
            waits=[(aS, a_sq[c])])
        t_stats.append(pr.op("TE", lambda e, c=c, xq=xq: e.matmul(
            ps[3][0:1, :], ones_c[:, :], xq[:, LH:NT],
            start=(c == 0), stop=(c == NCH - 1)), inc=tP))

    a_mean = None
    for h in range(2):
        a_mean = pr.op("SC", lambda e, h=h: e.activation(
            r_mean[0:1, h * LH:(h + 1) * LH], ps[h][0:1, :], AF.Copy,
            scale=1.0 / D), waits=[(tP, t_stats[-1])], inc=aS)
        a_mean = pr.op("SC", lambda e, h=h: e.activation(
            r_ex2[0:1, h * LH:(h + 1) * LH], ps[2 + h][0:1, :], AF.Copy,
            scale=1.0 / D), inc=aS)
    v_ms = pr.op("VE", lambda e: e.tensor_tensor(
        r_scr[0:1, :], r_mean[0:1, :], r_mean[0:1, :], ALU.mult),
        waits=[(aS, a_mean)], inc=aV)
    v_var = pr.op("VE", lambda e: e.tensor_tensor(
        r_scr[0:1, :], r_ex2[0:1, :], r_scr[0:1, :], ALU.subtract),
        waits=[(aV, v_ms)], inc=aV)
    a_lnv = pr.op("SC", lambda e: e.activation(
        r_ex2[0:1, :], r_scr[0:1, :], AF.Ln, bias=epsb[0:1, :]),
        waits=[(aV, v_var), (aV, ve_ones)], inc=aS)
    a_rstd = pr.op("SC", lambda e: e.activation(
        r_scr[0:1, :], r_ex2[0:1, :], AF.Exp, scale=-0.5),
        waits=[(aS, a_lnv)], inc=aS)
    v_mr = pr.op("VE", lambda e: e.tensor_tensor(
        r_mean[0:1, :], r_mean[0:1, :], r_scr[0:1, :], ALU.mult),
        waits=[(aS, a_rstd)], inc=aV)
    t_bc = None
    for h in range(2):
        pr.op("TE", lambda e, h=h: e.matmul(
            ps[4 + h][:, :], onesf[:, 1:129], r_scr[0:1, h * LH:(h + 1) * LH],
            start=True, stop=True), waits=[(aV, v_mr)])
        t_bc = pr.op("TE", lambda e, h=h: e.matmul(
            ps[6 + h][:, :], onesf[:, 1:129], r_mean[0:1, h * LH:(h + 1) * LH],
            start=True, stop=True), inc=tP)
    a_rsb = None
    for h in range(2):
        a_rsb = pr.op("SC", lambda e, h=h: e.activation(
            rstd_sb[:, h * LH:(h + 1) * LH], ps[4 + h][:, :], AF.Copy),
            waits=[(tP, t_bc)], inc=aS)
    a_mrb = None
    for h in range(2):
        a_mrb = pr.op("SC", lambda e, h=h: e.activation(
            mrsb[:, h * LH:(h + 1) * LH], ps[6 + h][:, :], AF.Copy),
            waits=[(tP, t_bc)], inc=aS)

    arena_h = [arena[:, 0:8192], arena[:, 8192:16384]]
    g_wkb = [pr.op("GP", lambda e: e.dma_start(
        out=arena_h[0], in_=d_wk[0]), inc=sWa, by=16)]
    g_wvb = [pr.op("GP", lambda e: e.dma_start(
        out=arena_h[1], in_=d_wv[0]), inc=sWb, by=16)]

    p_xn = []
    eng_cnt = {"VE": 0, "GP": 0}
    last_buf = {}
    for c in range(NCH):
        on_ve = True
        eng, sem = ("VE", aV) if on_ve else ("GP", aP)
        k = eng_cnt[eng]
        eng_cnt[eng] += 1
        lsl = ([vA, vB] if on_ve else [pA, pB])[k % 2]
        cs = slice(c * NT, (c + 1) * NT)
        mw = [(aS, a_mrb)]
        if (eng, k % 2) in last_buf:
            mw.append((sem, last_buf[(eng, k % 2)]))
        vmul = pr.op(eng, lambda e, cs=cs, lsl=lsl: e.tensor_tensor(
            lsl[:, :], xn[:, cs], rstd_sb[:, :], ALU.mult),
            waits=mw, inc=sem)
        lastp = pr.op(eng, lambda e, cs=cs, lsl=lsl: e.tensor_tensor(
            xn[:, cs], lsl[:, :], mrsb[:, :], ALU.subtract),
            waits=[(sem, vmul)], inc=sem)
        last_buf[(eng, k % 2)] = lastp
        p_xn.append((sem, lastp))
    rel45 = [(aS, a_rsb)]

    kts = [ktm[:, 0:2048], ktm[:, 2048:4096]]
    vts = [vtm[:, 0:2048], vtm[:, 2048:4096]]
    prev_vphi = None
    t_kmm = []
    t_vmm = []
    t_kvh = []
    t_ksum = []
    s_pack = []
    evA = {}
    vphi_blk = []
    acp_blk = []
    for b in range(4):
        if b >= 1:
            g_wkb.append(pr.op("GP", lambda e, b=b: e.dma_start(
                out=arena_h[0], in_=d_wk[b]),
                waits=[(tP, t_kmm[b - 1])], inc=sWa, by=16))
            g_wvb.append(pr.op("GP", lambda e, b=b: e.dma_start(
                out=arena_h[1], in_=d_wv[b]),
                waits=[(tP, t_vmm[b - 1])], inc=sWb, by=16))
        for is_k in (True, False):
            gld = g_wkb[b] if is_k else g_wvb[b]
            sld = sWa if is_k else sWb
            ah = arena_h[0] if is_k else arena_h[1]
            dstT = (kts if is_k else vts)[b % 2]
            for lt in range(8):
                bank = lt % 4
                bw = [(sld, gld), (aV, ve_ones), (sP, s_pre)]
                bw += evA.get(bank, [])
                if b >= 2:
                    bw.append((tP, t_ksum[b - 2]))
                for c in range(NCH):
                    cw = (bw if c == 0 else []) + [p_xn[c]]
                    pr.op("TE", lambda e, c=c, lt=lt, ah=ah, bank=bank: e.matmul(
                        ps[bank][:, 0:256],
                        xn[:, c * NT + lt * P: c * NT + (lt + 1) * P],
                        ah[:, c * 256:(c + 1) * 256],
                        start=(c == 0), stop=False),
                        waits=cw)
                brow = (bkb if is_k else bvb)[0:1, b * 256:(b + 1) * 256]
                t_this = pr.op("TE", lambda e, brow=brow, bank=bank: e.matmul(
                    ps[bank][:, 0:256], ones_r[:, :], brow,
                    start=False, stop=True), inc=tP)
                dst = dstT[:, lt * 256:(lt + 1) * 256]
                if is_k:
                    ew = [(tP, t_this)]
                    if prev_vphi is not None:
                        ew.append((aV, prev_vphi))
                    pr.op("SC", lambda e, bank=bank: e.activation(
                        lnAb[:, 0:256], ps[bank][:, 0:256], AF.Exp),
                        waits=ew, inc=aS)
                    a_r = pr.op("SC", lambda e, bank=bank: e.activation(
                        lnAb[:, 256:512], ps[bank][:, 0:256], AF.Relu),
                        inc=aS)
                    prev_vphi = pr.op("VE", lambda e, dst=dst:
                                      e.scalar_tensor_tensor(
                                          dst, lnAb[:, 0:256], 1.0,
                                          lnAb[:, 256:512], ALU.min, ALU.add),
                                      waits=[(aS, a_r)], inc=aV)
                    evA[bank] = [(aS, a_r)]
                else:
                    a_cp = pr.op("SC", lambda e, dst=dst, bank=bank: e.activation(
                        dst, ps[bank][:, 0:256], AF.Copy),
                        waits=[(tP, t_this)], inc=aS)
                    evA[bank] = [(aS, a_cp)]
            if is_k:
                t_kmm.append(t_this)
                vphi_blk.append(prev_vphi)
            else:
                t_vmm.append(t_this)
                acp_blk.append(aS.n)
        t_last = None
        for sub in range(2):
            hh = 2 * b + sub
            bank = 4 + sub
            bw = [(aV, vphi_blk[b]), (aS, acp_blk[b])]
            if b == 0:
                bw += rel45
            if b >= 1:
                bw.append((aV, t_kvh[b - 1][1][sub]))
            for lt in range(8):
                t_last = pr.op(
                    "TE", lambda e, lt=lt, sub=sub, b=b, bank=bank: e.matmul(
                        ps[bank][:, 0:P],
                        kts[b % 2][:, lt * 256 + sub * P: lt * 256 + (sub + 1) * P],
                        vts[b % 2][:, lt * 256 + sub * P: lt * 256 + (sub + 1) * P],
                        start=(lt == 0), stop=(lt == 7)),
                    waits=bw if lt == 0 else [],
                    inc=(tP if (sub == 1 and lt == 7) else None))
        v_evs = []
        for sub in range(2):
            hh = 2 * b + sub
            sK = sKa if sub == 0 else sKb
            vw = [(tP, t_last)]
            if b >= 1:
                vw.append((sK, s_pack[2 * (b - 1) + sub]))
            v_ev = pr.op("VE", lambda e, sub=sub, bank=4 + sub: e.tensor_copy(
                kvst[:, sub * P:(sub + 1) * P], ps[bank][:, 0:P]),
                waits=vw, inc=aV)
            v_evs.append(v_ev)
            s_pack.append(pr.op("SY", lambda e, hh=hh, sub=sub: e.dma_start(
                out=cc_in.ap()[hh * P:(hh + 1) * P, :],
                in_=kvst[:, sub * P:(sub + 1) * P]),
                waits=[(aV, v_ev)], inc=sK, by=16))
        t_kvh.append((t_last, v_evs))
        ksbank = 6 + b // 2
        kcol = (b % 2) * 256
        t_kb = None
        for lt in range(8):
            kw = [(aV, vphi_blk[b])]
            if b == 0:
                kw.append((aS, a_mrb))
            t_kb = pr.op("TE", lambda e, lt=lt, b=b, ksbank=ksbank, kcol=kcol:
                         e.matmul(
                             ps[ksbank][0:1, kcol:kcol + 256], ones_c[:, :],
                             kts[b % 2][:, lt * 256:(lt + 1) * 256],
                             start=(lt == 0), stop=(lt == 7)),
                         waits=kw if lt == 0 else [],
                         inc=(tP if lt == 7 else None))
        t_ksum.append(t_kb)
    t_ks = t_ksum[-1]
    a_ks = None
    for h in range(2):
        a_ks = pr.op("SC", lambda e, h=h: e.activation(
            r_scr[0:1, h * LH:(h + 1) * LH], ps[6 + h][0:1, :], AF.Copy),
            waits=[(tP, t_ks)], inc=aS)
    s_ks = None
    for hh in range(NHEAD):
        s_ks = pr.op("SY", lambda e, hh=hh: e.dma_start(
            out=cc_in.ap()[DK + hh:DK + hh + 1, :],
            in_=r_scr[0:1, hh * P:(hh + 1) * P]),
            waits=[(aS, a_ks)], inc=sKs, by=16)

    cc_waits = [(sKs, s_ks), (sKa, sKa.n), (sKb, sKb.n)]

    colq5 = [arena[:, 4096 * i:4096 * (i + 1)] for i in range(4)] + [qex[:, :]]
    qsems = [sWa, sQb, sBa, sBb, sQc]
    t_qcol = []
    ev_q = {}
    g_qcol = []
    v_phi_q = []
    for j in range(NCH):
        sW = qsems[j % 5]
        if j < 2:
            waits = [(tP, t_kmm[3])]
        elif j < 4:
            waits = [(tP, t_vmm[3])]
        elif j == 4:
            waits = [(tP, t_ks), (tP, t_kvh[3][0])]
        else:
            waits = [(tP, t_qcol[j - 5])]
        g_qcol.append(pr.op("GP", lambda e, j=j: e.dma_start(
            out=colq5[j % 5], in_=d_wq[j]), waits=waits, inc=sW, by=16))
        bank0, bank1 = (j % 2) * 2, (j % 2) * 2 + 1
        bw = [(sW, g_qcol[j])]
        for k in (bank0, bank1):
            bw += ev_q.get(k, [])
        tq = None
        for c in range(NCH):
            pr.op("TE", lambda e, j=j, c=c, bank0=bank0: e.matmul(
                ps[bank0][:, :], colq5[j % 5][:, c * P:(c + 1) * P],
                xn[:, c * NT: c * NT + LH],
                start=(c == 0), stop=(c == NCH - 1)),
                waits=bw if c == 0 else [])
            tq = pr.op("TE", lambda e, j=j, c=c, bank1=bank1: e.matmul(
                ps[bank1][:, :], colq5[j % 5][:, c * P:(c + 1) * P],
                xn[:, c * NT + LH: c * NT + NT],
                start=(c == 0), stop=(c == NCH - 1)),
                inc=(tP if c == NCH - 1 else None))
        t_qcol.append(tq)
        for h in range(2):
            bank = bank0 if h == 0 else bank1
            u = h
            s_e = qsc[:, (2 * u) * LH:(2 * u + 1) * LH]
            s_r = qsc[:, (2 * u + 1) * LH:(2 * u + 2) * LH]
            ew = [(tP, t_qcol[j]), (sP, s_pre)]
            if j >= 1:
                ew.append((aV, v_phi_q[2 * (j - 1) + h]))
            a_e = pr.op("SC", lambda e, bank=bank, j=j, s_e=s_e: e.activation(
                s_e, ps[bank][:, :], AF.Exp, bias=bqcol[:, j:j + 1]),
                waits=ew, inc=aS)
            a_r = pr.op("SC", lambda e, bank=bank, j=j, s_r=s_r: e.activation(
                s_r, ps[bank][:, :], AF.Relu, bias=bqcol[:, j:j + 1]), inc=aS)
            dst = qb[:, j * NT + h * LH: j * NT + (h + 1) * LH]
            vp = pr.op("VE", lambda e, dst=dst, s_e=s_e, s_r=s_r:
                       e.scalar_tensor_tensor(
                           dst, s_e, 1.0, s_r, ALU.min, ALU.add),
                       waits=[(aS, a_r)], inc=aV)
            v_phi_q.append(vp)
            ev_q[bank] = [(aS, a_r)]

        if j == 4:
            c_cc = pr.op("GP", lambda e: e.collective_compute(
                "AllReduce", ALU.add,
                replica_groups=[list(range(N_CORES))],
                ins=[cc_in.ap()], outs=[cc_out.ap()]),
                waits=cc_waits, inc=cC, by=1)

        if j == 16:
            g_kvtb = pr.op("GP", lambda e: e.dma_start(
                out=kvtb[:, :],
                in_=cc_out.ap()[0:DK, :].rearrange("(h d) v -> d h v", d=P)),
                waits=[(cC, c_cc)], inc=sX, by=16)
            s_ksb = pr.op("SY", lambda e: e.dma_start(
                out=ksb[:, :],
                in_=cc_out.ap()[DK:DK + NE, :].rearrange("h v -> v h"),
                allow_slow_non_contiguous=True),
                waits=[(cC, c_cc)], inc=sKR, by=16)
            v_k1 = pr.op("VE", lambda e: e.tensor_single_scalar(
                ksb_r[:, :], ksb[:, :], 1e-10, ALU.add),
                waits=[(sKR, s_ksb)], inc=aV)
            pr.op("VE", lambda e: e.reciprocal(ksb_r[:, :], ksb_r[:, :]),
                  waits=[(aV, v_k1)], inc=aV)
            v_ksr = aV.n

        if j == 20:
            t_rt = []
            for h in range(2):
                tr = None
                for c in range(NCH):
                    tr = pr.op("TE", lambda e, h=h, c=c: e.matmul(
                        ps[6][0:8, :] if h == 0 else ps[7][0:8, :],
                        gwT[:, c * NE:(c + 1) * NE],
                        xn[:, c * NT + h * LH: c * NT + h * LH + LH],
                        start=(c == 0), stop=(c == NCH - 1)),
                        waits=([(sP, s_pre), (aS, a_ks), (cC, c_cc)]
                               if c == 0 else []),
                        inc=(tP if c == NCH - 1 else None))
                t_rt.append(tr)
        if j == 21:
            a_exp = []
            for h in range(2):
                a_exp.append(pr.op("SC", lambda e, h=h: e.activation(
                    expL[0:8, h * LH:(h + 1) * LH], ps[6 + h][0:8, :], AF.Exp),
                    waits=[(tP, t_rt[h])], inc=aS))
        if j == 22:
            t_z = []
            for h in range(2):
                t_z.append(pr.op("TE", lambda e, h=h: e.matmul(
                    ps[6 + h][0:1, :], ones_c[0:8, :],
                    expL[0:8, h * LH:(h + 1) * LH],
                    start=True, stop=True), waits=[(aS, a_exp[h])], inc=tP))
        if j == 23:
            a_z = None
            for h in range(2):
                a_z = pr.op("SC", lambda e, h=h: e.activation(
                    r_ex2[0:1, h * LH:(h + 1) * LH], ps[6 + h][0:1, :],
                    AF.Copy), waits=[(tP, t_z[h])], inc=aS)
            a_zl = pr.op("SC", lambda e: e.activation(
                r_mean[0:1, :], r_ex2[0:1, :], AF.Ln),
                waits=[(aS, a_z)], inc=aS)
            a_zi = pr.op("SC", lambda e: e.activation(
                r_mean[0:1, :], r_mean[0:1, :], AF.Exp, scale=-1.0),
                waits=[(aS, a_zl)], inc=aS)
        if j == 24:
            t_zb = []
            for h in range(2):
                t_zb.append(pr.op("TE", lambda e, h=h: e.matmul(
                    ps[6 + h][0:8, :], onesf[:, 1:9],
                    r_mean[0:1, h * LH:(h + 1) * LH],
                    start=True, stop=True), waits=[(aS, a_zi)], inc=tP))
            v_rout = None
            for h in range(2):
                v_rout = pr.op("VE", lambda e, h=h: e.tensor_tensor(
                    routb[0:8, h * LH:(h + 1) * LH],
                    expL[0:8, h * LH:(h + 1) * LH],
                    ps[6 + h][0:8, :], ALU.mult),
                    waits=[(tP, t_zb[h])], inc=aV)
    v_q_done = aV.n

    bank_ev = {b: list(ev_q.get(b, [])) for b in range(6)}
    bank_ev[6] = [(aV, v_rout)]
    bank_ev[7] = [(aV, v_rout)]

    t_attn_done = None
    for qh in range(NQH):
        kvh = qh // 4
        for h in range(2):
            u = 2 * qh + h
            bank = u % 6
            bw = [(sX, g_kvtb), (aV, v_phi_q[u])] + bank_ev.get(bank, [])
            t_a = pr.op("TE", lambda e, qh=qh, h=h, kvh=kvh, bank=bank: e.matmul(
                ps[bank][:, :], kvtb[:, kvh * P:(kvh + 1) * P],
                qb[:, qh * NT + h * LH: qh * NT + (h + 1) * LH],
                start=True, stop=True), waits=bw, inc=tP)
            t_attn_done = t_a
            dst = qb[:, qh * NT + h * LH: qh * NT + (h + 1) * LH]
            if u % 2 == 0:
                v_a = pr.op("VE", lambda e, dst=dst, bank=bank, kvh=kvh:
                            e.tensor_single_scalar(
                                dst, ps[bank][:, :], ksb_r[:, kvh:kvh + 1],
                                ALU.mult),
                            waits=[(tP, t_a), (aV, v_ksr)], inc=aV)
                bank_ev[bank] = [(aV, v_a)]
            else:
                s_a = pr.op("SC", lambda e, dst=dst, bank=bank, kvh=kvh:
                            e.activation(
                                dst, ps[bank][:, :], AF.Copy,
                                scale=ksb_r[:, kvh:kvh + 1]),
                            waits=[(tP, t_a), (aV, v_ksr)], inc=aS)
                bank_ev[bank] = [(aS, s_a)]
    attn_v_done = aV.n
    attn_s_done = aS.n

    g_ict = []
    for j in range(8):
        g_ict.append(pr.op("GP", lambda e, j=j: e.dma_start(
            out=icst[:, (j % 8) * P:(j % 8 + 1) * P],
            in_=d_ics[:, j * P:(j + 1) * P]), inc=sIq[j % 8], by=16))

    colw = [arena[:, 0:4096], arena[:, 4096:8192]]
    colb = [arena[:, 8192:12288], arena[:, 12288:16384]]
    g_wo = []
    g_wb = []
    g_oct = []
    g_res = []
    a_sgn = []
    for j in range(2):
        sW = sWa if j % 2 == 0 else sWb
        sB = sBa if j % 2 == 0 else sBb
        sR = sRa if j % 2 == 0 else sRb
        sO = sOa if j % 2 == 0 else sOb
        g_oct.append(pr.op("GP", lambda e, j=j: e.dma_start(
            out=ocst[:, (j % 2) * P:(j % 2 + 1) * P],
            in_=d_ocs[:, j * P:(j + 1) * P]),
            waits=[(aV, v_rout)], inc=sO, by=16))
        g_wo.append(pr.op("GP", lambda e, j=j: e.dma_start(
            out=colw[j % 2], in_=d_wo[j]),
            waits=[(tP, t_qcol[-1])], inc=sW, by=16))
        g_wb.append(pr.op("GP", lambda e, j=j: e.dma_start(
            out=colb[j % 2], in_=d_wb[j]),
            waits=[(tP, t_qcol[-1])], inc=sB, by=16))
        a_sgn.append(pr.op("VE", lambda e, j=j: e.tensor_scalar(
            colb[j % 2][:, :], colb[j % 2][:, :], 0.0, 0.5,
            ALU.is_ge, ALU.subtract),
            waits=[(sB, g_wb[j])], inc=aV))
    for j in range(2):
        g_res.append(pr.op("GP", lambda e, j=j: e.dma_start(
            out=residb[:, (j % 2) * NT:(j % 2 + 1) * NT],
            in_=d_xT[:, j * NT:(j + 1) * NT]),
            waits=[(tP, t_attn_done)], inc=(sRa if j % 2 == 0 else sRb), by=16))

    pre_thunks = []
    for h in range(2):
        u = h % 2
        bS = 2 if u == 0 else 5
        bO = 4 if u == 0 else 7
        bw0 = [(sWa, g_wo[0]), (aV, attn_v_done), (aS, attn_s_done)]
        bw0 += bank_ev[bS]
        for c in range(NCH):
            pre_thunks.append((
                "S", h,
                lambda e, h=h, c=c, bS=bS: e.matmul(
                    ps[bS][:, :], colw[0][:, c * P:(c + 1) * P],
                    qb[:, c * NT + h * LH: c * NT + (h + 1) * LH],
                    start=(c == 0), stop=(c == NCH - 1)),
                bw0 if c == 0 else [], c == NCH - 1))
        bwo = [(sOa, g_oct[0]), (aV, v_rout)] + bank_ev[bO]
        pre_thunks.append((
            "O", h,
            lambda e, h=h, bO=bO: e.matmul(
                ps[bO][:, :], ocst[0:8, 0:P],
                routb[0:8, h * LH:(h + 1) * LH],
                start=True, stop=True), bwo, True))

    pre_tS = {}
    pre_tO = {}
    pre_iter = iter(pre_thunks)

    def drain_pre(n):
        for _ in range(n):
            item = next(pre_iter, None)
            if item is None:
                return
            kind, h, fn, w, is_end = item
            v = pr.op("TE", fn, waits=w, inc=(tP if is_end else None))
            if is_end:
                (pre_tS if kind == "S" else pre_tO)[h] = v

    xs_banks = [6, 0, 1, 3]
    t_ic = []
    for j in range(NCH):
        sI = sIq[j % 8]
        if j >= 8:
            g_ict.append(pr.op("GP", lambda e, j=j: e.dma_start(
                out=icst[:, (j % 8) * P:(j % 8 + 1) * P],
                in_=d_ics[:, j * P:(j + 1) * P]),
                waits=[(tP, t_ic[j - 8])], inc=sI, by=16))
        for h in range(2):
            bank = xs_banks[(2 * j + h) % 4]
            bw = [(sI, g_ict[j]), (aV, v_rout), (tP, t_qcol[-1])]
            bw += bank_ev.get(bank, [])
            t_i = pr.op("TE", lambda e, j=j, h=h, bank=bank: e.matmul(
                ps[bank][:, :], icst[0:8, (j % 8) * P:(j % 8 + 1) * P],
                routb[0:8, h * LH:(h + 1) * LH],
                start=True, stop=True), waits=bw, inc=tP)
            sl = slice(j * NT + h * LH, j * NT + (h + 1) * LH)
            vx = pr.op("VE", lambda e, sl=sl, bank=bank: e.tensor_tensor(
                xn[:, sl], xn[:, sl], ps[bank][:, :], ALU.mult),
                waits=[(tP, t_i)], inc=aV)
            bank_ev[bank] = [(aV, vx)]
            if j >= 6:
                drain_pre(2)
        t_ic.append(t_i)
    drain_pre(len(pre_thunks))
    xs_v_done = aV.n

    t_scol = []
    t_mcol = []
    t_oc = []
    s_store = []
    ev_fin = {
        ("M", 0): bank_ev[3][0],
        ("M", 1): bank_ev[6][0],
    }
    v_out = {}
    for j in range(NCH):
        sW = sWa if j % 2 == 0 else sWb
        sB = sBa if j % 2 == 0 else sBb
        sR = sRa if j % 2 == 0 else sRb
        sO = sOa if j % 2 == 0 else sOb
        sS = sSa if j % 2 == 0 else sSb
        if j >= 2:
            g_wo.append(pr.op("GP", lambda e, j=j: e.dma_start(
                out=colw[j % 2], in_=d_wo[j]),
                waits=[(tP, t_scol[j - 2])], inc=sW, by=16))
            g_wb.append(pr.op("GP", lambda e, j=j: e.dma_start(
                out=colb[j % 2], in_=d_wb[j]),
                waits=[(tP, t_mcol[j - 2])], inc=sB, by=16))
            a_sgn.append(pr.op("VE", lambda e, j=j: e.tensor_scalar(
                colb[j % 2][:, :], colb[j % 2][:, :], 0.0, 0.5,
                ALU.is_ge, ALU.subtract),
                waits=[(sB, g_wb[j])], inc=aV))
            g_res.append(pr.op("GP", lambda e, j=j: e.dma_start(
                out=residb[:, (j % 2) * NT:(j % 2 + 1) * NT],
                in_=d_xT[:, j * NT:(j + 1) * NT]),
                waits=[(aV, v_out[(j - 2, 1)])], inc=sR, by=16))
            g_oct.append(pr.op("GP", lambda e, j=j: e.dma_start(
                out=ocst[:, (j % 2) * P:(j % 2 + 1) * P],
                in_=d_ocs[:, j * P:(j + 1) * P]),
                waits=[(tP, t_oc[j - 2])], inc=sO, by=16))
        for h in range(2):
            u = (2 * j + h) % 2
            bS, bM, bO = (2, 3, 4) if u == 0 else (5, 6, 7)
            s_os = scr[:, (3 * u) * LH:(3 * u + 1) * LH]
            s_P = scr[:, (3 * u + 1) * LH:(3 * u + 2) * LH]
            s_g = scr[:, (3 * u + 2) * LH:(3 * u + 3) * LH]
            if j == 0:
                tS = pre_tS[h]
                tO = pre_tO[h]
            else:
                bw = [(sW, g_wo[j]), (aV, attn_v_done), (aS, attn_s_done)]
                if ("S", u) in ev_fin:
                    bw.append(ev_fin[("S", u)])
                tS = None
                for c in range(NCH):
                    tS = pr.op("TE", lambda e, j=j, h=h, c=c, bS=bS: e.matmul(
                        ps[bS][:, :], colw[j % 2][:, c * P:(c + 1) * P],
                        qb[:, c * NT + h * LH: c * NT + (h + 1) * LH],
                        start=(c == 0), stop=(c == NCH - 1)),
                        waits=bw if c == 0 else [],
                        inc=(tP if c == NCH - 1 else None))
                bw = [(sO, g_oct[j]), (aV, v_rout)]
                if ("O", u) in ev_fin:
                    bw.append(ev_fin[("O", u)])
                tO = pr.op("TE", lambda e, j=j, h=h, bO=bO: e.matmul(
                    ps[bO][:, :], ocst[0:8, (j % 2) * P:(j % 2 + 1) * P],
                    routb[0:8, h * LH:(h + 1) * LH],
                    start=True, stop=True), waits=bw, inc=tP)
            if h == 1:
                t_scol.append(tS)
                t_oc.append(tO)
            bw = [(aV, a_sgn[j]), (aV, xs_v_done)]
            if ("M", u) in ev_fin:
                bw.append(ev_fin[("M", u)])
            tM = None
            for c in range(NCH):
                tM = pr.op("TE", lambda e, j=j, h=h, c=c, bM=bM: e.matmul(
                    ps[bM][:, :], colb[j % 2][:, c * P:(c + 1) * P],
                    xn[:, c * NT + h * LH: c * NT + (h + 1) * LH],
                    start=(c == 0), stop=(c == NCH - 1)),
                    waits=bw if c == 0 else [],
                    inc=(tP if c == NCH - 1 else None))
            if h == 1:
                t_mcol.append(tM)
            a_os = pr.op("SC", lambda e, bO=bO, s_os=s_os: e.activation(
                s_os, ps[bO][:, :], AF.Copy), waits=[(tP, tO)], inc=aS)
            v_P = pr.op("VE", lambda e, bM=bM, s_P=s_P, s_os=s_os: e.tensor_tensor(
                s_P, ps[bM][:, :], s_os, ALU.mult),
                waits=[(tP, tM), (aS, a_os)], inc=aV)
            a_g = pr.op("SC", lambda e, j=j, s_P=s_P, s_g=s_g: e.activation(
                s_g, s_P, AF.Silu, bias=bfcol[:, j:j + 1], scale=2.0),
                waits=[(aV, v_P), (sP, s_pre)], inc=aS)
            v_sg = pr.op("VE", lambda e, j=j, bS=bS, s_os=s_os, s_g=s_g:
                         e.scalar_tensor_tensor(
                             s_os, ps[bS][:, :], bocol[:, j:j + 1], s_g,
                             ALU.add, ALU.mult),
                         waits=[(tP, tS), (aS, a_g), (sP, s_pre)], inc=aV)
            ow = [(aV, v_sg), (sR, g_res[j])]
            if j >= 2:
                ow.append((sS, s_store[j - 2]))
            vo = pr.op("VE", lambda e, j=j, h=h, s_os=s_os: e.tensor_tensor(
                outst[:, (j % 2) * NT + h * LH:(j % 2) * NT + (h + 1) * LH],
                s_os, residb[:, (j % 2) * NT + h * LH:(j % 2) * NT + (h + 1) * LH],
                ALU.add), waits=ow, inc=aV)
            v_out[(j, h)] = vo
            ev_fin[("S", u)] = (aV, v_sg)
            ev_fin[("M", u)] = (aV, v_P)
            ev_fin[("O", u)] = (aS, a_os)
        s_store.append(pr.op("SY", lambda e, j=j: e.dma_start(
            out=d_out[j], in_=outst[:, (j % 2) * NT:(j % 2 + 1) * NT]),
            waits=[(aV, v_out[(j, 1)])], inc=sS, by=16))

    pr.op("SY", lambda e: e.nop(), waits=[(sSa, sSa.n), (sSb, sSb.n)])

    from contextlib import ExitStack
    with ExitStack() as ctx:
        for s in sems:
            s.h = ctx.enter_context(nc.semaphore(s.name))
        block = ctx.enter_context(nc.Block())

        def runner(key):
            def cb(eng):
                hw = {}
                for (k, fn, waits, inc, by) in pr.ops:
                    if k != key:
                        continue
                    for (s, v) in waits:
                        if hw.get(s.name, 0) < v:
                            eng.wait_ge(s.h, v)
                            hw[s.name] = v
                    inst = fn(eng)
                    if inc is not None and inst is not None:
                        if by is None:
                            inst.then_inc(inc.h)
                        else:
                            inst.then_inc(inc.h, by)
            return cb

        block.sync(runner("SY"))
        block.scalar(runner("SC"))
        block.vector(runner("VE"))
        block.tensor(runner("TE"))
        block.gpsimd(runner("GP"))

    return nc


_NC_CACHE = []


def kernel(**inputs):
    f = np.float32

    def g(name):
        return np.asarray(inputs[name], dtype=f)

    x = g("L0_attn_hidden_states")[0]
    Wq, bq = g("L0_attn_Wq"), g("L0_attn_bq")
    Wk, bk = g("L0_attn_Wk"), g("L0_attn_bk")
    Wv, bv = g("L0_attn_Wv"), g("L0_attn_bv")
    Wo, bo = g("L0_attn_Wo"), g("L0_attn_bo")
    Wf, bf = g("L0_ffn_weight"), g("L0_ffn_bias")
    gw = g("L0_ffn_gate_w")
    ics = g("L0_ffn_in_channel_scale")
    ocs = g("L0_ffn_out_channel_scale")

    def colt(W):
        O = W.shape[0]
        return np.ascontiguousarray(
            W.T.reshape(NCH, P, O // P, P).transpose(2, 1, 0, 3)
        ).reshape(O // P, P, NCH * P)

    wq_t = colt(Wq)
    wo_t = colt(Wo)
    wb_t = colt(Wf)

    def blkt(W):
        return np.ascontiguousarray(
            W.T.reshape(NCH, P, 4, 256).transpose(2, 1, 0, 3)
        ).reshape(4, P, NCH * 256)

    wk_t = blkt(Wk)
    wv_t = blkt(Wv)
    gw_t = np.ascontiguousarray(
        gw.T.reshape(NCH, P, NE).transpose(1, 0, 2)).reshape(P, NCH * NE)
    bqc = np.ascontiguousarray(bq.reshape(NCH, P).T)
    boc = np.ascontiguousarray(bo.reshape(NCH, P).T)
    bfc = np.ascontiguousarray(bf.reshape(NCH, P).T)
    bk2 = np.ascontiguousarray(bk.reshape(1, DK))
    bv2 = np.ascontiguousarray(bv.reshape(1, DK))
    ics = np.ascontiguousarray(ics)
    ocs = np.ascontiguousarray(ocs)

    xT = np.ascontiguousarray(x.T)
    in_maps = []
    for c in range(N_CORES):
        xc = np.ascontiguousarray(
            xT[:, c * NT:(c + 1) * NT].reshape(NCH, P, NT)
            .transpose(1, 0, 2)).reshape(P, NCH * NT)
        in_maps.append({
            "xT": xc, "wq": wq_t, "wo": wo_t, "wb": wb_t,
            "wk": wk_t, "wv": wv_t, "gw": gw_t, "ics": ics, "ocs": ocs,
            "bqc": bqc, "boc": boc, "bfc": bfc, "bk": bk2, "bv": bv2,
        })

    if not _NC_CACHE:
        _NC_CACHE.append(build_program())
    nc = _NC_CACHE[0]
    res = run_bass_kernel_spmd(nc, in_maps, core_ids=list(range(N_CORES)))

    out = np.empty((8192, D), dtype=f)
    for c in range(N_CORES):
        oc = np.asarray(res.results[c]["outT"], dtype=f)
        out[c * NT:(c + 1) * NT, :] = oc.reshape(D, NT).T
    return out.reshape(1, 8192, D)


# version 22
# speedup vs baseline: 1.0781x; 1.0087x over previous
import numpy as np
import concourse.bass as bass
import concourse.mybir as mybir
from concourse.bass_utils import run_bass_kernel_spmd

F32 = mybir.dt.float32
BF16 = mybir.dt.bfloat16
AF = mybir.ActivationFunctionType
ALU = mybir.AluOpType

P = 128
NT = 1024
LH = 512
D = 4096
NCH = 32
DK = 1024
NHEAD = 8
NQH = 32
NE = 8
N_CORES = 8

SBUF_BYTES = 229376


class S:
    __slots__ = ("h", "n", "name")

    def __init__(self, h, name):
        self.h = h
        self.n = 0
        self.name = name


class Prog:
    def __init__(self):
        self.ops = []

    def op(self, eng, fn, waits=(), inc=None, by=1):
        for (s, v) in waits:
            assert v <= s.n, f"wait on future value {s.name} {v} > {s.n}"
        self.ops.append((eng, fn, [(s, v) for (s, v) in waits if v > 0], inc, by))
        if inc is not None:
            inc.n += (by if by is not None else 1)
            return inc.n
        return None


def build_program():
    nc = bass.Bass("TRN2", target_bir_lowering=False)

    d_xT = nc.dram_tensor("xT", [P, NCH * NT], F32, kind="ExternalInput").ap()
    d_wq = nc.dram_tensor("wq", [NCH, P, D], F32, kind="ExternalInput").ap()
    d_wo = nc.dram_tensor("wo", [NCH, P, D], F32, kind="ExternalInput").ap()
    d_wb = nc.dram_tensor("wb", [NCH, P, D], F32, kind="ExternalInput").ap()
    d_wk = nc.dram_tensor("wk", [4, P, 32 * 256], F32, kind="ExternalInput").ap()
    d_wv = nc.dram_tensor("wv", [4, P, 32 * 256], F32, kind="ExternalInput").ap()
    d_gw = nc.dram_tensor("gw", [P, NCH * NE], F32, kind="ExternalInput").ap()
    d_ics = nc.dram_tensor("ics", [P, D], F32, kind="ExternalInput").ap()
    d_ocs = nc.dram_tensor("ocs", [P, D], F32, kind="ExternalInput").ap()
    d_eyeT = nc.dram_tensor("eyeT", [NE, P], F32, kind="ExternalInput").ap()
    d_bqc = nc.dram_tensor("bqc", [P, NCH], F32, kind="ExternalInput").ap()
    d_boc = nc.dram_tensor("boc", [P, NCH], F32, kind="ExternalInput").ap()
    d_bfc = nc.dram_tensor("bfc", [P, NCH], F32, kind="ExternalInput").ap()
    d_bk = nc.dram_tensor("bk", [1, DK], F32, kind="ExternalInput").ap()
    d_bv = nc.dram_tensor("bv", [1, DK], F32, kind="ExternalInput").ap()
    d_out = nc.dram_tensor("outT", [NCH, P, NT], F32, kind="ExternalOutput").ap()
    cc_in = nc.dram_tensor("cc_in", [DK + NE, P], F32, kind="Internal")
    cc_out = nc.dram_tensor("cc_out", [DK + NE, P], F32, kind="Internal")

    resv = nc.alloc_sbuf_tensor("resv", [P, 212096], mybir.dt.uint8)
    BASE = (nc.lookup_mloc(resv).addr + 127) & ~127

    def at(name, shape, dtype, rel):
        off = BASE + rel
        end = off + shape[1] * (2 if dtype == BF16 else 4)
        assert end <= SBUF_BYTES, (name, off, end)
        return nc.alloc_sbuf_tensor_at(name, shape, dtype, offset=off)

    xn = at("xn", [P, NCH * NT], BF16, 0)
    qb = at("qb", [P, NCH * NT], BF16, 65536)
    arena = at("arena", [P, 16384], BF16, 131072)
    ktm = at("ktm", [P, 2 * 8 * 256], BF16, 163840)
    vtm = at("vtm", [P, 2 * 8 * 256], BF16, 172032)
    residb = at("residb", [P, 2 * NT], BF16, 163840)
    outst = at("outst", [P, 2 * NT], F32, 167936)
    scr = at("scr", [P, 6 * LH], F32, 176128)
    qex = at("qex", [P, 4096], BF16, 163840)
    kvtb = at("kvtb", [P, NHEAD * P], BF16, 182272)
    rstd_sb = at("rstd_sb", [P, NT], BF16, 176128)
    vA = at("vA", [P, NT], BF16, 178176)
    vB = at("vB", [P, NT], BF16, 180224)
    pA = at("pA", [P, NT], BF16, 182272)
    pB = at("pB", [P, NT], BF16, 184320)
    lnAb = at("lnAb", [P, LH], BF16, 186368)
    qsc = at("qsc", [P, 4 * LH], BF16, 176128)
    r_mean = at("r_mean", [1, NT], F32, 188416)
    r_ex2 = at("r_ex2", [1, NT], F32, 192512)
    r_scr = at("r_scr", [1, NT], F32, 196608)
    routb = at("routb", [NE, NT], BF16, 200704)
    bkb = at("bkb", [1, DK], BF16, 202752)
    bvb = at("bvb", [1, DK], BF16, 204800)
    gwT = at("gwT", [P, NCH * NE], BF16, 206848)
    kvst = at("kvst", [P, 2 * P], F32, 207360)
    kvscr = at("kvscr", [P, 256], F32, 208384)
    xsqb = at("xsqb", [P, NT], BF16, 207360)
    mrsb = at("mrsb", [P, NT], BF16, 207360)
    expL = at("expL", [NE, NT], BF16, 207360)
    xsqb2 = at("xsqb2", [P, NT], BF16, 176128)
    ones_r = at("ones_r", [1, P], BF16, 209408)
    onesf = at("onesf", [1, 129], F32, 209664)
    ones_c = at("ones_c", [P, 1], BF16, 210208)
    bqcol = at("bqcol", [P, NCH], F32, 210240)
    bocol = at("bocol", [P, NCH], F32, 210368)
    bfcol = at("bfcol", [P, NCH], F32, 210496)
    ksb = at("ksb", [P, NHEAD], F32, 210624)
    ksb_r = at("ksb_r", [P, NHEAD], F32, 210656)
    epsb = at("epsb", [1, 1], F32, 210688)
    icst = at("icst", [P, 8 * P], BF16, 210720)
    ocst = at("ocst", [P, 2 * P], BF16, 208384)
    eyeTb = at("eyeTb", [NE, P], BF16, 172032)
    routf = at("routf", [P, NT], BF16, 202752)

    ps = [nc.alloc_psum_tensor(f"ps{i}", [P, LH], F32) for i in range(8)]

    pr = Prog()
    sems = []

    def mk(name):
        s = S(None, name)
        sems.append(s)
        return s

    sP = mk("sP")
    sX = mk("sX")
    sX1 = mk("sX1")
    sX2 = mk("sX2")
    sX3 = mk("sX3")
    sX4 = mk("sX4")
    sX5 = mk("sX5")
    sX6 = mk("sX6")
    sX7 = mk("sX7")
    aP = mk("aP")
    sWa = mk("sWa")
    sWb = mk("sWb")
    sBa = mk("sBa")
    sBb = mk("sBb")
    sIq = [mk(f"sI{k}") for k in range(8)]
    sOa = mk("sOa")
    sOb = mk("sOb")
    sKa = mk("sKa")
    sKb = mk("sKb")
    sKs = mk("sKs")
    sKR = mk("sKR")
    sRa = mk("sRa")
    sRb = mk("sRb")
    sSa = mk("sSa")
    sSb = mk("sSb")
    sQb = mk("sQb")
    sQc = mk("sQc")
    sEy = mk("sEy")
    tP = mk("tP")
    aS = mk("aS")
    aV = mk("aV")
    cC = mk("cC")

    ve_ones = pr.op("VE", lambda e: e.memset(ones_r[:, :], 1.0), inc=aV)
    pr.op("VE", lambda e: e.memset(onesf[:, :], 1.0), inc=aV)
    pr.op("VE", lambda e: e.memset(ones_c[:, :], 1.0), inc=aV)
    ve_ones = pr.op("VE", lambda e: e.memset(epsb[:, :], 1e-5), inc=aV)

    sXq = [sX, sX1, sX2, sX3, sX4, sX5, sX6, sX7]
    QW = 4 * NT
    g_x4 = []
    for qq in range(8):
        g_x4.append(pr.op("GP", lambda e, qq=qq: e.dma_start(
            out=xn[:, qq * QW:(qq + 1) * QW],
            in_=d_xT[:, qq * QW:(qq + 1) * QW]), inc=sXq[qq], by=16))

    pr.op("GP", lambda e: e.dma_start(out=gwT[:, :], in_=d_gw), inc=sP, by=16)
    pr.op("GP", lambda e: e.dma_start(out=bkb[:, :], in_=d_bk), inc=sP, by=16)
    pr.op("GP", lambda e: e.dma_start(out=bvb[:, :], in_=d_bv), inc=sP, by=16)
    pr.op("GP", lambda e: e.dma_start(out=bqcol[:, :], in_=d_bqc), inc=sP, by=16)
    pr.op("GP", lambda e: e.dma_start(out=bocol[:, :], in_=d_boc), inc=sP, by=16)
    s_pre = pr.op("GP", lambda e: e.dma_start(out=bfcol[:, :], in_=d_bfc),
                  inc=sP, by=16)

    t_stats = []
    a_sq = []
    for c in range(NCH):
        xq = xsqb if c % 2 == 0 else xsqb2
        waits = [(sXq[c // 4], g_x4[c // 4])]
        if c >= 2:
            waits.append((tP, t_stats[c - 2]))
        a_sq.append(pr.op("SC", lambda e, c=c, xq=xq: e.activation(
            xq[:, :], xn[:, c * NT:(c + 1) * NT], AF.Square), waits=waits, inc=aS))
        for h in range(2):
            pr.op("TE", lambda e, c=c, h=h: e.matmul(
                ps[h][0:1, :], ones_c[:, :],
                xn[:, c * NT + h * LH: c * NT + (h + 1) * LH],
                start=(c == 0), stop=(c == NCH - 1)),
                waits=[(sXq[c // 4], g_x4[c // 4]), (aV, ve_ones)])
        pr.op("TE", lambda e, c=c, xq=xq: e.matmul(
            ps[2][0:1, :], ones_c[:, :], xq[:, 0:LH],
            start=(c == 0), stop=(c == NCH - 1)),
            waits=[(aS, a_sq[c])])
        t_stats.append(pr.op("TE", lambda e, c=c, xq=xq: e.matmul(
            ps[3][0:1, :], ones_c[:, :], xq[:, LH:NT],
            start=(c == 0), stop=(c == NCH - 1)), inc=tP))

    a_mean = None
    for h in range(2):
        a_mean = pr.op("SC", lambda e, h=h: e.activation(
            r_mean[0:1, h * LH:(h + 1) * LH], ps[h][0:1, :], AF.Copy,
            scale=1.0 / D), waits=[(tP, t_stats[-1])], inc=aS)
        a_mean = pr.op("SC", lambda e, h=h: e.activation(
            r_ex2[0:1, h * LH:(h + 1) * LH], ps[2 + h][0:1, :], AF.Copy,
            scale=1.0 / D), inc=aS)
    v_ms = pr.op("VE", lambda e: e.tensor_tensor(
        r_scr[0:1, :], r_mean[0:1, :], r_mean[0:1, :], ALU.mult),
        waits=[(aS, a_mean)], inc=aV)
    v_var = pr.op("VE", lambda e: e.tensor_tensor(
        r_scr[0:1, :], r_ex2[0:1, :], r_scr[0:1, :], ALU.subtract),
        waits=[(aV, v_ms)], inc=aV)
    a_lnv = pr.op("SC", lambda e: e.activation(
        r_ex2[0:1, :], r_scr[0:1, :], AF.Ln, bias=epsb[0:1, :]),
        waits=[(aV, v_var), (aV, ve_ones)], inc=aS)
    a_rstd = pr.op("SC", lambda e: e.activation(
        r_scr[0:1, :], r_ex2[0:1, :], AF.Exp, scale=-0.5),
        waits=[(aS, a_lnv)], inc=aS)
    v_mr = pr.op("VE", lambda e: e.tensor_tensor(
        r_mean[0:1, :], r_mean[0:1, :], r_scr[0:1, :], ALU.mult),
        waits=[(aS, a_rstd)], inc=aV)
    t_bc = None
    for h in range(2):
        pr.op("TE", lambda e, h=h: e.matmul(
            ps[4 + h][:, :], onesf[:, 1:129], r_scr[0:1, h * LH:(h + 1) * LH],
            start=True, stop=True), waits=[(aV, v_mr)])
        t_bc = pr.op("TE", lambda e, h=h: e.matmul(
            ps[6 + h][:, :], onesf[:, 1:129], r_mean[0:1, h * LH:(h + 1) * LH],
            start=True, stop=True), inc=tP)
    a_rsb = None
    for h in range(2):
        a_rsb = pr.op("SC", lambda e, h=h: e.activation(
            rstd_sb[:, h * LH:(h + 1) * LH], ps[4 + h][:, :], AF.Copy),
            waits=[(tP, t_bc)], inc=aS)
    a_mrb = None
    for h in range(2):
        a_mrb = pr.op("SC", lambda e, h=h: e.activation(
            mrsb[:, h * LH:(h + 1) * LH], ps[6 + h][:, :], AF.Copy),
            waits=[(tP, t_bc)], inc=aS)

    arena_h = [arena[:, 0:8192], arena[:, 8192:16384]]
    g_wkb = [pr.op("GP", lambda e: e.dma_start(
        out=arena_h[0], in_=d_wk[0]), inc=sWa, by=16)]
    g_wvb = [pr.op("GP", lambda e: e.dma_start(
        out=arena_h[1], in_=d_wv[0]), inc=sWb, by=16)]

    p_xn = []
    eng_cnt = {"VE": 0, "GP": 0}
    last_buf = {}
    for c in range(NCH):
        on_ve = True
        eng, sem = ("VE", aV) if on_ve else ("GP", aP)
        k = eng_cnt[eng]
        eng_cnt[eng] += 1
        lsl = ([vA, vB] if on_ve else [pA, pB])[k % 2]
        cs = slice(c * NT, (c + 1) * NT)
        mw = [(aS, a_mrb)]
        if (eng, k % 2) in last_buf:
            mw.append((sem, last_buf[(eng, k % 2)]))
        vmul = pr.op(eng, lambda e, cs=cs, lsl=lsl: e.tensor_tensor(
            lsl[:, :], xn[:, cs], rstd_sb[:, :], ALU.mult),
            waits=mw, inc=sem)
        lastp = pr.op(eng, lambda e, cs=cs, lsl=lsl: e.tensor_tensor(
            xn[:, cs], lsl[:, :], mrsb[:, :], ALU.subtract),
            waits=[(sem, vmul)], inc=sem)
        last_buf[(eng, k % 2)] = lastp
        p_xn.append((sem, lastp))
    rel45 = [(aS, a_rsb)]

    kts = [ktm[:, 0:2048], ktm[:, 2048:4096]]
    vts = [vtm[:, 0:2048], vtm[:, 2048:4096]]
    prev_vphi = None
    t_kmm = []
    t_vmm = []
    t_kvh = []
    t_ksum = []
    s_pack = []
    evA = {}
    vphi_blk = []
    acp_blk = []
    for b in range(4):
        if b >= 1:
            g_wkb.append(pr.op("GP", lambda e, b=b: e.dma_start(
                out=arena_h[0], in_=d_wk[b]),
                waits=[(tP, t_kmm[b - 1])], inc=sWa, by=16))
            g_wvb.append(pr.op("GP", lambda e, b=b: e.dma_start(
                out=arena_h[1], in_=d_wv[b]),
                waits=[(tP, t_vmm[b - 1])], inc=sWb, by=16))
        for is_k in (True, False):
            gld = g_wkb[b] if is_k else g_wvb[b]
            sld = sWa if is_k else sWb
            ah = arena_h[0] if is_k else arena_h[1]
            dstT = (kts if is_k else vts)[b % 2]
            for lt in range(8):
                bank = lt % 4
                bw = [(sld, gld), (aV, ve_ones), (sP, s_pre)]
                bw += evA.get(bank, [])
                if b >= 2:
                    bw.append((tP, t_ksum[b - 2]))
                for c in range(NCH):
                    cw = (bw if c == 0 else []) + [p_xn[c]]
                    pr.op("TE", lambda e, c=c, lt=lt, ah=ah, bank=bank: e.matmul(
                        ps[bank][:, 0:256],
                        xn[:, c * NT + lt * P: c * NT + (lt + 1) * P],
                        ah[:, c * 256:(c + 1) * 256],
                        start=(c == 0), stop=False),
                        waits=cw)
                brow = (bkb if is_k else bvb)[0:1, b * 256:(b + 1) * 256]
                t_this = pr.op("TE", lambda e, brow=brow, bank=bank: e.matmul(
                    ps[bank][:, 0:256], ones_r[:, :], brow,
                    start=False, stop=True), inc=tP)
                dst = dstT[:, lt * 256:(lt + 1) * 256]
                if is_k:
                    ew = [(tP, t_this)]
                    if prev_vphi is not None:
                        ew.append((aV, prev_vphi))
                    pr.op("SC", lambda e, bank=bank: e.activation(
                        lnAb[:, 0:256], ps[bank][:, 0:256], AF.Exp),
                        waits=ew, inc=aS)
                    a_r = pr.op("SC", lambda e, bank=bank: e.activation(
                        lnAb[:, 256:512], ps[bank][:, 0:256], AF.Relu),
                        inc=aS)
                    prev_vphi = pr.op("VE", lambda e, dst=dst:
                                      e.scalar_tensor_tensor(
                                          dst, lnAb[:, 0:256], 1.0,
                                          lnAb[:, 256:512], ALU.min, ALU.add),
                                      waits=[(aS, a_r)], inc=aV)
                    evA[bank] = [(aS, a_r)]
                else:
                    a_cp = pr.op("SC", lambda e, dst=dst, bank=bank: e.activation(
                        dst, ps[bank][:, 0:256], AF.Copy),
                        waits=[(tP, t_this)], inc=aS)
                    evA[bank] = [(aS, a_cp)]
            if is_k:
                t_kmm.append(t_this)
                vphi_blk.append(prev_vphi)
            else:
                t_vmm.append(t_this)
                acp_blk.append(aS.n)
        t_last = None
        for sub in range(2):
            hh = 2 * b + sub
            bank = 4 + sub
            bw = [(aV, vphi_blk[b]), (aS, acp_blk[b])]
            if b == 0:
                bw += rel45
            if b >= 1:
                bw.append((aV, t_kvh[b - 1][1][sub]))
            for lt in range(8):
                t_last = pr.op(
                    "TE", lambda e, lt=lt, sub=sub, b=b, bank=bank: e.matmul(
                        ps[bank][:, 0:P],
                        kts[b % 2][:, lt * 256 + sub * P: lt * 256 + (sub + 1) * P],
                        vts[b % 2][:, lt * 256 + sub * P: lt * 256 + (sub + 1) * P],
                        start=(lt == 0), stop=(lt == 7)),
                    waits=bw if lt == 0 else [],
                    inc=(tP if (sub == 1 and lt == 7) else None))
        v_evs = []
        for sub in range(2):
            hh = 2 * b + sub
            sK = sKa if sub == 0 else sKb
            vw = [(tP, t_last)]
            if b >= 1:
                vw.append((sK, s_pack[2 * (b - 1) + sub]))
            v_ev = pr.op("VE", lambda e, sub=sub, bank=4 + sub: e.tensor_copy(
                kvst[:, sub * P:(sub + 1) * P], ps[bank][:, 0:P]),
                waits=vw, inc=aV)
            v_evs.append(v_ev)
            s_pack.append(pr.op("SY", lambda e, hh=hh, sub=sub: e.dma_start(
                out=cc_in.ap()[hh * P:(hh + 1) * P, :],
                in_=kvst[:, sub * P:(sub + 1) * P]),
                waits=[(aV, v_ev)], inc=sK, by=16))
        t_kvh.append((t_last, v_evs))
        ksbank = 6 + b // 2
        kcol = (b % 2) * 256
        t_kb = None
        for lt in range(8):
            kw = [(aV, vphi_blk[b])]
            if b == 0:
                kw.append((aS, a_mrb))
            t_kb = pr.op("TE", lambda e, lt=lt, b=b, ksbank=ksbank, kcol=kcol:
                         e.matmul(
                             ps[ksbank][0:1, kcol:kcol + 256], ones_c[:, :],
                             kts[b % 2][:, lt * 256:(lt + 1) * 256],
                             start=(lt == 0), stop=(lt == 7)),
                         waits=kw if lt == 0 else [],
                         inc=(tP if lt == 7 else None))
        t_ksum.append(t_kb)
    t_ks = t_ksum[-1]
    a_ks = None
    for h in range(2):
        a_ks = pr.op("SC", lambda e, h=h: e.activation(
            r_scr[0:1, h * LH:(h + 1) * LH], ps[6 + h][0:1, :], AF.Copy),
            waits=[(tP, t_ks)], inc=aS)
    s_ks = None
    for hh in range(NHEAD):
        s_ks = pr.op("SY", lambda e, hh=hh: e.dma_start(
            out=cc_in.ap()[DK + hh:DK + hh + 1, :],
            in_=r_scr[0:1, hh * P:(hh + 1) * P]),
            waits=[(aS, a_ks)], inc=sKs, by=16)

    cc_waits = [(sKs, s_ks), (sKa, sKa.n), (sKb, sKb.n)]

    colq5 = [arena[:, 4096 * i:4096 * (i + 1)] for i in range(4)] + [qex[:, :]]
    qsems = [sWa, sQb, sBa, sBb, sQc]
    t_qcol = []
    ev_q = {}
    g_qcol = []
    v_phi_q = []
    for j in range(NCH):
        sW = qsems[j % 5]
        if j < 2:
            waits = [(tP, t_kmm[3])]
        elif j < 4:
            waits = [(tP, t_vmm[3])]
        elif j == 4:
            waits = [(tP, t_ks), (tP, t_kvh[3][0])]
        else:
            waits = [(tP, t_qcol[j - 5])]
        g_qcol.append(pr.op("GP", lambda e, j=j: e.dma_start(
            out=colq5[j % 5], in_=d_wq[j]), waits=waits, inc=sW, by=16))
        if j == 5:
            g_eye = pr.op("GP", lambda e: e.dma_start(
                out=eyeTb[:, :], in_=d_eyeT),
                waits=[(tP, t_kvh[3][0])], inc=sEy, by=16)
        bank0, bank1 = (j % 2) * 2, (j % 2) * 2 + 1
        bw = [(sW, g_qcol[j])]
        for k in (bank0, bank1):
            bw += ev_q.get(k, [])
        tq = None
        for c in range(NCH):
            pr.op("TE", lambda e, j=j, c=c, bank0=bank0: e.matmul(
                ps[bank0][:, :], colq5[j % 5][:, c * P:(c + 1) * P],
                xn[:, c * NT: c * NT + LH],
                start=(c == 0), stop=(c == NCH - 1)),
                waits=bw if c == 0 else [])
            tq = pr.op("TE", lambda e, j=j, c=c, bank1=bank1: e.matmul(
                ps[bank1][:, :], colq5[j % 5][:, c * P:(c + 1) * P],
                xn[:, c * NT + LH: c * NT + NT],
                start=(c == 0), stop=(c == NCH - 1)),
                inc=(tP if c == NCH - 1 else None))
        t_qcol.append(tq)
        for h in range(2):
            bank = bank0 if h == 0 else bank1
            u = h
            s_e = qsc[:, (2 * u) * LH:(2 * u + 1) * LH]
            s_r = qsc[:, (2 * u + 1) * LH:(2 * u + 2) * LH]
            ew = [(tP, t_qcol[j]), (sP, s_pre)]
            if j >= 1:
                ew.append((aV, v_phi_q[2 * (j - 1) + h]))
            a_e = pr.op("SC", lambda e, bank=bank, j=j, s_e=s_e: e.activation(
                s_e, ps[bank][:, :], AF.Exp, bias=bqcol[:, j:j + 1]),
                waits=ew, inc=aS)
            a_r = pr.op("SC", lambda e, bank=bank, j=j, s_r=s_r: e.activation(
                s_r, ps[bank][:, :], AF.Relu, bias=bqcol[:, j:j + 1]), inc=aS)
            dst = qb[:, j * NT + h * LH: j * NT + (h + 1) * LH]
            vp = pr.op("VE", lambda e, dst=dst, s_e=s_e, s_r=s_r:
                       e.scalar_tensor_tensor(
                           dst, s_e, 1.0, s_r, ALU.min, ALU.add),
                       waits=[(aS, a_r)], inc=aV)
            v_phi_q.append(vp)
            ev_q[bank] = [(aS, a_r)]

        if j == 4:
            c_cc = pr.op("GP", lambda e: e.collective_compute(
                "AllReduce", ALU.add,
                replica_groups=[list(range(N_CORES))],
                ins=[cc_in.ap()], outs=[cc_out.ap()]),
                waits=cc_waits, inc=cC, by=1)

        if j == 16:
            g_kvtb = pr.op("GP", lambda e: e.dma_start(
                out=kvtb[:, :],
                in_=cc_out.ap()[0:DK, :].rearrange("(h d) v -> d h v", d=P)),
                waits=[(cC, c_cc)], inc=sX, by=16)
            s_ksb = pr.op("SY", lambda e: e.dma_start(
                out=ksb[:, :],
                in_=cc_out.ap()[DK:DK + NE, :].rearrange("h v -> v h"),
                allow_slow_non_contiguous=True),
                waits=[(cC, c_cc)], inc=sKR, by=16)
            v_k1 = pr.op("VE", lambda e: e.tensor_single_scalar(
                ksb_r[:, :], ksb[:, :], 1e-10, ALU.add),
                waits=[(sKR, s_ksb)], inc=aV)
            pr.op("VE", lambda e: e.reciprocal(ksb_r[:, :], ksb_r[:, :]),
                  waits=[(aV, v_k1)], inc=aV)
            v_ksr = aV.n

        if j == 20:
            t_rt = []
            for h in range(2):
                tr = None
                for c in range(NCH):
                    tr = pr.op("TE", lambda e, h=h, c=c: e.matmul(
                        ps[6][0:8, :] if h == 0 else ps[7][0:8, :],
                        gwT[:, c * NE:(c + 1) * NE],
                        xn[:, c * NT + h * LH: c * NT + h * LH + LH],
                        start=(c == 0), stop=(c == NCH - 1)),
                        waits=([(sP, s_pre), (aS, a_ks), (cC, c_cc)]
                               if c == 0 else []),
                        inc=(tP if c == NCH - 1 else None))
                t_rt.append(tr)
        if j == 21:
            a_exp = []
            for h in range(2):
                a_exp.append(pr.op("SC", lambda e, h=h: e.activation(
                    expL[0:8, h * LH:(h + 1) * LH], ps[6 + h][0:8, :], AF.Exp),
                    waits=[(tP, t_rt[h])], inc=aS))
        if j == 22:
            t_z = []
            for h in range(2):
                t_z.append(pr.op("TE", lambda e, h=h: e.matmul(
                    ps[6 + h][0:1, :], ones_c[0:8, :],
                    expL[0:8, h * LH:(h + 1) * LH],
                    start=True, stop=True), waits=[(aS, a_exp[h])], inc=tP))
        if j == 23:
            a_z = None
            for h in range(2):
                a_z = pr.op("SC", lambda e, h=h: e.activation(
                    r_ex2[0:1, h * LH:(h + 1) * LH], ps[6 + h][0:1, :],
                    AF.Copy), waits=[(tP, t_z[h])], inc=aS)
            a_zl = pr.op("SC", lambda e: e.activation(
                r_mean[0:1, :], r_ex2[0:1, :], AF.Ln),
                waits=[(aS, a_z)], inc=aS)
            a_zi = pr.op("SC", lambda e: e.activation(
                r_mean[0:1, :], r_mean[0:1, :], AF.Exp, scale=-1.0),
                waits=[(aS, a_zl)], inc=aS)
        if j == 24:
            t_zb = []
            for h in range(2):
                t_zb.append(pr.op("TE", lambda e, h=h: e.matmul(
                    ps[6 + h][0:8, :], onesf[:, 1:9],
                    r_mean[0:1, h * LH:(h + 1) * LH],
                    start=True, stop=True), waits=[(aS, a_zi)], inc=tP))
            v_rout8 = None
            for h in range(2):
                v_rout8 = pr.op("VE", lambda e, h=h: e.tensor_tensor(
                    routb[0:8, h * LH:(h + 1) * LH],
                    expL[0:8, h * LH:(h + 1) * LH],
                    ps[6 + h][0:8, :], ALU.mult),
                    waits=[(tP, t_zb[h])], inc=aV)
        if j == 25:
            t_rf = []
            for h in range(2):
                t_rf.append(pr.op("TE", lambda e, h=h: e.matmul(
                    ps[6 + h][:, :], eyeTb[:, :],
                    routb[0:8, h * LH:(h + 1) * LH],
                    start=True, stop=True),
                    waits=[(aV, v_rout8), (sEy, g_eye)], inc=tP))
            v_rout = None
            for h in range(2):
                v_rout = pr.op("VE", lambda e, h=h: e.tensor_copy(
                    routf[:, h * LH:(h + 1) * LH], ps[6 + h][:, :]),
                    waits=[(tP, t_rf[h])], inc=aV)
    v_q_done = aV.n

    bank_ev = {b: list(ev_q.get(b, [])) for b in range(6)}
    bank_ev[6] = [(aV, v_rout)]
    bank_ev[7] = [(aV, v_rout)]

    t_attn_done = None
    for qh in range(NQH):
        kvh = qh // 4
        for h in range(2):
            u = 2 * qh + h
            bank = u % 6
            bw = [(sX, g_kvtb), (aV, v_phi_q[u])] + bank_ev.get(bank, [])
            t_a = pr.op("TE", lambda e, qh=qh, h=h, kvh=kvh, bank=bank: e.matmul(
                ps[bank][:, :], kvtb[:, kvh * P:(kvh + 1) * P],
                qb[:, qh * NT + h * LH: qh * NT + (h + 1) * LH],
                start=True, stop=True), waits=bw, inc=tP)
            t_attn_done = t_a
            dst = qb[:, qh * NT + h * LH: qh * NT + (h + 1) * LH]
            if u % 2 == 0:
                v_a = pr.op("VE", lambda e, dst=dst, bank=bank, kvh=kvh:
                            e.tensor_single_scalar(
                                dst, ps[bank][:, :], ksb_r[:, kvh:kvh + 1],
                                ALU.mult),
                            waits=[(tP, t_a), (aV, v_ksr)], inc=aV)
                bank_ev[bank] = [(aV, v_a)]
            else:
                s_a = pr.op("SC", lambda e, dst=dst, bank=bank, kvh=kvh:
                            e.activation(
                                dst, ps[bank][:, :], AF.Copy,
                                scale=ksb_r[:, kvh:kvh + 1]),
                            waits=[(tP, t_a), (aV, v_ksr)], inc=aS)
                bank_ev[bank] = [(aS, s_a)]
    attn_v_done = aV.n
    attn_s_done = aS.n

    g_ict = []
    for j in range(8):
        g_ict.append(pr.op("GP", lambda e, j=j: e.dma_start(
            out=icst[:, (j % 8) * P:(j % 8 + 1) * P],
            in_=d_ics[:, j * P:(j + 1) * P]), inc=sIq[j % 8], by=16))

    colw = [arena[:, 0:4096], arena[:, 4096:8192]]
    colb = [arena[:, 8192:12288], arena[:, 12288:16384]]
    g_wo = []
    g_wb = []
    g_oct = []
    g_res = []
    a_sgn = []
    for j in range(2):
        sW = sWa if j % 2 == 0 else sWb
        sB = sBa if j % 2 == 0 else sBb
        sR = sRa if j % 2 == 0 else sRb
        sO = sOa if j % 2 == 0 else sOb
        g_oct.append(pr.op("GP", lambda e, j=j: e.dma_start(
            out=ocst[:, (j % 2) * P:(j % 2 + 1) * P],
            in_=d_ocs[:, j * P:(j + 1) * P]),
            waits=[(aV, v_rout)], inc=sO, by=16))
        g_wo.append(pr.op("GP", lambda e, j=j: e.dma_start(
            out=colw[j % 2], in_=d_wo[j]),
            waits=[(tP, t_qcol[-1])], inc=sW, by=16))
        g_wb.append(pr.op("GP", lambda e, j=j: e.dma_start(
            out=colb[j % 2], in_=d_wb[j]),
            waits=[(tP, t_qcol[-1])], inc=sB, by=16))
        a_sgn.append(pr.op("VE", lambda e, j=j: e.tensor_scalar(
            colb[j % 2][:, :], colb[j % 2][:, :], 0.0, 0.5,
            ALU.is_ge, ALU.subtract),
            waits=[(sB, g_wb[j])], inc=aV))
    for j in range(2):
        g_res.append(pr.op("GP", lambda e, j=j: e.dma_start(
            out=residb[:, (j % 2) * NT:(j % 2 + 1) * NT],
            in_=d_xT[:, j * NT:(j + 1) * NT]),
            waits=[(tP, t_attn_done)], inc=(sRa if j % 2 == 0 else sRb), by=16))

    pre_thunks = []
    for h in range(2):
        u = h % 2
        bS = 2 if u == 0 else 5
        bO = 4 if u == 0 else 7
        bw0 = [(sWa, g_wo[0]), (aV, attn_v_done), (aS, attn_s_done)]
        bw0 += bank_ev[bS]
        for c in range(NCH):
            pre_thunks.append((
                "S", h,
                lambda e, h=h, c=c, bS=bS: e.matmul(
                    ps[bS][:, :], colw[0][:, c * P:(c + 1) * P],
                    qb[:, c * NT + h * LH: c * NT + (h + 1) * LH],
                    start=(c == 0), stop=(c == NCH - 1)),
                bw0 if c == 0 else [], c == NCH - 1))
        bwo = [(sOa, g_oct[0]), (aV, v_rout)] + bank_ev[bO]
        pre_thunks.append((
            "O", h,
            lambda e, h=h, bO=bO: e.matmul(
                ps[bO][:, :], ocst[:, 0:P],
                routf[:, h * LH:(h + 1) * LH],
                start=True, stop=True), bwo, True))

    pre_tS = {}
    pre_tO = {}
    pre_iter = iter(pre_thunks)

    def drain_pre(n):
        for _ in range(n):
            item = next(pre_iter, None)
            if item is None:
                return
            kind, h, fn, w, is_end = item
            v = pr.op("TE", fn, waits=w, inc=(tP if is_end else None))
            if is_end:
                (pre_tS if kind == "S" else pre_tO)[h] = v

    xs_banks = [6, 0, 1, 3]
    t_ic = []
    for j in range(NCH):
        sI = sIq[j % 8]
        if j >= 8:
            g_ict.append(pr.op("GP", lambda e, j=j: e.dma_start(
                out=icst[:, (j % 8) * P:(j % 8 + 1) * P],
                in_=d_ics[:, j * P:(j + 1) * P]),
                waits=[(tP, t_ic[j - 8])], inc=sI, by=16))
        for h in range(2):
            bank = xs_banks[(2 * j + h) % 4]
            bw = [(sI, g_ict[j]), (aV, v_rout), (tP, t_qcol[-1])]
            bw += bank_ev.get(bank, [])
            t_i = pr.op("TE", lambda e, j=j, h=h, bank=bank: e.matmul(
                ps[bank][:, :], icst[:, (j % 8) * P:(j % 8 + 1) * P],
                routf[:, h * LH:(h + 1) * LH],
                start=True, stop=True), waits=bw, inc=tP)
            sl = slice(j * NT + h * LH, j * NT + (h + 1) * LH)
            vx = pr.op("VE", lambda e, sl=sl, bank=bank:
                       e.scalar_tensor_tensor(
                           xn[:, sl], ps[bank][:, :], 1.0 / 16, xn[:, sl],
                           ALU.mult, ALU.mult),
                       waits=[(tP, t_i)], inc=aV)
            bank_ev[bank] = [(aV, vx)]
            if j >= 6:
                drain_pre(2)
        t_ic.append(t_i)
    drain_pre(len(pre_thunks))
    xs_v_done = aV.n

    t_scol = []
    t_mcol = []
    t_oc = []
    s_store = []
    ev_fin = {
        ("M", 0): bank_ev[3][0],
        ("M", 1): bank_ev[6][0],
    }
    v_out = {}
    for j in range(NCH):
        sW = sWa if j % 2 == 0 else sWb
        sB = sBa if j % 2 == 0 else sBb
        sR = sRa if j % 2 == 0 else sRb
        sO = sOa if j % 2 == 0 else sOb
        sS = sSa if j % 2 == 0 else sSb
        if j >= 2:
            g_wo.append(pr.op("GP", lambda e, j=j: e.dma_start(
                out=colw[j % 2], in_=d_wo[j]),
                waits=[(tP, t_scol[j - 2])], inc=sW, by=16))
            g_wb.append(pr.op("GP", lambda e, j=j: e.dma_start(
                out=colb[j % 2], in_=d_wb[j]),
                waits=[(tP, t_mcol[j - 2])], inc=sB, by=16))
            a_sgn.append(pr.op("VE", lambda e, j=j: e.tensor_scalar(
                colb[j % 2][:, :], colb[j % 2][:, :], 0.0, 0.5,
                ALU.is_ge, ALU.subtract),
                waits=[(sB, g_wb[j])], inc=aV))
            g_res.append(pr.op("GP", lambda e, j=j: e.dma_start(
                out=residb[:, (j % 2) * NT:(j % 2 + 1) * NT],
                in_=d_xT[:, j * NT:(j + 1) * NT]),
                waits=[(aV, v_out[(j - 2, 1)])], inc=sR, by=16))
            g_oct.append(pr.op("GP", lambda e, j=j: e.dma_start(
                out=ocst[:, (j % 2) * P:(j % 2 + 1) * P],
                in_=d_ocs[:, j * P:(j + 1) * P]),
                waits=[(tP, t_oc[j - 2])], inc=sO, by=16))
        for h in range(2):
            u = (2 * j + h) % 2
            bS, bM, bO = (2, 3, 4) if u == 0 else (5, 6, 7)
            s_os = scr[:, (3 * u) * LH:(3 * u + 1) * LH]
            s_P = scr[:, (3 * u + 1) * LH:(3 * u + 2) * LH]
            s_g = scr[:, (3 * u + 2) * LH:(3 * u + 3) * LH]
            if j == 0:
                tS = pre_tS[h]
                tO = pre_tO[h]
            else:
                bw = [(sW, g_wo[j]), (aV, attn_v_done), (aS, attn_s_done)]
                if ("S", u) in ev_fin:
                    bw.append(ev_fin[("S", u)])
                tS = None
                for c in range(NCH):
                    tS = pr.op("TE", lambda e, j=j, h=h, c=c, bS=bS: e.matmul(
                        ps[bS][:, :], colw[j % 2][:, c * P:(c + 1) * P],
                        qb[:, c * NT + h * LH: c * NT + (h + 1) * LH],
                        start=(c == 0), stop=(c == NCH - 1)),
                        waits=bw if c == 0 else [],
                        inc=(tP if c == NCH - 1 else None))
                bw = [(sO, g_oct[j]), (aV, v_rout)]
                if ("O", u) in ev_fin:
                    bw.append(ev_fin[("O", u)])
                tO = pr.op("TE", lambda e, j=j, h=h, bO=bO: e.matmul(
                    ps[bO][:, :], ocst[:, (j % 2) * P:(j % 2 + 1) * P],
                    routf[:, h * LH:(h + 1) * LH],
                    start=True, stop=True), waits=bw, inc=tP)
            if h == 1:
                t_scol.append(tS)
                t_oc.append(tO)
            bw = [(aV, a_sgn[j]), (aV, xs_v_done)]
            if ("M", u) in ev_fin:
                bw.append(ev_fin[("M", u)])
            tM = None
            for c in range(NCH):
                tM = pr.op("TE", lambda e, j=j, h=h, c=c, bM=bM: e.matmul(
                    ps[bM][:, :], colb[j % 2][:, c * P:(c + 1) * P],
                    xn[:, c * NT + h * LH: c * NT + (h + 1) * LH],
                    start=(c == 0), stop=(c == NCH - 1)),
                    waits=bw if c == 0 else [],
                    inc=(tP if c == NCH - 1 else None))
            if h == 1:
                t_mcol.append(tM)
            a_os = pr.op("SC", lambda e, bO=bO, s_os=s_os: e.activation(
                s_os, ps[bO][:, :], AF.Copy, scale=1.0 / 16),
                waits=[(tP, tO)], inc=aS)
            v_P = pr.op("VE", lambda e, bM=bM, s_P=s_P, s_os=s_os: e.tensor_tensor(
                s_P, ps[bM][:, :], s_os, ALU.mult),
                waits=[(tP, tM), (aS, a_os)], inc=aV)
            a_g = pr.op("SC", lambda e, j=j, s_P=s_P, s_g=s_g: e.activation(
                s_g, s_P, AF.Silu, bias=bfcol[:, j:j + 1], scale=2.0),
                waits=[(aV, v_P), (sP, s_pre)], inc=aS)
            v_sg = pr.op("VE", lambda e, j=j, bS=bS, s_os=s_os, s_g=s_g:
                         e.scalar_tensor_tensor(
                             s_os, ps[bS][:, :], bocol[:, j:j + 1], s_g,
                             ALU.add, ALU.mult),
                         waits=[(tP, tS), (aS, a_g), (sP, s_pre)], inc=aV)
            ow = [(aV, v_sg), (sR, g_res[j])]
            if j >= 2:
                ow.append((sS, s_store[j - 2]))
            vo = pr.op("VE", lambda e, j=j, h=h, s_os=s_os: e.tensor_tensor(
                outst[:, (j % 2) * NT + h * LH:(j % 2) * NT + (h + 1) * LH],
                s_os, residb[:, (j % 2) * NT + h * LH:(j % 2) * NT + (h + 1) * LH],
                ALU.add), waits=ow, inc=aV)
            v_out[(j, h)] = vo
            ev_fin[("S", u)] = (aV, v_sg)
            ev_fin[("M", u)] = (aV, v_P)
            ev_fin[("O", u)] = (aS, a_os)
        s_store.append(pr.op("SY", lambda e, j=j: e.dma_start(
            out=d_out[j], in_=outst[:, (j % 2) * NT:(j % 2 + 1) * NT]),
            waits=[(aV, v_out[(j, 1)])], inc=sS, by=16))

    pr.op("SY", lambda e: e.nop(), waits=[(sSa, sSa.n), (sSb, sSb.n)])

    from contextlib import ExitStack
    with ExitStack() as ctx:
        for s in sems:
            s.h = ctx.enter_context(nc.semaphore(s.name))
        block = ctx.enter_context(nc.Block())

        def runner(key):
            def cb(eng):
                hw = {}
                for (k, fn, waits, inc, by) in pr.ops:
                    if k != key:
                        continue
                    for (s, v) in waits:
                        if hw.get(s.name, 0) < v:
                            eng.wait_ge(s.h, v)
                            hw[s.name] = v
                    inst = fn(eng)
                    if inc is not None and inst is not None:
                        if by is None:
                            inst.then_inc(inc.h)
                        else:
                            inst.then_inc(inc.h, by)
            return cb

        block.sync(runner("SY"))
        block.scalar(runner("SC"))
        block.vector(runner("VE"))
        block.tensor(runner("TE"))
        block.gpsimd(runner("GP"))

    return nc


_NC_CACHE = []


def kernel(**inputs):
    f = np.float32

    def g(name):
        return np.asarray(inputs[name], dtype=f)

    x = g("L0_attn_hidden_states")[0]
    Wq, bq = g("L0_attn_Wq"), g("L0_attn_bq")
    Wk, bk = g("L0_attn_Wk"), g("L0_attn_bk")
    Wv, bv = g("L0_attn_Wv"), g("L0_attn_bv")
    Wo, bo = g("L0_attn_Wo"), g("L0_attn_bo")
    Wf, bf = g("L0_ffn_weight"), g("L0_ffn_bias")
    gw = g("L0_ffn_gate_w")
    ics = g("L0_ffn_in_channel_scale")
    ocs = g("L0_ffn_out_channel_scale")

    def colt(W):
        O = W.shape[0]
        return np.ascontiguousarray(
            W.T.reshape(NCH, P, O // P, P).transpose(2, 1, 0, 3)
        ).reshape(O // P, P, NCH * P)

    wq_t = colt(Wq)
    wo_t = colt(Wo)
    wb_t = colt(Wf)

    def blkt(W):
        return np.ascontiguousarray(
            W.T.reshape(NCH, P, 4, 256).transpose(2, 1, 0, 3)
        ).reshape(4, P, NCH * 256)

    wk_t = blkt(Wk)
    wv_t = blkt(Wv)
    gw_t = np.ascontiguousarray(
        gw.T.reshape(NCH, P, NE).transpose(1, 0, 2)).reshape(P, NCH * NE)
    bqc = np.ascontiguousarray(bq.reshape(NCH, P).T)
    boc = np.ascontiguousarray(bo.reshape(NCH, P).T)
    bfc = np.ascontiguousarray(bf.reshape(NCH, P).T)
    bk2 = np.ascontiguousarray(bk.reshape(1, DK))
    bv2 = np.ascontiguousarray(bv.reshape(1, DK))
    ics = np.ascontiguousarray(np.tile(ics, (16, 1)))
    ocs = np.ascontiguousarray(np.tile(ocs, (16, 1)))
    eyeT = np.zeros((NE, P), dtype=f)
    for e_ in range(NE):
        eyeT[e_, e_::NE] = 1.0

    xT = np.ascontiguousarray(x.T)
    in_maps = []
    for c in range(N_CORES):
        xc = np.ascontiguousarray(
            xT[:, c * NT:(c + 1) * NT].reshape(NCH, P, NT)
            .transpose(1, 0, 2)).reshape(P, NCH * NT)
        in_maps.append({
            "xT": xc, "wq": wq_t, "wo": wo_t, "wb": wb_t,
            "wk": wk_t, "wv": wv_t, "gw": gw_t, "ics": ics, "ocs": ocs,
            "bqc": bqc, "boc": boc, "bfc": bfc, "bk": bk2, "bv": bv2,
            "eyeT": eyeT,
        })

    if not _NC_CACHE:
        _NC_CACHE.append(build_program())
    nc = _NC_CACHE[0]
    res = run_bass_kernel_spmd(nc, in_maps, core_ids=list(range(N_CORES)))

    out = np.empty((8192, D), dtype=f)
    for c in range(N_CORES):
        oc = np.asarray(res.results[c]["outT"], dtype=f)
        out[c * NT:(c + 1) * NT, :] = oc.reshape(D, NT).T
    return out.reshape(1, 8192, D)


# version 24
# speedup vs baseline: 1.0836x; 1.0051x over previous
import numpy as np
import concourse.bass as bass
import concourse.mybir as mybir
from concourse.bass_utils import run_bass_kernel_spmd

F32 = mybir.dt.float32
BF16 = mybir.dt.bfloat16
AF = mybir.ActivationFunctionType
ALU = mybir.AluOpType

P = 128
NT = 1024
LH = 512
D = 4096
NCH = 32
DK = 1024
NHEAD = 8
NQH = 32
NE = 8
N_CORES = 8

SBUF_BYTES = 229376


class S:
    __slots__ = ("h", "n", "name")

    def __init__(self, h, name):
        self.h = h
        self.n = 0
        self.name = name


class Prog:
    def __init__(self):
        self.ops = []

    def op(self, eng, fn, waits=(), inc=None, by=1):
        for (s, v) in waits:
            assert v <= s.n, f"wait on future value {s.name} {v} > {s.n}"
        self.ops.append((eng, fn, [(s, v) for (s, v) in waits if v > 0], inc, by))
        if inc is not None:
            inc.n += (by if by is not None else 1)
            return inc.n
        return None


def build_program():
    nc = bass.Bass("TRN2", target_bir_lowering=False)

    d_xT = nc.dram_tensor("xT", [P, NCH * NT], F32, kind="ExternalInput").ap()
    d_wq = nc.dram_tensor("wq", [NCH, P, D], F32, kind="ExternalInput").ap()
    d_wo = nc.dram_tensor("wo", [NCH, P, D], F32, kind="ExternalInput").ap()
    d_wb = nc.dram_tensor("wb", [NCH, P, D], F32, kind="ExternalInput").ap()
    d_wk = nc.dram_tensor("wk", [4, P, 32 * 256], F32, kind="ExternalInput").ap()
    d_wv = nc.dram_tensor("wv", [4, P, 32 * 256], F32, kind="ExternalInput").ap()
    d_gw = nc.dram_tensor("gw", [P, NCH * NE], F32, kind="ExternalInput").ap()
    d_ics = nc.dram_tensor("ics", [P, D], F32, kind="ExternalInput").ap()
    d_ocs = nc.dram_tensor("ocs", [P, D], F32, kind="ExternalInput").ap()
    d_eyeT = nc.dram_tensor("eyeT", [NE, P], F32, kind="ExternalInput").ap()
    d_bqc = nc.dram_tensor("bqc", [P, NCH], F32, kind="ExternalInput").ap()
    d_boc = nc.dram_tensor("boc", [P, NCH], F32, kind="ExternalInput").ap()
    d_bfc = nc.dram_tensor("bfc", [P, NCH], F32, kind="ExternalInput").ap()
    d_bk = nc.dram_tensor("bk", [1, DK], F32, kind="ExternalInput").ap()
    d_bv = nc.dram_tensor("bv", [1, DK], F32, kind="ExternalInput").ap()
    d_out = nc.dram_tensor("outT", [NCH, P, NT], F32, kind="ExternalOutput").ap()
    cc_in = nc.dram_tensor("cc_in", [DK + NE, P], F32, kind="Internal")
    cc_out = nc.dram_tensor("cc_out", [DK + NE, P], F32, kind="Internal")

    resv = nc.alloc_sbuf_tensor("resv", [P, 212096], mybir.dt.uint8)
    BASE = (nc.lookup_mloc(resv).addr + 127) & ~127

    def at(name, shape, dtype, rel):
        off = BASE + rel
        end = off + shape[1] * (2 if dtype == BF16 else 4)
        assert end <= SBUF_BYTES, (name, off, end)
        return nc.alloc_sbuf_tensor_at(name, shape, dtype, offset=off)

    xn = at("xn", [P, NCH * NT], BF16, 0)
    qb = at("qb", [P, NCH * NT], BF16, 65536)
    arena = at("arena", [P, 16384], BF16, 131072)
    ktm = at("ktm", [P, 2 * 8 * 256], BF16, 163840)
    vtm = at("vtm", [P, 2 * 8 * 256], BF16, 172032)
    residb = at("residb", [P, 2 * NT], BF16, 163840)
    outst = at("outst", [P, 2 * NT], F32, 167936)
    scr = at("scr", [P, 6 * LH], F32, 176128)
    qex = at("qex", [P, 4096], BF16, 163840)
    kvtb = at("kvtb", [P, NHEAD * P], BF16, 182272)
    rstd_sb = at("rstd_sb", [P, NT], BF16, 176128)
    vA = at("vA", [P, NT], BF16, 178176)
    vB = at("vB", [P, NT], BF16, 180224)
    pA = at("pA", [P, NT], BF16, 182272)
    pB = at("pB", [P, NT], BF16, 184320)
    lnAb = at("lnAb", [P, LH], BF16, 186368)
    qsc = at("qsc", [P, 4 * LH], BF16, 176128)
    r_mean = at("r_mean", [1, NT], F32, 188416)
    r_ex2 = at("r_ex2", [1, NT], F32, 192512)
    r_scr = at("r_scr", [1, NT], F32, 196608)
    routb = at("routb", [NE, NT], BF16, 200704)
    bkb = at("bkb", [1, DK], BF16, 202752)
    bvb = at("bvb", [1, DK], BF16, 204800)
    gwT = at("gwT", [P, NCH * NE], BF16, 206848)
    kvst = at("kvst", [P, 2 * P], F32, 207360)
    kvscr = at("kvscr", [P, 256], F32, 208384)
    xsqb = at("xsqb", [P, NT], BF16, 207360)
    mrsb = at("mrsb", [P, NT], BF16, 207360)
    expL = at("expL", [NE, NT], BF16, 207360)
    xsqb2 = at("xsqb2", [P, NT], BF16, 176128)
    ones_r = at("ones_r", [1, P], BF16, 209408)
    onesf = at("onesf", [1, 129], F32, 209664)
    ones_c = at("ones_c", [P, 1], BF16, 210208)
    bqcol = at("bqcol", [P, NCH], F32, 210240)
    bocol = at("bocol", [P, NCH], F32, 210368)
    bfcol = at("bfcol", [P, NCH], F32, 210496)
    ksb = at("ksb", [P, NHEAD], F32, 210624)
    ksb_r = at("ksb_r", [P, NHEAD], F32, 210656)
    epsb = at("epsb", [1, 1], F32, 210688)
    icst = at("icst", [P, 8 * P], BF16, 210720)
    ocst = at("ocst", [P, 2 * P], BF16, 208384)
    eyeTb = at("eyeTb", [NE, P], BF16, 172032)
    routf = at("routf", [P, NT], BF16, 202752)

    ps = [nc.alloc_psum_tensor(f"ps{i}", [P, LH], F32) for i in range(8)]

    pr = Prog()
    sems = []

    def mk(name):
        s = S(None, name)
        sems.append(s)
        return s

    sP = mk("sP")
    sX = mk("sX")
    sX1 = mk("sX1")
    sX2 = mk("sX2")
    sX3 = mk("sX3")
    sX4 = mk("sX4")
    sX5 = mk("sX5")
    sX6 = mk("sX6")
    sX7 = mk("sX7")
    aP = mk("aP")
    sWa = mk("sWa")
    sWb = mk("sWb")
    sBa = mk("sBa")
    sBb = mk("sBb")
    sIq = [mk(f"sI{k}") for k in range(8)]
    sOa = mk("sOa")
    sOb = mk("sOb")
    sKa = mk("sKa")
    sKb = mk("sKb")
    sKs = mk("sKs")
    sKR = mk("sKR")
    sRa = mk("sRa")
    sRb = mk("sRb")
    sSa = mk("sSa")
    sSb = mk("sSb")
    sQb = mk("sQb")
    sQc = mk("sQc")
    sEy = mk("sEy")
    tP = mk("tP")
    aS = mk("aS")
    aV = mk("aV")
    cC = mk("cC")

    ve_ones = pr.op("VE", lambda e: e.memset(ones_r[:, :], 1.0), inc=aV)
    pr.op("VE", lambda e: e.memset(onesf[:, :], 1.0), inc=aV)
    pr.op("VE", lambda e: e.memset(ones_c[:, :], 1.0), inc=aV)
    ve_ones = pr.op("VE", lambda e: e.memset(epsb[:, :], 1e-5), inc=aV)

    sXq2 = [sX, sX1]
    QW = 2 * NT
    g_x16 = []
    for qq in range(16):
        g_x16.append(pr.op("GP", lambda e, qq=qq: e.dma_start(
            out=xn[:, qq * QW:(qq + 1) * QW],
            in_=d_xT[:, qq * QW:(qq + 1) * QW]), inc=sXq2[qq % 2], by=16))

    pr.op("GP", lambda e: e.dma_start(out=gwT[:, :], in_=d_gw), inc=sP, by=16)
    pr.op("GP", lambda e: e.dma_start(out=bkb[:, :], in_=d_bk), inc=sP, by=16)
    pr.op("GP", lambda e: e.dma_start(out=bvb[:, :], in_=d_bv), inc=sP, by=16)
    pr.op("GP", lambda e: e.dma_start(out=bqcol[:, :], in_=d_bqc), inc=sP, by=16)
    pr.op("GP", lambda e: e.dma_start(out=bocol[:, :], in_=d_boc), inc=sP, by=16)
    s_pre = pr.op("GP", lambda e: e.dma_start(out=bfcol[:, :], in_=d_bfc),
                  inc=sP, by=16)

    t_stats = []
    a_sq = []
    for c in range(NCH):
        xq = xsqb if c % 2 == 0 else xsqb2
        waits = [(sXq2[c // 2 % 2], g_x16[c // 2])]
        if c >= 2:
            waits.append((tP, t_stats[c - 2]))
        a_sq.append(pr.op("SC", lambda e, c=c, xq=xq: e.activation(
            xq[:, :], xn[:, c * NT:(c + 1) * NT], AF.Square), waits=waits, inc=aS))
        for h in range(2):
            pr.op("TE", lambda e, c=c, h=h: e.matmul(
                ps[h][0:1, :], ones_c[:, :],
                xn[:, c * NT + h * LH: c * NT + (h + 1) * LH],
                start=(c == 0), stop=(c == NCH - 1)),
                waits=[(sXq2[c // 2 % 2], g_x16[c // 2]), (aV, ve_ones)])
        pr.op("TE", lambda e, c=c, xq=xq: e.matmul(
            ps[2][0:1, :], ones_c[:, :], xq[:, 0:LH],
            start=(c == 0), stop=(c == NCH - 1)),
            waits=[(aS, a_sq[c])])
        t_stats.append(pr.op("TE", lambda e, c=c, xq=xq: e.matmul(
            ps[3][0:1, :], ones_c[:, :], xq[:, LH:NT],
            start=(c == 0), stop=(c == NCH - 1)), inc=tP))

    a_mean = None
    for h in range(2):
        a_mean = pr.op("SC", lambda e, h=h: e.activation(
            r_mean[0:1, h * LH:(h + 1) * LH], ps[h][0:1, :], AF.Copy,
            scale=1.0 / D), waits=[(tP, t_stats[-1])], inc=aS)
        a_mean = pr.op("SC", lambda e, h=h: e.activation(
            r_ex2[0:1, h * LH:(h + 1) * LH], ps[2 + h][0:1, :], AF.Copy,
            scale=1.0 / D), inc=aS)
    v_ms = pr.op("VE", lambda e: e.tensor_tensor(
        r_scr[0:1, :], r_mean[0:1, :], r_mean[0:1, :], ALU.mult),
        waits=[(aS, a_mean)], inc=aV)
    v_var = pr.op("VE", lambda e: e.tensor_tensor(
        r_scr[0:1, :], r_ex2[0:1, :], r_scr[0:1, :], ALU.subtract),
        waits=[(aV, v_ms)], inc=aV)
    a_lnv = pr.op("SC", lambda e: e.activation(
        r_ex2[0:1, :], r_scr[0:1, :], AF.Ln, bias=epsb[0:1, :]),
        waits=[(aV, v_var), (aV, ve_ones)], inc=aS)
    a_rstd = pr.op("SC", lambda e: e.activation(
        r_scr[0:1, :], r_ex2[0:1, :], AF.Exp, scale=-0.5),
        waits=[(aS, a_lnv)], inc=aS)
    v_mr = pr.op("VE", lambda e: e.tensor_tensor(
        r_mean[0:1, :], r_mean[0:1, :], r_scr[0:1, :], ALU.mult),
        waits=[(aS, a_rstd)], inc=aV)
    t_bc = None
    for h in range(2):
        pr.op("TE", lambda e, h=h: e.matmul(
            ps[4 + h][:, :], onesf[:, 1:129], r_scr[0:1, h * LH:(h + 1) * LH],
            start=True, stop=True), waits=[(aV, v_mr)])
        t_bc = pr.op("TE", lambda e, h=h: e.matmul(
            ps[6 + h][:, :], onesf[:, 1:129], r_mean[0:1, h * LH:(h + 1) * LH],
            start=True, stop=True), inc=tP)
    a_rsb = None
    for h in range(2):
        a_rsb = pr.op("SC", lambda e, h=h: e.activation(
            rstd_sb[:, h * LH:(h + 1) * LH], ps[4 + h][:, :], AF.Copy),
            waits=[(tP, t_bc)], inc=aS)
    a_mrb = None
    for h in range(2):
        a_mrb = pr.op("SC", lambda e, h=h: e.activation(
            mrsb[:, h * LH:(h + 1) * LH], ps[6 + h][:, :], AF.Copy),
            waits=[(tP, t_bc)], inc=aS)

    arena_h = [arena[:, 0:8192], arena[:, 8192:16384]]
    g_wkb = [pr.op("GP", lambda e: e.dma_start(
        out=arena_h[0], in_=d_wk[0]), inc=sWa, by=16)]
    g_wvb = [pr.op("GP", lambda e: e.dma_start(
        out=arena_h[1], in_=d_wv[0]), inc=sWb, by=16)]

    p_xn = []
    eng_cnt = {"VE": 0, "GP": 0}
    last_buf = {}
    for c in range(NCH):
        on_ve = True
        eng, sem = ("VE", aV) if on_ve else ("GP", aP)
        k = eng_cnt[eng]
        eng_cnt[eng] += 1
        lsl = ([vA, vB] if on_ve else [pA, pB])[k % 2]
        cs = slice(c * NT, (c + 1) * NT)
        mw = [(aS, a_mrb)]
        if (eng, k % 2) in last_buf:
            mw.append((sem, last_buf[(eng, k % 2)]))
        vmul = pr.op(eng, lambda e, cs=cs, lsl=lsl: e.tensor_tensor(
            lsl[:, :], xn[:, cs], rstd_sb[:, :], ALU.mult),
            waits=mw, inc=sem)
        lastp = pr.op(eng, lambda e, cs=cs, lsl=lsl: e.tensor_tensor(
            xn[:, cs], lsl[:, :], mrsb[:, :], ALU.subtract),
            waits=[(sem, vmul)], inc=sem)
        last_buf[(eng, k % 2)] = lastp
        p_xn.append((sem, lastp))
    rel45 = [(aS, a_rsb)]

    kts = [ktm[:, 0:2048], ktm[:, 2048:4096]]
    vts = [vtm[:, 0:2048], vtm[:, 2048:4096]]
    prev_vphi = None
    t_kmm = []
    t_vmm = []
    t_kvh = []
    t_ksum = []
    s_pack = []
    evA = {}
    vphi_blk = []
    acp_blk = []
    for b in range(4):
        if b >= 1:
            g_wkb.append(pr.op("GP", lambda e, b=b: e.dma_start(
                out=arena_h[0], in_=d_wk[b]),
                waits=[(tP, t_kmm[b - 1])], inc=sWa, by=16))
            g_wvb.append(pr.op("GP", lambda e, b=b: e.dma_start(
                out=arena_h[1], in_=d_wv[b]),
                waits=[(tP, t_vmm[b - 1])], inc=sWb, by=16))
        for is_k in (True, False):
            gld = g_wkb[b] if is_k else g_wvb[b]
            sld = sWa if is_k else sWb
            ah = arena_h[0] if is_k else arena_h[1]
            dstT = (kts if is_k else vts)[b % 2]
            for lt in range(8):
                bank = lt % 4
                bw = [(sld, gld), (aV, ve_ones), (sP, s_pre)]
                bw += evA.get(bank, [])
                if b >= 2:
                    bw.append((tP, t_ksum[b - 2]))
                for c in range(NCH):
                    cw = (bw if c == 0 else []) + [p_xn[c]]
                    pr.op("TE", lambda e, c=c, lt=lt, ah=ah, bank=bank: e.matmul(
                        ps[bank][:, 0:256],
                        xn[:, c * NT + lt * P: c * NT + (lt + 1) * P],
                        ah[:, c * 256:(c + 1) * 256],
                        start=(c == 0), stop=False),
                        waits=cw)
                brow = (bkb if is_k else bvb)[0:1, b * 256:(b + 1) * 256]
                t_this = pr.op("TE", lambda e, brow=brow, bank=bank: e.matmul(
                    ps[bank][:, 0:256], ones_r[:, :], brow,
                    start=False, stop=True), inc=tP)
                dst = dstT[:, lt * 256:(lt + 1) * 256]
                if is_k:
                    ew = [(tP, t_this)]
                    if prev_vphi is not None:
                        ew.append((aV, prev_vphi))
                    pr.op("SC", lambda e, bank=bank: e.activation(
                        lnAb[:, 0:256], ps[bank][:, 0:256], AF.Exp),
                        waits=ew, inc=aS)
                    a_r = pr.op("SC", lambda e, bank=bank: e.activation(
                        lnAb[:, 256:512], ps[bank][:, 0:256], AF.Relu),
                        inc=aS)
                    prev_vphi = pr.op("VE", lambda e, dst=dst:
                                      e.scalar_tensor_tensor(
                                          dst, lnAb[:, 0:256], 1.0,
                                          lnAb[:, 256:512], ALU.min, ALU.add),
                                      waits=[(aS, a_r)], inc=aV)
                    evA[bank] = [(aS, a_r)]
                else:
                    a_cp = pr.op("SC", lambda e, dst=dst, bank=bank: e.activation(
                        dst, ps[bank][:, 0:256], AF.Copy),
                        waits=[(tP, t_this)], inc=aS)
                    evA[bank] = [(aS, a_cp)]
            if is_k:
                t_kmm.append(t_this)
                vphi_blk.append(prev_vphi)
            else:
                t_vmm.append(t_this)
                acp_blk.append(aS.n)
        t_last = None
        for sub in range(2):
            hh = 2 * b + sub
            bank = 4 + sub
            bw = [(aV, vphi_blk[b]), (aS, acp_blk[b])]
            if b == 0:
                bw += rel45
            if b >= 1:
                bw.append((aV, t_kvh[b - 1][1][sub]))
            for lt in range(8):
                t_last = pr.op(
                    "TE", lambda e, lt=lt, sub=sub, b=b, bank=bank: e.matmul(
                        ps[bank][:, 0:P],
                        kts[b % 2][:, lt * 256 + sub * P: lt * 256 + (sub + 1) * P],
                        vts[b % 2][:, lt * 256 + sub * P: lt * 256 + (sub + 1) * P],
                        start=(lt == 0), stop=(lt == 7)),
                    waits=bw if lt == 0 else [],
                    inc=(tP if (sub == 1 and lt == 7) else None))
        v_evs = []
        for sub in range(2):
            hh = 2 * b + sub
            sK = sKa if sub == 0 else sKb
            vw = [(tP, t_last)]
            if b >= 1:
                vw.append((sK, s_pack[2 * (b - 1) + sub]))
            v_ev = pr.op("VE", lambda e, sub=sub, bank=4 + sub: e.tensor_copy(
                kvst[:, sub * P:(sub + 1) * P], ps[bank][:, 0:P]),
                waits=vw, inc=aV)
            v_evs.append(v_ev)
            s_pack.append(pr.op("SY", lambda e, hh=hh, sub=sub: e.dma_start(
                out=cc_in.ap()[hh * P:(hh + 1) * P, :],
                in_=kvst[:, sub * P:(sub + 1) * P]),
                waits=[(aV, v_ev)], inc=sK, by=16))
        t_kvh.append((t_last, v_evs))
        ksbank = 6 + b // 2
        kcol = (b % 2) * 256
        t_kb = None
        for lt in range(8):
            kw = [(aV, vphi_blk[b])]
            if b == 0:
                kw.append((aS, a_mrb))
            t_kb = pr.op("TE", lambda e, lt=lt, b=b, ksbank=ksbank, kcol=kcol:
                         e.matmul(
                             ps[ksbank][0:1, kcol:kcol + 256], ones_c[:, :],
                             kts[b % 2][:, lt * 256:(lt + 1) * 256],
                             start=(lt == 0), stop=(lt == 7)),
                         waits=kw if lt == 0 else [],
                         inc=(tP if lt == 7 else None))
        t_ksum.append(t_kb)
    t_ks = t_ksum[-1]
    a_ks = None
    for h in range(2):
        a_ks = pr.op("SC", lambda e, h=h: e.activation(
            r_scr[0:1, h * LH:(h + 1) * LH], ps[6 + h][0:1, :], AF.Copy),
            waits=[(tP, t_ks)], inc=aS)
    s_ks = None
    for hh in range(NHEAD):
        s_ks = pr.op("SY", lambda e, hh=hh: e.dma_start(
            out=cc_in.ap()[DK + hh:DK + hh + 1, :],
            in_=r_scr[0:1, hh * P:(hh + 1) * P]),
            waits=[(aS, a_ks)], inc=sKs, by=16)

    cc_waits = [(sKs, s_ks), (sKa, sKa.n), (sKb, sKb.n)]

    colq5 = [arena[:, 4096 * i:4096 * (i + 1)] for i in range(4)] + [qex[:, :]]
    qsems = [sWa, sQb, sBa, sBb, sQc]
    t_qcol = []
    ev_q = {}
    g_qcol = []
    v_phi_q = []
    for j in range(NCH):
        sW = qsems[j % 5]
        if j < 2:
            waits = [(tP, t_kmm[3])]
        elif j < 4:
            waits = [(tP, t_vmm[3])]
        elif j == 4:
            waits = [(tP, t_ks), (tP, t_kvh[3][0])]
        else:
            waits = [(tP, t_qcol[j - 5])]
        g_qcol.append(pr.op("GP", lambda e, j=j: e.dma_start(
            out=colq5[j % 5], in_=d_wq[j]), waits=waits, inc=sW, by=16))
        if j == 5:
            g_eye = pr.op("GP", lambda e: e.dma_start(
                out=eyeTb[:, :], in_=d_eyeT),
                waits=[(tP, t_kvh[3][0])], inc=sEy, by=16)
        bank0, bank1 = (j % 2) * 2, (j % 2) * 2 + 1
        bw = [(sW, g_qcol[j])]
        for k in (bank0, bank1):
            bw += ev_q.get(k, [])
        tq = None
        for c in range(NCH):
            pr.op("TE", lambda e, j=j, c=c, bank0=bank0: e.matmul(
                ps[bank0][:, :], colq5[j % 5][:, c * P:(c + 1) * P],
                xn[:, c * NT: c * NT + LH],
                start=(c == 0), stop=(c == NCH - 1)),
                waits=bw if c == 0 else [])
            tq = pr.op("TE", lambda e, j=j, c=c, bank1=bank1: e.matmul(
                ps[bank1][:, :], colq5[j % 5][:, c * P:(c + 1) * P],
                xn[:, c * NT + LH: c * NT + NT],
                start=(c == 0), stop=(c == NCH - 1)),
                inc=(tP if c == NCH - 1 else None))
        t_qcol.append(tq)
        for h in range(2):
            bank = bank0 if h == 0 else bank1
            u = h
            s_e = qsc[:, (2 * u) * LH:(2 * u + 1) * LH]
            s_r = qsc[:, (2 * u + 1) * LH:(2 * u + 2) * LH]
            ew = [(tP, t_qcol[j]), (sP, s_pre)]
            if j >= 1:
                ew.append((aV, v_phi_q[2 * (j - 1) + h]))
            a_e = pr.op("SC", lambda e, bank=bank, j=j, s_e=s_e: e.activation(
                s_e, ps[bank][:, :], AF.Exp, bias=bqcol[:, j:j + 1]),
                waits=ew, inc=aS)
            a_r = pr.op("SC", lambda e, bank=bank, j=j, s_r=s_r: e.activation(
                s_r, ps[bank][:, :], AF.Relu, bias=bqcol[:, j:j + 1]), inc=aS)
            dst = qb[:, j * NT + h * LH: j * NT + (h + 1) * LH]
            vp = pr.op("VE", lambda e, dst=dst, s_e=s_e, s_r=s_r:
                       e.scalar_tensor_tensor(
                           dst, s_e, 1.0, s_r, ALU.min, ALU.add),
                       waits=[(aS, a_r)], inc=aV)
            v_phi_q.append(vp)
            ev_q[bank] = [(aS, a_r)]

        if j == 4:
            c_cc = pr.op("GP", lambda e: e.collective_compute(
                "AllReduce", ALU.add,
                replica_groups=[list(range(N_CORES))],
                ins=[cc_in.ap()], outs=[cc_out.ap()]),
                waits=cc_waits, inc=cC, by=1)

        if j == 16:
            g_kvtb = pr.op("GP", lambda e: e.dma_start(
                out=kvtb[:, :],
                in_=cc_out.ap()[0:DK, :].rearrange("(h d) v -> d h v", d=P)),
                waits=[(cC, c_cc)], inc=sX, by=16)
            s_ksb = pr.op("SY", lambda e: e.dma_start(
                out=ksb[:, :],
                in_=cc_out.ap()[DK:DK + NE, :].rearrange("h v -> v h"),
                allow_slow_non_contiguous=True),
                waits=[(cC, c_cc)], inc=sKR, by=16)
            v_k1 = pr.op("VE", lambda e: e.tensor_single_scalar(
                ksb_r[:, :], ksb[:, :], 1e-10, ALU.add),
                waits=[(sKR, s_ksb)], inc=aV)
            pr.op("VE", lambda e: e.reciprocal(ksb_r[:, :], ksb_r[:, :]),
                  waits=[(aV, v_k1)], inc=aV)
            v_ksr = aV.n

        if j == 20:
            t_rt = []
            for h in range(2):
                tr = None
                for c in range(NCH):
                    tr = pr.op("TE", lambda e, h=h, c=c: e.matmul(
                        ps[6][0:8, :] if h == 0 else ps[7][0:8, :],
                        gwT[:, c * NE:(c + 1) * NE],
                        xn[:, c * NT + h * LH: c * NT + h * LH + LH],
                        start=(c == 0), stop=(c == NCH - 1)),
                        waits=([(sP, s_pre), (aS, a_ks), (cC, c_cc)]
                               if c == 0 else []),
                        inc=(tP if c == NCH - 1 else None))
                t_rt.append(tr)
        if j == 21:
            a_exp = []
            for h in range(2):
                a_exp.append(pr.op("SC", lambda e, h=h: e.activation(
                    expL[0:8, h * LH:(h + 1) * LH], ps[6 + h][0:8, :], AF.Exp),
                    waits=[(tP, t_rt[h])], inc=aS))
        if j == 22:
            t_z = []
            for h in range(2):
                t_z.append(pr.op("TE", lambda e, h=h: e.matmul(
                    ps[6 + h][0:1, :], ones_c[0:8, :],
                    expL[0:8, h * LH:(h + 1) * LH],
                    start=True, stop=True), waits=[(aS, a_exp[h])], inc=tP))
        if j == 23:
            a_z = None
            for h in range(2):
                a_z = pr.op("SC", lambda e, h=h: e.activation(
                    r_ex2[0:1, h * LH:(h + 1) * LH], ps[6 + h][0:1, :],
                    AF.Copy), waits=[(tP, t_z[h])], inc=aS)
            a_zl = pr.op("SC", lambda e: e.activation(
                r_mean[0:1, :], r_ex2[0:1, :], AF.Ln),
                waits=[(aS, a_z)], inc=aS)
            a_zi = pr.op("SC", lambda e: e.activation(
                r_mean[0:1, :], r_mean[0:1, :], AF.Exp, scale=-1.0),
                waits=[(aS, a_zl)], inc=aS)
        if j == 24:
            t_zb = []
            for h in range(2):
                t_zb.append(pr.op("TE", lambda e, h=h: e.matmul(
                    ps[6 + h][0:8, :], onesf[:, 1:9],
                    r_mean[0:1, h * LH:(h + 1) * LH],
                    start=True, stop=True), waits=[(aS, a_zi)], inc=tP))
            v_rout8 = None
            for h in range(2):
                v_rout8 = pr.op("VE", lambda e, h=h: e.tensor_tensor(
                    routb[0:8, h * LH:(h + 1) * LH],
                    expL[0:8, h * LH:(h + 1) * LH],
                    ps[6 + h][0:8, :], ALU.mult),
                    waits=[(tP, t_zb[h])], inc=aV)
        if j == 25:
            t_rf = []
            for h in range(2):
                t_rf.append(pr.op("TE", lambda e, h=h: e.matmul(
                    ps[6 + h][:, :], eyeTb[:, :],
                    routb[0:8, h * LH:(h + 1) * LH],
                    start=True, stop=True),
                    waits=[(aV, v_rout8), (sEy, g_eye)], inc=tP))
            v_rout = None
            for h in range(2):
                v_rout = pr.op("VE", lambda e, h=h: e.tensor_copy(
                    routf[:, h * LH:(h + 1) * LH], ps[6 + h][:, :]),
                    waits=[(tP, t_rf[h])], inc=aV)
    v_q_done = aV.n

    bank_ev = {b: list(ev_q.get(b, [])) for b in range(6)}
    bank_ev[6] = [(aV, v_rout)]
    bank_ev[7] = [(aV, v_rout)]

    t_attn_done = None
    for qh in range(NQH):
        kvh = qh // 4
        for h in range(2):
            u = 2 * qh + h
            bank = u % 6
            bw = [(sX, g_kvtb), (aV, v_phi_q[u])] + bank_ev.get(bank, [])
            t_a = pr.op("TE", lambda e, qh=qh, h=h, kvh=kvh, bank=bank: e.matmul(
                ps[bank][:, :], kvtb[:, kvh * P:(kvh + 1) * P],
                qb[:, qh * NT + h * LH: qh * NT + (h + 1) * LH],
                start=True, stop=True), waits=bw, inc=tP)
            t_attn_done = t_a
            dst = qb[:, qh * NT + h * LH: qh * NT + (h + 1) * LH]
            if u % 2 == 0:
                v_a = pr.op("VE", lambda e, dst=dst, bank=bank, kvh=kvh:
                            e.tensor_single_scalar(
                                dst, ps[bank][:, :], ksb_r[:, kvh:kvh + 1],
                                ALU.mult),
                            waits=[(tP, t_a), (aV, v_ksr)], inc=aV)
                bank_ev[bank] = [(aV, v_a)]
            else:
                s_a = pr.op("SC", lambda e, dst=dst, bank=bank, kvh=kvh:
                            e.activation(
                                dst, ps[bank][:, :], AF.Copy,
                                scale=ksb_r[:, kvh:kvh + 1]),
                            waits=[(tP, t_a), (aV, v_ksr)], inc=aS)
                bank_ev[bank] = [(aS, s_a)]
    attn_v_done = aV.n
    attn_s_done = aS.n

    g_ict = []
    for j in range(8):
        g_ict.append(pr.op("GP", lambda e, j=j: e.dma_start(
            out=icst[:, (j % 8) * P:(j % 8 + 1) * P],
            in_=d_ics[:, j * P:(j + 1) * P]), inc=sIq[j % 8], by=16))

    colw = [arena[:, 0:4096], arena[:, 4096:8192]]
    colb = [arena[:, 8192:12288], arena[:, 12288:16384]]
    g_wo = []
    g_wb = []
    g_oct = []
    g_res = []
    a_sgn = []
    for j in range(2):
        sW = sWa if j % 2 == 0 else sWb
        sB = sBa if j % 2 == 0 else sBb
        sR = sRa if j % 2 == 0 else sRb
        sO = sOa if j % 2 == 0 else sOb
        g_oct.append(pr.op("GP", lambda e, j=j: e.dma_start(
            out=ocst[:, (j % 2) * P:(j % 2 + 1) * P],
            in_=d_ocs[:, j * P:(j + 1) * P]),
            waits=[(aV, v_rout)], inc=sO, by=16))
        g_wo.append(pr.op("GP", lambda e, j=j: e.dma_start(
            out=colw[j % 2], in_=d_wo[j]),
            waits=[(tP, t_qcol[-1])], inc=sW, by=16))
        g_wb.append(pr.op("GP", lambda e, j=j: e.dma_start(
            out=colb[j % 2], in_=d_wb[j]),
            waits=[(tP, t_qcol[-1])], inc=sB, by=16))
        a_sgn.append(pr.op("VE", lambda e, j=j: e.tensor_scalar(
            colb[j % 2][:, :], colb[j % 2][:, :], 0.0, 0.5,
            ALU.is_ge, ALU.subtract),
            waits=[(sB, g_wb[j])], inc=aV))
    for j in range(2):
        g_res.append(pr.op("GP", lambda e, j=j: e.dma_start(
            out=residb[:, (j % 2) * NT:(j % 2 + 1) * NT],
            in_=d_xT[:, j * NT:(j + 1) * NT]),
            waits=[(tP, t_attn_done)], inc=(sRa if j % 2 == 0 else sRb), by=16))

    pre_thunks = []
    for h in range(2):
        u = h % 2
        bS = 2 if u == 0 else 5
        bO = 4 if u == 0 else 7
        bw0 = [(sWa, g_wo[0]), (aV, attn_v_done), (aS, attn_s_done)]
        bw0 += bank_ev[bS]
        for c in range(NCH):
            pre_thunks.append((
                "S", h,
                lambda e, h=h, c=c, bS=bS: e.matmul(
                    ps[bS][:, :], colw[0][:, c * P:(c + 1) * P],
                    qb[:, c * NT + h * LH: c * NT + (h + 1) * LH],
                    start=(c == 0), stop=(c == NCH - 1)),
                bw0 if c == 0 else [], c == NCH - 1))
        bwo = [(sOa, g_oct[0]), (aV, v_rout)] + bank_ev[bO]
        pre_thunks.append((
            "O", h,
            lambda e, h=h, bO=bO: e.matmul(
                ps[bO][:, :], ocst[:, 0:P],
                routf[:, h * LH:(h + 1) * LH],
                start=True, stop=True), bwo, True))

    pre_tS = {}
    pre_tO = {}
    pre_iter = iter(pre_thunks)

    def drain_pre(n):
        for _ in range(n):
            item = next(pre_iter, None)
            if item is None:
                return
            kind, h, fn, w, is_end = item
            v = pr.op("TE", fn, waits=w, inc=(tP if is_end else None))
            if is_end:
                (pre_tS if kind == "S" else pre_tO)[h] = v

    xs_banks = [6, 0, 1, 3]
    t_ic = []
    for j in range(NCH):
        sI = sIq[j % 8]
        if j >= 8:
            g_ict.append(pr.op("GP", lambda e, j=j: e.dma_start(
                out=icst[:, (j % 8) * P:(j % 8 + 1) * P],
                in_=d_ics[:, j * P:(j + 1) * P]),
                waits=[(tP, t_ic[j - 8])], inc=sI, by=16))
        for h in range(2):
            bank = xs_banks[(2 * j + h) % 4]
            bw = [(sI, g_ict[j]), (aV, v_rout), (tP, t_qcol[-1])]
            bw += bank_ev.get(bank, [])
            t_i = pr.op("TE", lambda e, j=j, h=h, bank=bank: e.matmul(
                ps[bank][:, :], icst[:, (j % 8) * P:(j % 8 + 1) * P],
                routf[:, h * LH:(h + 1) * LH],
                start=True, stop=True), waits=bw, inc=tP)
            sl = slice(j * NT + h * LH, j * NT + (h + 1) * LH)
            vx = pr.op("VE", lambda e, sl=sl, bank=bank:
                       e.scalar_tensor_tensor(
                           xn[:, sl], ps[bank][:, :], 1.0 / 16, xn[:, sl],
                           ALU.mult, ALU.mult),
                       waits=[(tP, t_i)], inc=aV)
            bank_ev[bank] = [(aV, vx)]
            if j >= 6:
                drain_pre(2)
        t_ic.append(t_i)
    drain_pre(len(pre_thunks))
    xs_v_done = aV.n

    t_scol = []
    t_mcol = []
    t_oc = []
    s_store = []
    ev_fin = {
        ("M", 0): bank_ev[3][0],
        ("M", 1): bank_ev[6][0],
    }
    v_out = {}
    for j in range(NCH):
        sW = sWa if j % 2 == 0 else sWb
        sB = sBa if j % 2 == 0 else sBb
        sR = sRa if j % 2 == 0 else sRb
        sO = sOa if j % 2 == 0 else sOb
        sS = sSa if j % 2 == 0 else sSb
        if j >= 2:
            g_wo.append(pr.op("GP", lambda e, j=j: e.dma_start(
                out=colw[j % 2], in_=d_wo[j]),
                waits=[(tP, t_scol[j - 2])], inc=sW, by=16))
            g_wb.append(pr.op("GP", lambda e, j=j: e.dma_start(
                out=colb[j % 2], in_=d_wb[j]),
                waits=[(tP, t_mcol[j - 2])], inc=sB, by=16))
            a_sgn.append(pr.op("VE", lambda e, j=j: e.tensor_scalar(
                colb[j % 2][:, :], colb[j % 2][:, :], 0.0, 0.5,
                ALU.is_ge, ALU.subtract),
                waits=[(sB, g_wb[j])], inc=aV))
            g_res.append(pr.op("GP", lambda e, j=j: e.dma_start(
                out=residb[:, (j % 2) * NT:(j % 2 + 1) * NT],
                in_=d_xT[:, j * NT:(j + 1) * NT]),
                waits=[(aV, v_out[(j - 2, 1)])], inc=sR, by=16))
            g_oct.append(pr.op("GP", lambda e, j=j: e.dma_start(
                out=ocst[:, (j % 2) * P:(j % 2 + 1) * P],
                in_=d_ocs[:, j * P:(j + 1) * P]),
                waits=[(tP, t_oc[j - 2])], inc=sO, by=16))
        for h in range(2):
            u = (2 * j + h) % 2
            bS, bM, bO = (2, 3, 4) if u == 0 else (5, 6, 7)
            s_os = scr[:, (3 * u) * LH:(3 * u + 1) * LH]
            s_P = scr[:, (3 * u + 1) * LH:(3 * u + 2) * LH]
            s_g = scr[:, (3 * u + 2) * LH:(3 * u + 3) * LH]
            if j == 0:
                tS = pre_tS[h]
                tO = pre_tO[h]
            else:
                bw = [(sW, g_wo[j]), (aV, attn_v_done), (aS, attn_s_done)]
                if ("S", u) in ev_fin:
                    bw.append(ev_fin[("S", u)])
                tS = None
                for c in range(NCH):
                    tS = pr.op("TE", lambda e, j=j, h=h, c=c, bS=bS: e.matmul(
                        ps[bS][:, :], colw[j % 2][:, c * P:(c + 1) * P],
                        qb[:, c * NT + h * LH: c * NT + (h + 1) * LH],
                        start=(c == 0), stop=(c == NCH - 1)),
                        waits=bw if c == 0 else [],
                        inc=(tP if c == NCH - 1 else None))
                bw = [(sO, g_oct[j]), (aV, v_rout)]
                if ("O", u) in ev_fin:
                    bw.append(ev_fin[("O", u)])
                tO = pr.op("TE", lambda e, j=j, h=h, bO=bO: e.matmul(
                    ps[bO][:, :], ocst[:, (j % 2) * P:(j % 2 + 1) * P],
                    routf[:, h * LH:(h + 1) * LH],
                    start=True, stop=True), waits=bw, inc=tP)
            if h == 1:
                t_scol.append(tS)
                t_oc.append(tO)
            bw = [(aV, a_sgn[j]), (aV, xs_v_done)]
            if ("M", u) in ev_fin:
                bw.append(ev_fin[("M", u)])
            tM = None
            for c in range(NCH):
                tM = pr.op("TE", lambda e, j=j, h=h, c=c, bM=bM: e.matmul(
                    ps[bM][:, :], colb[j % 2][:, c * P:(c + 1) * P],
                    xn[:, c * NT + h * LH: c * NT + (h + 1) * LH],
                    start=(c == 0), stop=(c == NCH - 1)),
                    waits=bw if c == 0 else [],
                    inc=(tP if c == NCH - 1 else None))
            if h == 1:
                t_mcol.append(tM)
            a_os = pr.op("SC", lambda e, bO=bO, s_os=s_os: e.activation(
                s_os, ps[bO][:, :], AF.Copy, scale=1.0 / 16),
                waits=[(tP, tO)], inc=aS)
            v_P = pr.op("VE", lambda e, bM=bM, s_P=s_P, s_os=s_os: e.tensor_tensor(
                s_P, ps[bM][:, :], s_os, ALU.mult),
                waits=[(tP, tM), (aS, a_os)], inc=aV)
            a_g = pr.op("SC", lambda e, j=j, s_P=s_P, s_g=s_g: e.activation(
                s_g, s_P, AF.Silu, bias=bfcol[:, j:j + 1], scale=2.0),
                waits=[(aV, v_P), (sP, s_pre)], inc=aS)
            v_sg = pr.op("VE", lambda e, j=j, bS=bS, s_os=s_os, s_g=s_g:
                         e.scalar_tensor_tensor(
                             s_os, ps[bS][:, :], bocol[:, j:j + 1], s_g,
                             ALU.add, ALU.mult),
                         waits=[(tP, tS), (aS, a_g), (sP, s_pre)], inc=aV)
            ow = [(aV, v_sg), (sR, g_res[j])]
            if j >= 2:
                ow.append((sS, s_store[j - 2]))
            vo = pr.op("VE", lambda e, j=j, h=h, s_os=s_os: e.tensor_tensor(
                outst[:, (j % 2) * NT + h * LH:(j % 2) * NT + (h + 1) * LH],
                s_os, residb[:, (j % 2) * NT + h * LH:(j % 2) * NT + (h + 1) * LH],
                ALU.add), waits=ow, inc=aV)
            v_out[(j, h)] = vo
            ev_fin[("S", u)] = (aV, v_sg)
            ev_fin[("M", u)] = (aV, v_P)
            ev_fin[("O", u)] = (aS, a_os)
        s_store.append(pr.op("SY", lambda e, j=j: e.dma_start(
            out=d_out[j], in_=outst[:, (j % 2) * NT:(j % 2 + 1) * NT]),
            waits=[(aV, v_out[(j, 1)])], inc=sS, by=16))

    pr.op("SY", lambda e: e.nop(), waits=[(sSa, sSa.n), (sSb, sSb.n)])

    from contextlib import ExitStack
    with ExitStack() as ctx:
        for s in sems:
            s.h = ctx.enter_context(nc.semaphore(s.name))
        block = ctx.enter_context(nc.Block())

        def runner(key):
            def cb(eng):
                hw = {}
                for (k, fn, waits, inc, by) in pr.ops:
                    if k != key:
                        continue
                    for (s, v) in waits:
                        if hw.get(s.name, 0) < v:
                            eng.wait_ge(s.h, v)
                            hw[s.name] = v
                    inst = fn(eng)
                    if inc is not None and inst is not None:
                        if by is None:
                            inst.then_inc(inc.h)
                        else:
                            inst.then_inc(inc.h, by)
            return cb

        block.sync(runner("SY"))
        block.scalar(runner("SC"))
        block.vector(runner("VE"))
        block.tensor(runner("TE"))
        block.gpsimd(runner("GP"))

    return nc


_NC_CACHE = []


def kernel(**inputs):
    f = np.float32

    def g(name):
        return np.asarray(inputs[name], dtype=f)

    x = g("L0_attn_hidden_states")[0]
    Wq, bq = g("L0_attn_Wq"), g("L0_attn_bq")
    Wk, bk = g("L0_attn_Wk"), g("L0_attn_bk")
    Wv, bv = g("L0_attn_Wv"), g("L0_attn_bv")
    Wo, bo = g("L0_attn_Wo"), g("L0_attn_bo")
    Wf, bf = g("L0_ffn_weight"), g("L0_ffn_bias")
    gw = g("L0_ffn_gate_w")
    ics = g("L0_ffn_in_channel_scale")
    ocs = g("L0_ffn_out_channel_scale")

    def colt(W):
        O = W.shape[0]
        return np.ascontiguousarray(
            W.T.reshape(NCH, P, O // P, P).transpose(2, 1, 0, 3)
        ).reshape(O // P, P, NCH * P)

    wq_t = colt(Wq)
    wo_t = colt(Wo)
    wb_t = colt(Wf)

    def blkt(W):
        return np.ascontiguousarray(
            W.T.reshape(NCH, P, 4, 256).transpose(2, 1, 0, 3)
        ).reshape(4, P, NCH * 256)

    wk_t = blkt(Wk)
    wv_t = blkt(Wv)
    gw_t = np.ascontiguousarray(
        gw.T.reshape(NCH, P, NE).transpose(1, 0, 2)).reshape(P, NCH * NE)
    bqc = np.ascontiguousarray(bq.reshape(NCH, P).T)
    boc = np.ascontiguousarray(bo.reshape(NCH, P).T)
    bfc = np.ascontiguousarray(bf.reshape(NCH, P).T)
    bk2 = np.ascontiguousarray(bk.reshape(1, DK))
    bv2 = np.ascontiguousarray(bv.reshape(1, DK))
    ics = np.ascontiguousarray(np.tile(ics, (16, 1)))
    ocs = np.ascontiguousarray(np.tile(ocs, (16, 1)))
    eyeT = np.zeros((NE, P), dtype=f)
    for e_ in range(NE):
        eyeT[e_, e_::NE] = 1.0

    xT = np.ascontiguousarray(x.T)
    in_maps = []
    for c in range(N_CORES):
        xc = np.ascontiguousarray(
            xT[:, c * NT:(c + 1) * NT].reshape(NCH, P, NT)
            .transpose(1, 0, 2)).reshape(P, NCH * NT)
        in_maps.append({
            "xT": xc, "wq": wq_t, "wo": wo_t, "wb": wb_t,
            "wk": wk_t, "wv": wv_t, "gw": gw_t, "ics": ics, "ocs": ocs,
            "bqc": bqc, "boc": boc, "bfc": bfc, "bk": bk2, "bv": bv2,
            "eyeT": eyeT,
        })

    if not _NC_CACHE:
        _NC_CACHE.append(build_program())
    nc = _NC_CACHE[0]
    res = run_bass_kernel_spmd(nc, in_maps, core_ids=list(range(N_CORES)))

    out = np.empty((8192, D), dtype=f)
    for c in range(N_CORES):
        oc = np.asarray(res.results[c]["outT"], dtype=f)
        out[c * NT:(c + 1) * NT, :] = oc.reshape(D, NT).T
    return out.reshape(1, 8192, D)


# version 25
# speedup vs baseline: 1.0848x; 1.0011x over previous
import numpy as np
import concourse.bass as bass
import concourse.mybir as mybir
from concourse.bass_utils import run_bass_kernel_spmd

F32 = mybir.dt.float32
BF16 = mybir.dt.bfloat16
AF = mybir.ActivationFunctionType
ALU = mybir.AluOpType

P = 128
NT = 1024
LH = 512
D = 4096
NCH = 32
DK = 1024
NHEAD = 8
NQH = 32
NE = 8
N_CORES = 8

SBUF_BYTES = 229376


class S:
    __slots__ = ("h", "n", "name")

    def __init__(self, h, name):
        self.h = h
        self.n = 0
        self.name = name


class Prog:
    def __init__(self):
        self.ops = []

    def op(self, eng, fn, waits=(), inc=None, by=1):
        for (s, v) in waits:
            assert v <= s.n, f"wait on future value {s.name} {v} > {s.n}"
        self.ops.append((eng, fn, [(s, v) for (s, v) in waits if v > 0], inc, by))
        if inc is not None:
            inc.n += (by if by is not None else 1)
            return inc.n
        return None


def build_program():
    nc = bass.Bass("TRN2", target_bir_lowering=False)

    d_xT = nc.dram_tensor("xT", [P, NCH * NT], F32, kind="ExternalInput").ap()
    d_wq = nc.dram_tensor("wq", [NCH, P, D], F32, kind="ExternalInput").ap()
    d_wo = nc.dram_tensor("wo", [NCH, P, D], F32, kind="ExternalInput").ap()
    d_wb = nc.dram_tensor("wb", [NCH, P, D], F32, kind="ExternalInput").ap()
    d_wk = nc.dram_tensor("wk", [4, P, 32 * 256], F32, kind="ExternalInput").ap()
    d_wv = nc.dram_tensor("wv", [4, P, 32 * 256], F32, kind="ExternalInput").ap()
    d_gw = nc.dram_tensor("gw", [P, NCH * NE], F32, kind="ExternalInput").ap()
    d_ics = nc.dram_tensor("ics", [P, D], F32, kind="ExternalInput").ap()
    d_ocs = nc.dram_tensor("ocs", [P, D], F32, kind="ExternalInput").ap()
    d_eyeT = nc.dram_tensor("eyeT", [NE, P], F32, kind="ExternalInput").ap()
    d_bqc = nc.dram_tensor("bqc", [P, NCH], F32, kind="ExternalInput").ap()
    d_boc = nc.dram_tensor("boc", [P, NCH], F32, kind="ExternalInput").ap()
    d_bfc = nc.dram_tensor("bfc", [P, NCH], F32, kind="ExternalInput").ap()
    d_bk = nc.dram_tensor("bk", [1, DK], F32, kind="ExternalInput").ap()
    d_bv = nc.dram_tensor("bv", [1, DK], F32, kind="ExternalInput").ap()
    d_out = nc.dram_tensor("outT", [NCH, P, NT], F32, kind="ExternalOutput").ap()
    cc_in = nc.dram_tensor("cc_in", [DK + NE, P], F32, kind="Internal")
    cc_out = nc.dram_tensor("cc_out", [DK + NE, P], F32, kind="Internal")

    resv = nc.alloc_sbuf_tensor("resv", [P, 212096], mybir.dt.uint8)
    BASE = (nc.lookup_mloc(resv).addr + 127) & ~127

    def at(name, shape, dtype, rel):
        off = BASE + rel
        end = off + shape[1] * (2 if dtype == BF16 else 4)
        assert end <= SBUF_BYTES, (name, off, end)
        return nc.alloc_sbuf_tensor_at(name, shape, dtype, offset=off)

    xn = at("xn", [P, NCH * NT], BF16, 0)
    qb = at("qb", [P, NCH * NT], BF16, 65536)
    arena = at("arena", [P, 16384], BF16, 131072)
    ktm = at("ktm", [P, 2 * 8 * 256], BF16, 163840)
    vtm = at("vtm", [P, 2 * 8 * 256], BF16, 172032)
    residb = at("residb", [P, 2 * NT], BF16, 163840)
    outst = at("outst", [P, 2 * NT], F32, 167936)
    scr = at("scr", [P, 6 * LH], F32, 176128)
    qex = at("qex", [P, 4096], BF16, 163840)
    kvtb = at("kvtb", [P, NHEAD * P], BF16, 182272)
    rstd_sb = at("rstd_sb", [P, NT], BF16, 176128)
    vA = at("vA", [P, NT], BF16, 178176)
    vB = at("vB", [P, NT], BF16, 180224)
    pA = at("pA", [P, NT], BF16, 182272)
    pB = at("pB", [P, NT], BF16, 184320)
    lnAb = at("lnAb", [P, LH], BF16, 186368)
    qsc = at("qsc", [P, 4 * LH], BF16, 176128)
    r_mean = at("r_mean", [1, NT], F32, 188416)
    r_ex2 = at("r_ex2", [1, NT], F32, 192512)
    r_scr = at("r_scr", [1, NT], F32, 196608)
    routb = at("routb", [NE, NT], BF16, 200704)
    bkb = at("bkb", [1, DK], BF16, 202752)
    bvb = at("bvb", [1, DK], BF16, 204800)
    gwT = at("gwT", [P, NCH * NE], BF16, 206848)
    kvst = at("kvst", [P, 2 * P], F32, 207360)
    kvscr = at("kvscr", [P, 256], F32, 208384)
    xsqb = at("xsqb", [P, NT], BF16, 207360)
    mrsb = at("mrsb", [P, NT], BF16, 207360)
    expL = at("expL", [NE, NT], BF16, 207360)
    xsqb2 = at("xsqb2", [P, NT], BF16, 176128)
    ones_r = at("ones_r", [1, P], BF16, 209408)
    onesf = at("onesf", [1, 129], F32, 209664)
    ones_c = at("ones_c", [P, 1], BF16, 210208)
    bqcol = at("bqcol", [P, NCH], F32, 210240)
    bocol = at("bocol", [P, NCH], F32, 210368)
    bfcol = at("bfcol", [P, NCH], F32, 210496)
    ksb = at("ksb", [P, NHEAD], F32, 210624)
    ksb_r = at("ksb_r", [P, NHEAD], F32, 210656)
    epsb = at("epsb", [1, 1], F32, 210688)
    icst = at("icst", [P, 8 * P], BF16, 210720)
    ocst = at("ocst", [P, 2 * P], BF16, 208384)
    eyeTb = at("eyeTb", [NE, P], BF16, 172032)
    routf = at("routf", [P, NT], BF16, 202752)

    ps = [nc.alloc_psum_tensor(f"ps{i}", [P, LH], F32) for i in range(8)]

    pr = Prog()
    sems = []

    def mk(name):
        s = S(None, name)
        sems.append(s)
        return s

    sP = mk("sP")
    sX = mk("sX")
    sX1 = mk("sX1")
    sX2 = mk("sX2")
    sX3 = mk("sX3")
    sX4 = mk("sX4")
    sX5 = mk("sX5")
    sX6 = mk("sX6")
    sX7 = mk("sX7")
    aP = mk("aP")
    sWa = mk("sWa")
    sWb = mk("sWb")
    sBa = mk("sBa")
    sBb = mk("sBb")
    sIq = [mk(f"sI{k}") for k in range(8)]
    sOa = mk("sOa")
    sOb = mk("sOb")
    sKa = mk("sKa")
    sKb = mk("sKb")
    sKs = mk("sKs")
    sKR = mk("sKR")
    sRa = mk("sRa")
    sRb = mk("sRb")
    sSa = mk("sSa")
    sSb = mk("sSb")
    sQb = mk("sQb")
    sQc = mk("sQc")
    sEy = mk("sEy")
    tP = mk("tP")
    aS = mk("aS")
    aV = mk("aV")
    cC = mk("cC")

    ve_ones = pr.op("VE", lambda e: e.memset(ones_r[:, :], 1.0), inc=aV)
    pr.op("VE", lambda e: e.memset(onesf[:, :], 1.0), inc=aV)
    pr.op("VE", lambda e: e.memset(ones_c[:, :], 1.0), inc=aV)
    ve_ones = pr.op("VE", lambda e: e.memset(epsb[:, :], 1e-5), inc=aV)

    sXq2 = [sX, sX1]
    QW = 2 * NT
    g_x16 = []
    for qq in range(16):
        g_x16.append(pr.op("GP", lambda e, qq=qq: e.dma_start(
            out=xn[:, qq * QW:(qq + 1) * QW],
            in_=d_xT[:, qq * QW:(qq + 1) * QW]), inc=sXq2[qq % 2], by=16))

    pr.op("GP", lambda e: e.dma_start(out=gwT[:, :], in_=d_gw), inc=sP, by=16)
    pr.op("GP", lambda e: e.dma_start(out=bkb[:, :], in_=d_bk), inc=sP, by=16)
    pr.op("GP", lambda e: e.dma_start(out=bvb[:, :], in_=d_bv), inc=sP, by=16)
    pr.op("GP", lambda e: e.dma_start(out=bqcol[:, :], in_=d_bqc), inc=sP, by=16)
    pr.op("GP", lambda e: e.dma_start(out=bocol[:, :], in_=d_boc), inc=sP, by=16)
    s_pre = pr.op("GP", lambda e: e.dma_start(out=bfcol[:, :], in_=d_bfc),
                  inc=sP, by=16)

    t_stats = []
    a_sq = []
    for c in range(NCH):
        xq = xsqb if c % 2 == 0 else xsqb2
        waits = [(sXq2[c // 2 % 2], g_x16[c // 2])]
        if c >= 2:
            waits.append((tP, t_stats[c - 2]))
        a_sq.append(pr.op("SC", lambda e, c=c, xq=xq: e.activation(
            xq[:, :], xn[:, c * NT:(c + 1) * NT], AF.Square), waits=waits, inc=aS))
        for h in range(2):
            pr.op("TE", lambda e, c=c, h=h: e.matmul(
                ps[h][0:1, :], ones_c[:, :],
                xn[:, c * NT + h * LH: c * NT + (h + 1) * LH],
                start=(c == 0), stop=(c == NCH - 1)),
                waits=[(sXq2[c // 2 % 2], g_x16[c // 2]), (aV, ve_ones)])
        pr.op("TE", lambda e, c=c, xq=xq: e.matmul(
            ps[2][0:1, :], ones_c[:, :], xq[:, 0:LH],
            start=(c == 0), stop=(c == NCH - 1)),
            waits=[(aS, a_sq[c])])
        t_stats.append(pr.op("TE", lambda e, c=c, xq=xq: e.matmul(
            ps[3][0:1, :], ones_c[:, :], xq[:, LH:NT],
            start=(c == 0), stop=(c == NCH - 1)), inc=tP))

    a_mean = None
    for h in range(2):
        a_mean = pr.op("SC", lambda e, h=h: e.activation(
            r_mean[0:1, h * LH:(h + 1) * LH], ps[h][0:1, :], AF.Copy,
            scale=1.0 / D), waits=[(tP, t_stats[-1])], inc=aS)
        a_mean = pr.op("SC", lambda e, h=h: e.activation(
            r_ex2[0:1, h * LH:(h + 1) * LH], ps[2 + h][0:1, :], AF.Copy,
            scale=1.0 / D), inc=aS)
    t_bcm = None
    for h in range(2):
        t_bcm = pr.op("TE", lambda e, h=h: e.matmul(
            ps[6 + h][:, :], onesf[:, 1:129], r_mean[0:1, h * LH:(h + 1) * LH],
            start=True, stop=True), waits=[(aS, a_mean)], inc=tP)
    a_mrb = None
    for h in range(2):
        a_mrb = pr.op("SC", lambda e, h=h: e.activation(
            mrsb[:, h * LH:(h + 1) * LH], ps[6 + h][:, :], AF.Copy),
            waits=[(tP, t_bcm)], inc=aS)
    v_ms = pr.op("VE", lambda e: e.tensor_tensor(
        r_scr[0:1, :], r_mean[0:1, :], r_mean[0:1, :], ALU.mult),
        waits=[(aS, a_mean)], inc=aV)
    v_var = pr.op("VE", lambda e: e.tensor_tensor(
        r_scr[0:1, :], r_ex2[0:1, :], r_scr[0:1, :], ALU.subtract),
        waits=[(aV, v_ms)], inc=aV)
    a_lnv = pr.op("SC", lambda e: e.activation(
        r_ex2[0:1, :], r_scr[0:1, :], AF.Ln, bias=epsb[0:1, :]),
        waits=[(aV, v_var), (aV, ve_ones)], inc=aS)
    a_rstd = pr.op("SC", lambda e: e.activation(
        r_scr[0:1, :], r_ex2[0:1, :], AF.Exp, scale=-0.5),
        waits=[(aS, a_lnv)], inc=aS)
    t_bc = None
    for h in range(2):
        t_bc = pr.op("TE", lambda e, h=h: e.matmul(
            ps[4 + h][:, :], onesf[:, 1:129], r_scr[0:1, h * LH:(h + 1) * LH],
            start=True, stop=True), waits=[(aS, a_rstd)], inc=tP)
    a_rsb = None
    for h in range(2):
        a_rsb = pr.op("SC", lambda e, h=h: e.activation(
            rstd_sb[:, h * LH:(h + 1) * LH], ps[4 + h][:, :], AF.Copy),
            waits=[(tP, t_bc)], inc=aS)

    arena_h = [arena[:, 0:8192], arena[:, 8192:16384]]
    g_wkb = [pr.op("GP", lambda e: e.dma_start(
        out=arena_h[0], in_=d_wk[0]), inc=sWa, by=16)]
    g_wvb = [pr.op("GP", lambda e: e.dma_start(
        out=arena_h[1], in_=d_wv[0]), inc=sWb, by=16)]

    p_xn = []
    eng_cnt = {"VE": 0, "GP": 0}
    last_buf = {}
    for c in range(NCH):
        on_ve = True
        eng, sem = ("VE", aV) if on_ve else ("GP", aP)
        k = eng_cnt[eng]
        eng_cnt[eng] += 1
        lsl = ([vA, vB] if on_ve else [pA, pB])[k % 2]
        cs = slice(c * NT, (c + 1) * NT)
        mw = [(aS, a_mrb)]
        if (eng, k % 2) in last_buf:
            mw.append((sem, last_buf[(eng, k % 2)]))
        vmul = pr.op(eng, lambda e, cs=cs, lsl=lsl: e.tensor_tensor(
            lsl[:, :], xn[:, cs], mrsb[:, :], ALU.subtract),
            waits=mw, inc=sem)
        lastp = pr.op(eng, lambda e, cs=cs, lsl=lsl: e.tensor_tensor(
            xn[:, cs], lsl[:, :], rstd_sb[:, :], ALU.mult),
            waits=[(sem, vmul), (aS, a_rsb)], inc=sem)
        last_buf[(eng, k % 2)] = lastp
        p_xn.append((sem, lastp))
    rel45 = [(aS, a_rsb)]

    kts = [ktm[:, 0:2048], ktm[:, 2048:4096]]
    vts = [vtm[:, 0:2048], vtm[:, 2048:4096]]
    prev_vphi = None
    t_kmm = []
    t_vmm = []
    t_kvh = []
    t_ksum = []
    s_pack = []
    evA = {}
    vphi_blk = []
    acp_blk = []
    for b in range(4):
        if b >= 1:
            g_wkb.append(pr.op("GP", lambda e, b=b: e.dma_start(
                out=arena_h[0], in_=d_wk[b]),
                waits=[(tP, t_kmm[b - 1])], inc=sWa, by=16))
            g_wvb.append(pr.op("GP", lambda e, b=b: e.dma_start(
                out=arena_h[1], in_=d_wv[b]),
                waits=[(tP, t_vmm[b - 1])], inc=sWb, by=16))
        for is_k in (True, False):
            gld = g_wkb[b] if is_k else g_wvb[b]
            sld = sWa if is_k else sWb
            ah = arena_h[0] if is_k else arena_h[1]
            dstT = (kts if is_k else vts)[b % 2]
            for lt in range(8):
                bank = lt % 4
                bw = [(sld, gld), (aV, ve_ones), (sP, s_pre)]
                bw += evA.get(bank, [])
                if b >= 2:
                    bw.append((tP, t_ksum[b - 2]))
                for c in range(NCH):
                    cw = (bw if c == 0 else []) + [p_xn[c]]
                    pr.op("TE", lambda e, c=c, lt=lt, ah=ah, bank=bank: e.matmul(
                        ps[bank][:, 0:256],
                        xn[:, c * NT + lt * P: c * NT + (lt + 1) * P],
                        ah[:, c * 256:(c + 1) * 256],
                        start=(c == 0), stop=False),
                        waits=cw)
                brow = (bkb if is_k else bvb)[0:1, b * 256:(b + 1) * 256]
                t_this = pr.op("TE", lambda e, brow=brow, bank=bank: e.matmul(
                    ps[bank][:, 0:256], ones_r[:, :], brow,
                    start=False, stop=True), inc=tP)
                dst = dstT[:, lt * 256:(lt + 1) * 256]
                if is_k:
                    ew = [(tP, t_this)]
                    if prev_vphi is not None:
                        ew.append((aV, prev_vphi))
                    pr.op("SC", lambda e, bank=bank: e.activation(
                        lnAb[:, 0:256], ps[bank][:, 0:256], AF.Exp),
                        waits=ew, inc=aS)
                    a_r = pr.op("SC", lambda e, bank=bank: e.activation(
                        lnAb[:, 256:512], ps[bank][:, 0:256], AF.Relu),
                        inc=aS)
                    prev_vphi = pr.op("VE", lambda e, dst=dst:
                                      e.scalar_tensor_tensor(
                                          dst, lnAb[:, 0:256], 1.0,
                                          lnAb[:, 256:512], ALU.min, ALU.add),
                                      waits=[(aS, a_r)], inc=aV)
                    evA[bank] = [(aS, a_r)]
                else:
                    a_cp = pr.op("SC", lambda e, dst=dst, bank=bank: e.activation(
                        dst, ps[bank][:, 0:256], AF.Copy),
                        waits=[(tP, t_this)], inc=aS)
                    evA[bank] = [(aS, a_cp)]
            if is_k:
                t_kmm.append(t_this)
                vphi_blk.append(prev_vphi)
            else:
                t_vmm.append(t_this)
                acp_blk.append(aS.n)
        t_last = None
        for sub in range(2):
            hh = 2 * b + sub
            bank = 4 + sub
            bw = [(aV, vphi_blk[b]), (aS, acp_blk[b])]
            if b == 0:
                bw += rel45
            if b >= 1:
                bw.append((aV, t_kvh[b - 1][1][sub]))
            for lt in range(8):
                t_last = pr.op(
                    "TE", lambda e, lt=lt, sub=sub, b=b, bank=bank: e.matmul(
                        ps[bank][:, 0:P],
                        kts[b % 2][:, lt * 256 + sub * P: lt * 256 + (sub + 1) * P],
                        vts[b % 2][:, lt * 256 + sub * P: lt * 256 + (sub + 1) * P],
                        start=(lt == 0), stop=(lt == 7)),
                    waits=bw if lt == 0 else [],
                    inc=(tP if (sub == 1 and lt == 7) else None))
        v_evs = []
        for sub in range(2):
            hh = 2 * b + sub
            sK = sKa if sub == 0 else sKb
            vw = [(tP, t_last)]
            if b >= 1:
                vw.append((sK, s_pack[2 * (b - 1) + sub]))
            v_ev = pr.op("VE", lambda e, sub=sub, bank=4 + sub: e.tensor_copy(
                kvst[:, sub * P:(sub + 1) * P], ps[bank][:, 0:P]),
                waits=vw, inc=aV)
            v_evs.append(v_ev)
            s_pack.append(pr.op("SY", lambda e, hh=hh, sub=sub: e.dma_start(
                out=cc_in.ap()[hh * P:(hh + 1) * P, :],
                in_=kvst[:, sub * P:(sub + 1) * P]),
                waits=[(aV, v_ev)], inc=sK, by=16))
        t_kvh.append((t_last, v_evs))
        ksbank = 6 + b // 2
        kcol = (b % 2) * 256
        t_kb = None
        for lt in range(8):
            kw = [(aV, vphi_blk[b])]
            if b == 0:
                kw.append((aS, a_mrb))
            t_kb = pr.op("TE", lambda e, lt=lt, b=b, ksbank=ksbank, kcol=kcol:
                         e.matmul(
                             ps[ksbank][0:1, kcol:kcol + 256], ones_c[:, :],
                             kts[b % 2][:, lt * 256:(lt + 1) * 256],
                             start=(lt == 0), stop=(lt == 7)),
                         waits=kw if lt == 0 else [],
                         inc=(tP if lt == 7 else None))
        t_ksum.append(t_kb)
    t_ks = t_ksum[-1]
    a_ks = None
    for h in range(2):
        a_ks = pr.op("SC", lambda e, h=h: e.activation(
            r_scr[0:1, h * LH:(h + 1) * LH], ps[6 + h][0:1, :], AF.Copy),
            waits=[(tP, t_ks)], inc=aS)
    s_ks = None
    for hh in range(NHEAD):
        s_ks = pr.op("SY", lambda e, hh=hh: e.dma_start(
            out=cc_in.ap()[DK + hh:DK + hh + 1, :],
            in_=r_scr[0:1, hh * P:(hh + 1) * P]),
            waits=[(aS, a_ks)], inc=sKs, by=16)

    cc_waits = [(sKs, s_ks), (sKa, sKa.n), (sKb, sKb.n)]

    colq5 = [arena[:, 4096 * i:4096 * (i + 1)] for i in range(4)] + [qex[:, :]]
    qsems = [sWa, sQb, sBa, sBb, sQc]
    t_qcol = []
    ev_q = {}
    g_qcol = []
    v_phi_q = []
    for j in range(NCH):
        sW = qsems[j % 5]
        if j < 2:
            waits = [(tP, t_kmm[3])]
        elif j < 4:
            waits = [(tP, t_vmm[3])]
        elif j == 4:
            waits = [(tP, t_ks), (tP, t_kvh[3][0])]
        else:
            waits = [(tP, t_qcol[j - 5])]
        g_qcol.append(pr.op("GP", lambda e, j=j: e.dma_start(
            out=colq5[j % 5], in_=d_wq[j]), waits=waits, inc=sW, by=16))
        if j == 5:
            g_eye = pr.op("GP", lambda e: e.dma_start(
                out=eyeTb[:, :], in_=d_eyeT),
                waits=[(tP, t_kvh[3][0])], inc=sEy, by=16)
        bank0, bank1 = (j % 2) * 2, (j % 2) * 2 + 1
        bw = [(sW, g_qcol[j])]
        for k in (bank0, bank1):
            bw += ev_q.get(k, [])
        tq = None
        for c in range(NCH):
            pr.op("TE", lambda e, j=j, c=c, bank0=bank0: e.matmul(
                ps[bank0][:, :], colq5[j % 5][:, c * P:(c + 1) * P],
                xn[:, c * NT: c * NT + LH],
                start=(c == 0), stop=(c == NCH - 1)),
                waits=bw if c == 0 else [])
            tq = pr.op("TE", lambda e, j=j, c=c, bank1=bank1: e.matmul(
                ps[bank1][:, :], colq5[j % 5][:, c * P:(c + 1) * P],
                xn[:, c * NT + LH: c * NT + NT],
                start=(c == 0), stop=(c == NCH - 1)),
                inc=(tP if c == NCH - 1 else None))
        t_qcol.append(tq)
        for h in range(2):
            bank = bank0 if h == 0 else bank1
            u = h
            s_e = qsc[:, (2 * u) * LH:(2 * u + 1) * LH]
            s_r = qsc[:, (2 * u + 1) * LH:(2 * u + 2) * LH]
            ew = [(tP, t_qcol[j]), (sP, s_pre)]
            if j >= 1:
                ew.append((aV, v_phi_q[2 * (j - 1) + h]))
            a_e = pr.op("SC", lambda e, bank=bank, j=j, s_e=s_e: e.activation(
                s_e, ps[bank][:, :], AF.Exp, bias=bqcol[:, j:j + 1]),
                waits=ew, inc=aS)
            a_r = pr.op("SC", lambda e, bank=bank, j=j, s_r=s_r: e.activation(
                s_r, ps[bank][:, :], AF.Relu, bias=bqcol[:, j:j + 1]), inc=aS)
            dst = qb[:, j * NT + h * LH: j * NT + (h + 1) * LH]
            vp = pr.op("VE", lambda e, dst=dst, s_e=s_e, s_r=s_r:
                       e.scalar_tensor_tensor(
                           dst, s_e, 1.0, s_r, ALU.min, ALU.add),
                       waits=[(aS, a_r)], inc=aV)
            v_phi_q.append(vp)
            ev_q[bank] = [(aS, a_r)]

        if j == 4:
            c_cc = pr.op("GP", lambda e: e.collective_compute(
                "AllReduce", ALU.add,
                replica_groups=[list(range(N_CORES))],
                ins=[cc_in.ap()], outs=[cc_out.ap()]),
                waits=cc_waits, inc=cC, by=1)

        if j == 16:
            g_kvtb = pr.op("GP", lambda e: e.dma_start(
                out=kvtb[:, :],
                in_=cc_out.ap()[0:DK, :].rearrange("(h d) v -> d h v", d=P)),
                waits=[(cC, c_cc)], inc=sX, by=16)
            s_ksb = pr.op("SY", lambda e: e.dma_start(
                out=ksb[:, :],
                in_=cc_out.ap()[DK:DK + NE, :].rearrange("h v -> v h"),
                allow_slow_non_contiguous=True),
                waits=[(cC, c_cc)], inc=sKR, by=16)
            v_k1 = pr.op("VE", lambda e: e.tensor_single_scalar(
                ksb_r[:, :], ksb[:, :], 1e-10, ALU.add),
                waits=[(sKR, s_ksb)], inc=aV)
            pr.op("VE", lambda e: e.reciprocal(ksb_r[:, :], ksb_r[:, :]),
                  waits=[(aV, v_k1)], inc=aV)
            v_ksr = aV.n

        if j == 20:
            t_rt = []
            for h in range(2):
                tr = None
                for c in range(NCH):
                    tr = pr.op("TE", lambda e, h=h, c=c: e.matmul(
                        ps[6][0:8, :] if h == 0 else ps[7][0:8, :],
                        gwT[:, c * NE:(c + 1) * NE],
                        xn[:, c * NT + h * LH: c * NT + h * LH + LH],
                        start=(c == 0), stop=(c == NCH - 1)),
                        waits=([(sP, s_pre), (aS, a_ks), (cC, c_cc)]
                               if c == 0 else []),
                        inc=(tP if c == NCH - 1 else None))
                t_rt.append(tr)
        if j == 21:
            a_exp = []
            for h in range(2):
                a_exp.append(pr.op("SC", lambda e, h=h: e.activation(
                    expL[0:8, h * LH:(h + 1) * LH], ps[6 + h][0:8, :], AF.Exp),
                    waits=[(tP, t_rt[h])], inc=aS))
        if j == 22:
            t_z = []
            for h in range(2):
                t_z.append(pr.op("TE", lambda e, h=h: e.matmul(
                    ps[6 + h][0:1, :], ones_c[0:8, :],
                    expL[0:8, h * LH:(h + 1) * LH],
                    start=True, stop=True), waits=[(aS, a_exp[h])], inc=tP))
        if j == 23:
            a_z = None
            for h in range(2):
                a_z = pr.op("SC", lambda e, h=h: e.activation(
                    r_ex2[0:1, h * LH:(h + 1) * LH], ps[6 + h][0:1, :],
                    AF.Copy), waits=[(tP, t_z[h])], inc=aS)
            a_zl = pr.op("SC", lambda e: e.activation(
                r_mean[0:1, :], r_ex2[0:1, :], AF.Ln),
                waits=[(aS, a_z)], inc=aS)
            a_zi = pr.op("SC", lambda e: e.activation(
                r_mean[0:1, :], r_mean[0:1, :], AF.Exp, scale=-1.0),
                waits=[(aS, a_zl)], inc=aS)
        if j == 24:
            t_zb = []
            for h in range(2):
                t_zb.append(pr.op("TE", lambda e, h=h: e.matmul(
                    ps[6 + h][0:8, :], onesf[:, 1:9],
                    r_mean[0:1, h * LH:(h + 1) * LH],
                    start=True, stop=True), waits=[(aS, a_zi)], inc=tP))
            v_rout8 = None
            for h in range(2):
                v_rout8 = pr.op("VE", lambda e, h=h: e.tensor_tensor(
                    routb[0:8, h * LH:(h + 1) * LH],
                    expL[0:8, h * LH:(h + 1) * LH],
                    ps[6 + h][0:8, :], ALU.mult),
                    waits=[(tP, t_zb[h])], inc=aV)
        if j == 25:
            t_rf = []
            for h in range(2):
                t_rf.append(pr.op("TE", lambda e, h=h: e.matmul(
                    ps[6 + h][:, :], eyeTb[:, :],
                    routb[0:8, h * LH:(h + 1) * LH],
                    start=True, stop=True),
                    waits=[(aV, v_rout8), (sEy, g_eye)], inc=tP))
            v_rout = None
            for h in range(2):
                v_rout = pr.op("VE", lambda e, h=h: e.tensor_copy(
                    routf[:, h * LH:(h + 1) * LH], ps[6 + h][:, :]),
                    waits=[(tP, t_rf[h])], inc=aV)
    v_q_done = aV.n

    bank_ev = {b: list(ev_q.get(b, [])) for b in range(6)}
    bank_ev[6] = [(aV, v_rout)]
    bank_ev[7] = [(aV, v_rout)]

    t_attn_done = None
    for qh in range(NQH):
        kvh = qh // 4
        for h in range(2):
            u = 2 * qh + h
            bank = u % 6
            bw = [(sX, g_kvtb), (aV, v_phi_q[u])] + bank_ev.get(bank, [])
            t_a = pr.op("TE", lambda e, qh=qh, h=h, kvh=kvh, bank=bank: e.matmul(
                ps[bank][:, :], kvtb[:, kvh * P:(kvh + 1) * P],
                qb[:, qh * NT + h * LH: qh * NT + (h + 1) * LH],
                start=True, stop=True), waits=bw, inc=tP)
            t_attn_done = t_a
            dst = qb[:, qh * NT + h * LH: qh * NT + (h + 1) * LH]
            if u % 2 == 0:
                v_a = pr.op("VE", lambda e, dst=dst, bank=bank, kvh=kvh:
                            e.tensor_single_scalar(
                                dst, ps[bank][:, :], ksb_r[:, kvh:kvh + 1],
                                ALU.mult),
                            waits=[(tP, t_a), (aV, v_ksr)], inc=aV)
                bank_ev[bank] = [(aV, v_a)]
            else:
                s_a = pr.op("SC", lambda e, dst=dst, bank=bank, kvh=kvh:
                            e.activation(
                                dst, ps[bank][:, :], AF.Copy,
                                scale=ksb_r[:, kvh:kvh + 1]),
                            waits=[(tP, t_a), (aV, v_ksr)], inc=aS)
                bank_ev[bank] = [(aS, s_a)]
    attn_v_done = aV.n
    attn_s_done = aS.n

    g_ict = []
    for j in range(8):
        g_ict.append(pr.op("GP", lambda e, j=j: e.dma_start(
            out=icst[:, (j % 8) * P:(j % 8 + 1) * P],
            in_=d_ics[:, j * P:(j + 1) * P]), inc=sIq[j % 8], by=16))

    colw = [arena[:, 0:4096], arena[:, 4096:8192]]
    colb = [arena[:, 8192:12288], arena[:, 12288:16384]]
    g_wo = []
    g_wb = []
    g_oct = []
    g_res = []
    a_sgn = []
    for j in range(2):
        sW = sWa if j % 2 == 0 else sWb
        sB = sBa if j % 2 == 0 else sBb
        sR = sRa if j % 2 == 0 else sRb
        sO = sOa if j % 2 == 0 else sOb
        g_oct.append(pr.op("GP", lambda e, j=j: e.dma_start(
            out=ocst[:, (j % 2) * P:(j % 2 + 1) * P],
            in_=d_ocs[:, j * P:(j + 1) * P]),
            waits=[(aV, v_rout)], inc=sO, by=16))
        g_wo.append(pr.op("GP", lambda e, j=j: e.dma_start(
            out=colw[j % 2], in_=d_wo[j]),
            waits=[(tP, t_qcol[-1])], inc=sW, by=16))
        g_wb.append(pr.op("GP", lambda e, j=j: e.dma_start(
            out=colb[j % 2], in_=d_wb[j]),
            waits=[(tP, t_qcol[-1])], inc=sB, by=16))
        a_sgn.append(pr.op("VE", lambda e, j=j: e.tensor_scalar(
            colb[j % 2][:, :], colb[j % 2][:, :], 0.0, 0.5,
            ALU.is_ge, ALU.subtract),
            waits=[(sB, g_wb[j])], inc=aV))
    for j in range(2):
        g_res.append(pr.op("GP", lambda e, j=j: e.dma_start(
            out=residb[:, (j % 2) * NT:(j % 2 + 1) * NT],
            in_=d_xT[:, j * NT:(j + 1) * NT]),
            waits=[(tP, t_attn_done)], inc=(sRa if j % 2 == 0 else sRb), by=16))

    pre_thunks = []
    for h in range(2):
        u = h % 2
        bS = 2 if u == 0 else 5
        bO = 4 if u == 0 else 7
        bw0 = [(sWa, g_wo[0]), (aV, attn_v_done), (aS, attn_s_done)]
        bw0 += bank_ev[bS]
        for c in range(NCH):
            pre_thunks.append((
                "S", h,
                lambda e, h=h, c=c, bS=bS: e.matmul(
                    ps[bS][:, :], colw[0][:, c * P:(c + 1) * P],
                    qb[:, c * NT + h * LH: c * NT + (h + 1) * LH],
                    start=(c == 0), stop=(c == NCH - 1)),
                bw0 if c == 0 else [], c == NCH - 1))
        bwo = [(sOa, g_oct[0]), (aV, v_rout)] + bank_ev[bO]
        pre_thunks.append((
            "O", h,
            lambda e, h=h, bO=bO: e.matmul(
                ps[bO][:, :], ocst[:, 0:P],
                routf[:, h * LH:(h + 1) * LH],
                start=True, stop=True), bwo, True))

    pre_tS = {}
    pre_tO = {}
    pre_iter = iter(pre_thunks)

    def drain_pre(n):
        for _ in range(n):
            item = next(pre_iter, None)
            if item is None:
                return
            kind, h, fn, w, is_end = item
            v = pr.op("TE", fn, waits=w, inc=(tP if is_end else None))
            if is_end:
                (pre_tS if kind == "S" else pre_tO)[h] = v

    xs_banks = [6, 0, 1, 3]
    t_ic = []
    for j in range(NCH):
        sI = sIq[j % 8]
        if j >= 8:
            g_ict.append(pr.op("GP", lambda e, j=j: e.dma_start(
                out=icst[:, (j % 8) * P:(j % 8 + 1) * P],
                in_=d_ics[:, j * P:(j + 1) * P]),
                waits=[(tP, t_ic[j - 8])], inc=sI, by=16))
        for h in range(2):
            bank = xs_banks[(2 * j + h) % 4]
            bw = [(sI, g_ict[j]), (aV, v_rout), (tP, t_qcol[-1])]
            bw += bank_ev.get(bank, [])
            t_i = pr.op("TE", lambda e, j=j, h=h, bank=bank: e.matmul(
                ps[bank][:, :], icst[:, (j % 8) * P:(j % 8 + 1) * P],
                routf[:, h * LH:(h + 1) * LH],
                start=True, stop=True), waits=bw, inc=tP)
            sl = slice(j * NT + h * LH, j * NT + (h + 1) * LH)
            vx = pr.op("VE", lambda e, sl=sl, bank=bank:
                       e.scalar_tensor_tensor(
                           xn[:, sl], ps[bank][:, :], 1.0 / 16, xn[:, sl],
                           ALU.mult, ALU.mult),
                       waits=[(tP, t_i)], inc=aV)
            bank_ev[bank] = [(aV, vx)]
            if j >= 6:
                drain_pre(2)
        t_ic.append(t_i)
    drain_pre(len(pre_thunks))
    xs_v_done = aV.n

    t_scol = []
    t_mcol = []
    t_oc = []
    s_store = []
    ev_fin = {
        ("M", 0): bank_ev[3][0],
        ("M", 1): bank_ev[6][0],
    }
    v_out = {}
    for j in range(NCH):
        sW = sWa if j % 2 == 0 else sWb
        sB = sBa if j % 2 == 0 else sBb
        sR = sRa if j % 2 == 0 else sRb
        sO = sOa if j % 2 == 0 else sOb
        sS = sSa if j % 2 == 0 else sSb
        if j >= 2:
            g_wo.append(pr.op("GP", lambda e, j=j: e.dma_start(
                out=colw[j % 2], in_=d_wo[j]),
                waits=[(tP, t_scol[j - 2])], inc=sW, by=16))
            g_wb.append(pr.op("GP", lambda e, j=j: e.dma_start(
                out=colb[j % 2], in_=d_wb[j]),
                waits=[(tP, t_mcol[j - 2])], inc=sB, by=16))
            a_sgn.append(pr.op("VE", lambda e, j=j: e.tensor_scalar(
                colb[j % 2][:, :], colb[j % 2][:, :], 0.0, 0.5,
                ALU.is_ge, ALU.subtract),
                waits=[(sB, g_wb[j])], inc=aV))
            g_res.append(pr.op("GP", lambda e, j=j: e.dma_start(
                out=residb[:, (j % 2) * NT:(j % 2 + 1) * NT],
                in_=d_xT[:, j * NT:(j + 1) * NT]),
                waits=[(aV, v_out[(j - 2, 1)])], inc=sR, by=16))
            g_oct.append(pr.op("GP", lambda e, j=j: e.dma_start(
                out=ocst[:, (j % 2) * P:(j % 2 + 1) * P],
                in_=d_ocs[:, j * P:(j + 1) * P]),
                waits=[(tP, t_oc[j - 2])], inc=sO, by=16))
        for h in range(2):
            u = (2 * j + h) % 2
            bS, bM, bO = (2, 3, 4) if u == 0 else (5, 6, 7)
            s_os = scr[:, (3 * u) * LH:(3 * u + 1) * LH]
            s_P = scr[:, (3 * u + 1) * LH:(3 * u + 2) * LH]
            s_g = scr[:, (3 * u + 2) * LH:(3 * u + 3) * LH]
            if j == 0:
                tS = pre_tS[h]
                tO = pre_tO[h]
            else:
                bw = [(sW, g_wo[j]), (aV, attn_v_done), (aS, attn_s_done)]
                if ("S", u) in ev_fin:
                    bw.append(ev_fin[("S", u)])
                tS = None
                for c in range(NCH):
                    tS = pr.op("TE", lambda e, j=j, h=h, c=c, bS=bS: e.matmul(
                        ps[bS][:, :], colw[j % 2][:, c * P:(c + 1) * P],
                        qb[:, c * NT + h * LH: c * NT + (h + 1) * LH],
                        start=(c == 0), stop=(c == NCH - 1)),
                        waits=bw if c == 0 else [],
                        inc=(tP if c == NCH - 1 else None))
                bw = [(sO, g_oct[j]), (aV, v_rout)]
                if ("O", u) in ev_fin:
                    bw.append(ev_fin[("O", u)])
                tO = pr.op("TE", lambda e, j=j, h=h, bO=bO: e.matmul(
                    ps[bO][:, :], ocst[:, (j % 2) * P:(j % 2 + 1) * P],
                    routf[:, h * LH:(h + 1) * LH],
                    start=True, stop=True), waits=bw, inc=tP)
            if h == 1:
                t_scol.append(tS)
                t_oc.append(tO)
            bw = [(aV, a_sgn[j]), (aV, xs_v_done)]
            if ("M", u) in ev_fin:
                bw.append(ev_fin[("M", u)])
            tM = None
            for c in range(NCH):
                tM = pr.op("TE", lambda e, j=j, h=h, c=c, bM=bM: e.matmul(
                    ps[bM][:, :], colb[j % 2][:, c * P:(c + 1) * P],
                    xn[:, c * NT + h * LH: c * NT + (h + 1) * LH],
                    start=(c == 0), stop=(c == NCH - 1)),
                    waits=bw if c == 0 else [],
                    inc=(tP if c == NCH - 1 else None))
            if h == 1:
                t_mcol.append(tM)
            a_os = pr.op("SC", lambda e, bO=bO, s_os=s_os: e.activation(
                s_os, ps[bO][:, :], AF.Copy, scale=1.0 / 16),
                waits=[(tP, tO)], inc=aS)
            v_P = pr.op("VE", lambda e, bM=bM, s_P=s_P, s_os=s_os: e.tensor_tensor(
                s_P, ps[bM][:, :], s_os, ALU.mult),
                waits=[(tP, tM), (aS, a_os)], inc=aV)
            a_g = pr.op("SC", lambda e, j=j, s_P=s_P, s_g=s_g: e.activation(
                s_g, s_P, AF.Silu, bias=bfcol[:, j:j + 1], scale=2.0),
                waits=[(aV, v_P), (sP, s_pre)], inc=aS)
            v_sg = pr.op("VE", lambda e, j=j, bS=bS, s_os=s_os, s_g=s_g:
                         e.scalar_tensor_tensor(
                             s_os, ps[bS][:, :], bocol[:, j:j + 1], s_g,
                             ALU.add, ALU.mult),
                         waits=[(tP, tS), (aS, a_g), (sP, s_pre)], inc=aV)
            ow = [(aV, v_sg), (sR, g_res[j])]
            if j >= 2:
                ow.append((sS, s_store[j - 2]))
            vo = pr.op("VE", lambda e, j=j, h=h, s_os=s_os: e.tensor_tensor(
                outst[:, (j % 2) * NT + h * LH:(j % 2) * NT + (h + 1) * LH],
                s_os, residb[:, (j % 2) * NT + h * LH:(j % 2) * NT + (h + 1) * LH],
                ALU.add), waits=ow, inc=aV)
            v_out[(j, h)] = vo
            ev_fin[("S", u)] = (aV, v_sg)
            ev_fin[("M", u)] = (aV, v_P)
            ev_fin[("O", u)] = (aS, a_os)
        s_store.append(pr.op("SY", lambda e, j=j: e.dma_start(
            out=d_out[j], in_=outst[:, (j % 2) * NT:(j % 2 + 1) * NT]),
            waits=[(aV, v_out[(j, 1)])], inc=sS, by=16))

    pr.op("SY", lambda e: e.nop(), waits=[(sSa, sSa.n), (sSb, sSb.n)])

    from contextlib import ExitStack
    with ExitStack() as ctx:
        for s in sems:
            s.h = ctx.enter_context(nc.semaphore(s.name))
        block = ctx.enter_context(nc.Block())

        def runner(key):
            def cb(eng):
                hw = {}
                for (k, fn, waits, inc, by) in pr.ops:
                    if k != key:
                        continue
                    for (s, v) in waits:
                        if hw.get(s.name, 0) < v:
                            eng.wait_ge(s.h, v)
                            hw[s.name] = v
                    inst = fn(eng)
                    if inc is not None and inst is not None:
                        if by is None:
                            inst.then_inc(inc.h)
                        else:
                            inst.then_inc(inc.h, by)
            return cb

        block.sync(runner("SY"))
        block.scalar(runner("SC"))
        block.vector(runner("VE"))
        block.tensor(runner("TE"))
        block.gpsimd(runner("GP"))

    return nc


_NC_CACHE = []


def kernel(**inputs):
    f = np.float32

    def g(name):
        return np.asarray(inputs[name], dtype=f)

    x = g("L0_attn_hidden_states")[0]
    Wq, bq = g("L0_attn_Wq"), g("L0_attn_bq")
    Wk, bk = g("L0_attn_Wk"), g("L0_attn_bk")
    Wv, bv = g("L0_attn_Wv"), g("L0_attn_bv")
    Wo, bo = g("L0_attn_Wo"), g("L0_attn_bo")
    Wf, bf = g("L0_ffn_weight"), g("L0_ffn_bias")
    gw = g("L0_ffn_gate_w")
    ics = g("L0_ffn_in_channel_scale")
    ocs = g("L0_ffn_out_channel_scale")

    def colt(W):
        O = W.shape[0]
        return np.ascontiguousarray(
            W.T.reshape(NCH, P, O // P, P).transpose(2, 1, 0, 3)
        ).reshape(O // P, P, NCH * P)

    wq_t = colt(Wq)
    wo_t = colt(Wo)
    wb_t = colt(Wf)

    def blkt(W):
        return np.ascontiguousarray(
            W.T.reshape(NCH, P, 4, 256).transpose(2, 1, 0, 3)
        ).reshape(4, P, NCH * 256)

    wk_t = blkt(Wk)
    wv_t = blkt(Wv)
    gw_t = np.ascontiguousarray(
        gw.T.reshape(NCH, P, NE).transpose(1, 0, 2)).reshape(P, NCH * NE)
    bqc = np.ascontiguousarray(bq.reshape(NCH, P).T)
    boc = np.ascontiguousarray(bo.reshape(NCH, P).T)
    bfc = np.ascontiguousarray(bf.reshape(NCH, P).T)
    bk2 = np.ascontiguousarray(bk.reshape(1, DK))
    bv2 = np.ascontiguousarray(bv.reshape(1, DK))
    ics = np.ascontiguousarray(np.tile(ics, (16, 1)))
    ocs = np.ascontiguousarray(np.tile(ocs, (16, 1)))
    eyeT = np.zeros((NE, P), dtype=f)
    for e_ in range(NE):
        eyeT[e_, e_::NE] = 1.0

    xT = np.ascontiguousarray(x.T)
    in_maps = []
    for c in range(N_CORES):
        xc = np.ascontiguousarray(
            xT[:, c * NT:(c + 1) * NT].reshape(NCH, P, NT)
            .transpose(1, 0, 2)).reshape(P, NCH * NT)
        in_maps.append({
            "xT": xc, "wq": wq_t, "wo": wo_t, "wb": wb_t,
            "wk": wk_t, "wv": wv_t, "gw": gw_t, "ics": ics, "ocs": ocs,
            "bqc": bqc, "boc": boc, "bfc": bfc, "bk": bk2, "bv": bv2,
            "eyeT": eyeT,
        })

    if not _NC_CACHE:
        _NC_CACHE.append(build_program())
    nc = _NC_CACHE[0]
    res = run_bass_kernel_spmd(nc, in_maps, core_ids=list(range(N_CORES)))

    out = np.empty((8192, D), dtype=f)
    for c in range(N_CORES):
        oc = np.asarray(res.results[c]["outT"], dtype=f)
        out[c * NT:(c + 1) * NT, :] = oc.reshape(D, NT).T
    return out.reshape(1, 8192, D)


# version 26
# speedup vs baseline: 1.0909x; 1.0055x over previous
import numpy as np
import concourse.bass as bass
import concourse.mybir as mybir
from concourse.bass_utils import run_bass_kernel_spmd

F32 = mybir.dt.float32
BF16 = mybir.dt.bfloat16
AF = mybir.ActivationFunctionType
ALU = mybir.AluOpType

P = 128
NT = 1024
LH = 512
D = 4096
NCH = 32
DK = 1024
NHEAD = 8
NQH = 32
NE = 8
N_CORES = 8

SBUF_BYTES = 229376


class S:
    __slots__ = ("h", "n", "name")

    def __init__(self, h, name):
        self.h = h
        self.n = 0
        self.name = name


class Prog:
    def __init__(self):
        self.ops = []

    def op(self, eng, fn, waits=(), inc=None, by=1):
        for (s, v) in waits:
            assert v <= s.n, f"wait on future value {s.name} {v} > {s.n}"
        self.ops.append((eng, fn, [(s, v) for (s, v) in waits if v > 0], inc, by))
        if inc is not None:
            inc.n += (by if by is not None else 1)
            return inc.n
        return None


def build_program():
    nc = bass.Bass("TRN2", target_bir_lowering=False)

    d_xT = nc.dram_tensor("xT", [P, NCH * NT], F32, kind="ExternalInput").ap()
    d_wq = nc.dram_tensor("wq", [NCH, P, D], F32, kind="ExternalInput").ap()
    d_wo = nc.dram_tensor("wo", [NCH, P, D], F32, kind="ExternalInput").ap()
    d_wb = nc.dram_tensor("wb", [NCH, P, D], F32, kind="ExternalInput").ap()
    d_wk = nc.dram_tensor("wk", [4, P, 32 * 256], F32, kind="ExternalInput").ap()
    d_wv = nc.dram_tensor("wv", [4, P, 32 * 256], F32, kind="ExternalInput").ap()
    d_gw = nc.dram_tensor("gw", [P, NCH * NE], F32, kind="ExternalInput").ap()
    d_ics = nc.dram_tensor("ics", [P, D], F32, kind="ExternalInput").ap()
    d_ocs = nc.dram_tensor("ocs", [P, D], F32, kind="ExternalInput").ap()
    d_eyeT = nc.dram_tensor("eyeT", [NE, P], F32, kind="ExternalInput").ap()
    d_bqc = nc.dram_tensor("bqc", [P, NCH], F32, kind="ExternalInput").ap()
    d_boc = nc.dram_tensor("boc", [P, NCH], F32, kind="ExternalInput").ap()
    d_bfc = nc.dram_tensor("bfc", [P, NCH], F32, kind="ExternalInput").ap()
    d_bk = nc.dram_tensor("bk", [1, DK], F32, kind="ExternalInput").ap()
    d_bv = nc.dram_tensor("bv", [1, DK], F32, kind="ExternalInput").ap()
    d_out = nc.dram_tensor("outT", [NCH, P, NT], F32, kind="ExternalOutput").ap()
    cc_in = nc.dram_tensor("cc_in", [DK + NE, P], F32, kind="Internal")
    cc_out = nc.dram_tensor("cc_out", [DK + NE, P], F32, kind="Internal")

    resv = nc.alloc_sbuf_tensor("resv", [P, 212096], mybir.dt.uint8)
    BASE = (nc.lookup_mloc(resv).addr + 127) & ~127

    def at(name, shape, dtype, rel):
        off = BASE + rel
        end = off + shape[1] * (2 if dtype == BF16 else 4)
        assert end <= SBUF_BYTES, (name, off, end)
        return nc.alloc_sbuf_tensor_at(name, shape, dtype, offset=off)

    xn = at("xn", [P, NCH * NT], BF16, 0)
    qb = at("qb", [P, NCH * NT], BF16, 65536)
    arena = at("arena", [P, 16384], BF16, 131072)
    ktm = at("ktm", [P, 2 * 8 * 256], BF16, 163840)
    vtm = at("vtm", [P, 2 * 8 * 256], BF16, 172032)
    residb = at("residb", [P, 2 * NT], BF16, 163840)
    outst = at("outst", [P, 2 * NT], F32, 167936)
    scr = at("scr", [P, 6 * LH], F32, 176128)
    qex = at("qex", [P, 4096], BF16, 163840)
    kvtb = at("kvtb", [P, NHEAD * P], BF16, 182272)
    rstd_sb = at("rstd_sb", [P, NT], BF16, 176128)
    vA = at("vA", [P, NT], BF16, 178176)
    vB = at("vB", [P, NT], BF16, 180224)
    pA = at("pA", [P, NT], BF16, 182272)
    pB = at("pB", [P, NT], BF16, 184320)
    lnAb = at("lnAb", [P, LH], BF16, 186368)
    qsc = at("qsc", [P, 4 * LH], BF16, 176128)
    r_mean = at("r_mean", [1, NT], F32, 188416)
    r_ex2 = at("r_ex2", [1, NT], F32, 192512)
    r_scr = at("r_scr", [1, NT], F32, 196608)
    routb = at("routb", [NE, NT], BF16, 200704)
    bkb = at("bkb", [1, DK], BF16, 202752)
    bvb = at("bvb", [1, DK], BF16, 204800)
    gwT = at("gwT", [P, NCH * NE], BF16, 206848)
    kvst = at("kvst", [P, 2 * P], F32, 207360)
    kvscr = at("kvscr", [P, 256], F32, 208384)
    xsqb = at("xsqb", [P, NT], BF16, 207360)
    mrsb = at("mrsb", [P, NT], BF16, 207360)
    expL = at("expL", [NE, NT], BF16, 207360)
    xsqb2 = at("xsqb2", [P, NT], BF16, 176128)
    ones_r = at("ones_r", [1, P], BF16, 209408)
    onesf = at("onesf", [1, 129], F32, 209664)
    ones_c = at("ones_c", [P, 1], BF16, 210208)
    bqcol = at("bqcol", [P, NCH], F32, 210240)
    bocol = at("bocol", [P, NCH], F32, 210368)
    bfcol = at("bfcol", [P, NCH], F32, 210496)
    ksb = at("ksb", [P, NHEAD], F32, 210624)
    ksb_r = at("ksb_r", [P, NHEAD], F32, 210656)
    epsb = at("epsb", [1, 1], F32, 210688)
    icst = at("icst", [P, 8 * P], BF16, 210720)
    ocst = at("ocst", [P, 2 * P], BF16, 208384)
    eyeTb = at("eyeTb", [NE, P], BF16, 172032)
    routf = at("routf", [P, NT], BF16, 202752)

    ps = [nc.alloc_psum_tensor(f"ps{i}", [P, LH], F32) for i in range(8)]

    pr = Prog()
    sems = []

    def mk(name):
        s = S(None, name)
        sems.append(s)
        return s

    sP = mk("sP")
    sX = mk("sX")
    sX1 = mk("sX1")
    sX2 = mk("sX2")
    sX3 = mk("sX3")
    sX4 = mk("sX4")
    sX5 = mk("sX5")
    sX6 = mk("sX6")
    sX7 = mk("sX7")
    aP = mk("aP")
    sWa = mk("sWa")
    sWb = mk("sWb")
    sBa = mk("sBa")
    sBb = mk("sBb")
    sIq = [mk(f"sI{k}") for k in range(8)]
    sOa = mk("sOa")
    sOb = mk("sOb")
    sKa = mk("sKa")
    sKb = mk("sKb")
    sKs = mk("sKs")
    sKR = mk("sKR")
    sRa = mk("sRa")
    sRb = mk("sRb")
    sSa = mk("sSa")
    sSb = mk("sSb")
    sQb = mk("sQb")
    sQc = mk("sQc")
    sEy = mk("sEy")
    tP = mk("tP")
    aS = mk("aS")
    aV = mk("aV")
    cC = mk("cC")

    ve_ones = pr.op("VE", lambda e: e.memset(ones_r[:, :], 1.0), inc=aV)
    pr.op("VE", lambda e: e.memset(onesf[:, :], 1.0), inc=aV)
    pr.op("VE", lambda e: e.memset(ones_c[:, :], 1.0), inc=aV)
    ve_ones = pr.op("VE", lambda e: e.memset(epsb[:, :], 1e-5), inc=aV)

    sXq2 = [sX, sX1]
    QW = 2 * NT
    g_x16 = []
    for qq in range(16):
        g_x16.append(pr.op("GP", lambda e, qq=qq: e.dma_start(
            out=xn[:, qq * QW:(qq + 1) * QW],
            in_=d_xT[:, qq * QW:(qq + 1) * QW]), inc=sXq2[qq % 2], by=16))

    pr.op("GP", lambda e: e.dma_start(out=gwT[:, :], in_=d_gw), inc=sP, by=16)
    pr.op("GP", lambda e: e.dma_start(out=bkb[:, :], in_=d_bk), inc=sP, by=16)
    pr.op("GP", lambda e: e.dma_start(out=bvb[:, :], in_=d_bv), inc=sP, by=16)
    pr.op("GP", lambda e: e.dma_start(out=bqcol[:, :], in_=d_bqc), inc=sP, by=16)
    pr.op("GP", lambda e: e.dma_start(out=bocol[:, :], in_=d_boc), inc=sP, by=16)
    s_pre = pr.op("GP", lambda e: e.dma_start(out=bfcol[:, :], in_=d_bfc),
                  inc=sP, by=16)

    t_stats = []
    a_sq = []
    for c in range(NCH):
        xq = xsqb if c % 2 == 0 else xsqb2
        waits = [(sXq2[c // 2 % 2], g_x16[c // 2])]
        if c >= 2:
            waits.append((tP, t_stats[c - 2]))
        a_sq.append(pr.op("SC", lambda e, c=c, xq=xq: e.activation(
            xq[:, :], xn[:, c * NT:(c + 1) * NT], AF.Square), waits=waits, inc=aS))
        for h in range(2):
            pr.op("TE", lambda e, c=c, h=h: e.matmul(
                ps[h][0:1, :], ones_c[:, :],
                xn[:, c * NT + h * LH: c * NT + (h + 1) * LH],
                start=(c == 0), stop=(c == NCH - 1)),
                waits=[(sXq2[c // 2 % 2], g_x16[c // 2]), (aV, ve_ones)])
        pr.op("TE", lambda e, c=c, xq=xq: e.matmul(
            ps[2][0:1, :], ones_c[:, :], xq[:, 0:LH],
            start=(c == 0), stop=(c == NCH - 1)),
            waits=[(aS, a_sq[c])])
        t_stats.append(pr.op("TE", lambda e, c=c, xq=xq: e.matmul(
            ps[3][0:1, :], ones_c[:, :], xq[:, LH:NT],
            start=(c == 0), stop=(c == NCH - 1)), inc=tP))

    a_mean = None
    for h in range(2):
        a_mean = pr.op("SC", lambda e, h=h: e.activation(
            r_mean[0:1, h * LH:(h + 1) * LH], ps[h][0:1, :], AF.Copy,
            scale=1.0 / D), waits=[(tP, t_stats[-1])], inc=aS)
        a_mean = pr.op("SC", lambda e, h=h: e.activation(
            r_ex2[0:1, h * LH:(h + 1) * LH], ps[2 + h][0:1, :], AF.Copy,
            scale=1.0 / D), inc=aS)
    v_ms = pr.op("VE", lambda e: e.tensor_tensor(
        r_scr[0:1, :], r_mean[0:1, :], r_mean[0:1, :], ALU.mult),
        waits=[(aS, a_mean)], inc=aV)
    v_var = pr.op("VE", lambda e: e.tensor_tensor(
        r_scr[0:1, :], r_ex2[0:1, :], r_scr[0:1, :], ALU.subtract),
        waits=[(aV, v_ms)], inc=aV)
    a_lnv = pr.op("SC", lambda e: e.activation(
        r_ex2[0:1, :], r_scr[0:1, :], AF.Ln, bias=epsb[0:1, :]),
        waits=[(aV, v_var), (aV, ve_ones)], inc=aS)
    a_rstd = pr.op("SC", lambda e: e.activation(
        r_scr[0:1, :], r_ex2[0:1, :], AF.Exp, scale=-0.5),
        waits=[(aS, a_lnv)], inc=aS)
    v_mr = pr.op("VE", lambda e: e.tensor_tensor(
        r_mean[0:1, :], r_mean[0:1, :], r_scr[0:1, :], ALU.mult),
        waits=[(aS, a_rstd)], inc=aV)
    t_bc = None
    for h in range(2):
        pr.op("TE", lambda e, h=h: e.matmul(
            ps[4 + h][:, :], onesf[:, 1:129], r_scr[0:1, h * LH:(h + 1) * LH],
            start=True, stop=True), waits=[(aV, v_mr)])
        t_bc = pr.op("TE", lambda e, h=h: e.matmul(
            ps[6 + h][:, :], onesf[:, 1:129], r_mean[0:1, h * LH:(h + 1) * LH],
            start=True, stop=True), inc=tP)
    a_rsb = None
    for h in range(2):
        a_rsb = pr.op("SC", lambda e, h=h: e.activation(
            rstd_sb[:, h * LH:(h + 1) * LH], ps[4 + h][:, :], AF.Copy),
            waits=[(tP, t_bc)], inc=aS)
    a_mrb = None
    for h in range(2):
        a_mrb = pr.op("SC", lambda e, h=h: e.activation(
            mrsb[:, h * LH:(h + 1) * LH], ps[6 + h][:, :], AF.Copy),
            waits=[(tP, t_bc)], inc=aS)

    arena_h = [arena[:, 0:8192], arena[:, 8192:16384]]
    g_wkb = [pr.op("GP", lambda e: e.dma_start(
        out=arena_h[0], in_=d_wk[0]), inc=sWa, by=16)]
    g_wvb = [pr.op("GP", lambda e: e.dma_start(
        out=arena_h[1], in_=d_wv[0]), inc=sWb, by=16)]

    p_xn = []
    eng_cnt = {"VE": 0, "GP": 0}
    last_buf = {}
    for c in range(NCH):
        on_ve = True
        eng, sem = ("VE", aV) if on_ve else ("GP", aP)
        k = eng_cnt[eng]
        eng_cnt[eng] += 1
        lsl = ([vA, vB] if on_ve else [pA, pB])[k % 2]
        cs = slice(c * NT, (c + 1) * NT)
        mw = [(aS, a_mrb)]
        if (eng, k % 2) in last_buf:
            mw.append((sem, last_buf[(eng, k % 2)]))
        vmul = pr.op(eng, lambda e, cs=cs, lsl=lsl: e.tensor_tensor(
            lsl[:, :], xn[:, cs], rstd_sb[:, :], ALU.mult),
            waits=mw, inc=sem)
        lastp = pr.op(eng, lambda e, cs=cs, lsl=lsl: e.tensor_tensor(
            xn[:, cs], lsl[:, :], mrsb[:, :], ALU.subtract),
            waits=[(sem, vmul)], inc=sem)
        last_buf[(eng, k % 2)] = lastp
        p_xn.append((sem, lastp))
    rel45 = [(aS, a_rsb)]

    kts = [ktm[:, 0:2048], ktm[:, 2048:4096]]
    vts = [vtm[:, 0:2048], vtm[:, 2048:4096]]
    prev_vphi = None
    t_kmm = []
    t_vmm = []
    t_kvh = []
    t_ksum = []
    s_pack = []
    evA = {}
    vphi_blk = []
    acp_blk = []
    for b in range(4):
        if b >= 1:
            g_wkb.append(pr.op("GP", lambda e, b=b: e.dma_start(
                out=arena_h[0], in_=d_wk[b]),
                waits=[(tP, t_kmm[b - 1])], inc=sWa, by=16))
            g_wvb.append(pr.op("GP", lambda e, b=b: e.dma_start(
                out=arena_h[1], in_=d_wv[b]),
                waits=[(tP, t_vmm[b - 1])], inc=sWb, by=16))
        for is_k in (True, False):
            gld = g_wkb[b] if is_k else g_wvb[b]
            sld = sWa if is_k else sWb
            ah = arena_h[0] if is_k else arena_h[1]
            dstT = (kts if is_k else vts)[b % 2]
            if b == 0 and is_k:
                bw0 = [(sld, gld), (aV, ve_ones), (sP, s_pre)]
                for c in range(NCH):
                    cw = (bw0 if c == 0 else []) + [p_xn[c]]
                    for lt in range(8):
                        pr.op("TE", lambda e, c=c, lt=lt, ah=ah: e.matmul(
                            ps[lt][:, 0:256],
                            xn[:, c * NT + lt * P: c * NT + (lt + 1) * P],
                            ah[:, c * 256:(c + 1) * 256],
                            start=(c == 0), stop=False),
                            waits=cw if lt == 0 else [])
                for lt in range(8):
                    brow = bkb[0:1, 0:256]
                    t_this = pr.op("TE", lambda e, brow=brow, lt=lt: e.matmul(
                        ps[lt][:, 0:256], ones_r[:, :], brow,
                        start=False, stop=True), inc=tP)
                    dst = dstT[:, lt * 256:(lt + 1) * 256]
                    ew = [(tP, t_this)]
                    if prev_vphi is not None:
                        ew.append((aV, prev_vphi))
                    pr.op("SC", lambda e, lt=lt: e.activation(
                        lnAb[:, 0:256], ps[lt][:, 0:256], AF.Exp),
                        waits=ew, inc=aS)
                    a_r = pr.op("SC", lambda e, lt=lt: e.activation(
                        lnAb[:, 256:512], ps[lt][:, 0:256], AF.Relu),
                        inc=aS)
                    prev_vphi = pr.op("VE", lambda e, dst=dst:
                                      e.scalar_tensor_tensor(
                                          dst, lnAb[:, 0:256], 1.0,
                                          lnAb[:, 256:512], ALU.min, ALU.add),
                                      waits=[(aS, a_r)], inc=aV)
                    evA[lt % 4] = [(aS, a_r)]
                t_kmm.append(t_this)
                vphi_blk.append(prev_vphi)
                continue
            for lt in range(8):
                bank = lt % 4
                bw = [(sld, gld), (aV, ve_ones), (sP, s_pre)]
                bw += evA.get(bank, [])
                if b >= 2:
                    bw.append((tP, t_ksum[b - 2]))
                for c in range(NCH):
                    cw = (bw if c == 0 else []) + [p_xn[c]]
                    pr.op("TE", lambda e, c=c, lt=lt, ah=ah, bank=bank: e.matmul(
                        ps[bank][:, 0:256],
                        xn[:, c * NT + lt * P: c * NT + (lt + 1) * P],
                        ah[:, c * 256:(c + 1) * 256],
                        start=(c == 0), stop=False),
                        waits=cw)
                brow = (bkb if is_k else bvb)[0:1, b * 256:(b + 1) * 256]
                t_this = pr.op("TE", lambda e, brow=brow, bank=bank: e.matmul(
                    ps[bank][:, 0:256], ones_r[:, :], brow,
                    start=False, stop=True), inc=tP)
                dst = dstT[:, lt * 256:(lt + 1) * 256]
                if is_k:
                    ew = [(tP, t_this)]
                    if prev_vphi is not None:
                        ew.append((aV, prev_vphi))
                    pr.op("SC", lambda e, bank=bank: e.activation(
                        lnAb[:, 0:256], ps[bank][:, 0:256], AF.Exp),
                        waits=ew, inc=aS)
                    a_r = pr.op("SC", lambda e, bank=bank: e.activation(
                        lnAb[:, 256:512], ps[bank][:, 0:256], AF.Relu),
                        inc=aS)
                    prev_vphi = pr.op("VE", lambda e, dst=dst:
                                      e.scalar_tensor_tensor(
                                          dst, lnAb[:, 0:256], 1.0,
                                          lnAb[:, 256:512], ALU.min, ALU.add),
                                      waits=[(aS, a_r)], inc=aV)
                    evA[bank] = [(aS, a_r)]
                else:
                    a_cp = pr.op("SC", lambda e, dst=dst, bank=bank: e.activation(
                        dst, ps[bank][:, 0:256], AF.Copy),
                        waits=[(tP, t_this)], inc=aS)
                    evA[bank] = [(aS, a_cp)]
            if is_k:
                t_kmm.append(t_this)
                vphi_blk.append(prev_vphi)
            else:
                t_vmm.append(t_this)
                acp_blk.append(aS.n)
        t_last = None
        for sub in range(2):
            hh = 2 * b + sub
            bank = 4 + sub
            bw = [(aV, vphi_blk[b]), (aS, acp_blk[b])]
            if b == 0:
                bw += rel45
            if b >= 1:
                bw.append((aV, t_kvh[b - 1][1][sub]))
            for lt in range(8):
                t_last = pr.op(
                    "TE", lambda e, lt=lt, sub=sub, b=b, bank=bank: e.matmul(
                        ps[bank][:, 0:P],
                        kts[b % 2][:, lt * 256 + sub * P: lt * 256 + (sub + 1) * P],
                        vts[b % 2][:, lt * 256 + sub * P: lt * 256 + (sub + 1) * P],
                        start=(lt == 0), stop=(lt == 7)),
                    waits=bw if lt == 0 else [],
                    inc=(tP if (sub == 1 and lt == 7) else None))
        v_evs = []
        for sub in range(2):
            hh = 2 * b + sub
            sK = sKa if sub == 0 else sKb
            vw = [(tP, t_last)]
            if b >= 1:
                vw.append((sK, s_pack[2 * (b - 1) + sub]))
            v_ev = pr.op("VE", lambda e, sub=sub, bank=4 + sub: e.tensor_copy(
                kvst[:, sub * P:(sub + 1) * P], ps[bank][:, 0:P]),
                waits=vw, inc=aV)
            v_evs.append(v_ev)
            s_pack.append(pr.op("SY", lambda e, hh=hh, sub=sub: e.dma_start(
                out=cc_in.ap()[hh * P:(hh + 1) * P, :],
                in_=kvst[:, sub * P:(sub + 1) * P]),
                waits=[(aV, v_ev)], inc=sK, by=16))
        t_kvh.append((t_last, v_evs))
        ksbank = 6 + b // 2
        kcol = (b % 2) * 256
        t_kb = None
        for lt in range(8):
            kw = [(aV, vphi_blk[b])]
            if b == 0:
                kw.append((aS, a_mrb))
            t_kb = pr.op("TE", lambda e, lt=lt, b=b, ksbank=ksbank, kcol=kcol:
                         e.matmul(
                             ps[ksbank][0:1, kcol:kcol + 256], ones_c[:, :],
                             kts[b % 2][:, lt * 256:(lt + 1) * 256],
                             start=(lt == 0), stop=(lt == 7)),
                         waits=kw if lt == 0 else [],
                         inc=(tP if lt == 7 else None))
        t_ksum.append(t_kb)
    t_ks = t_ksum[-1]
    a_ks = None
    for h in range(2):
        a_ks = pr.op("SC", lambda e, h=h: e.activation(
            r_scr[0:1, h * LH:(h + 1) * LH], ps[6 + h][0:1, :], AF.Copy),
            waits=[(tP, t_ks)], inc=aS)
    s_ks = None
    for hh in range(NHEAD):
        s_ks = pr.op("SY", lambda e, hh=hh: e.dma_start(
            out=cc_in.ap()[DK + hh:DK + hh + 1, :],
            in_=r_scr[0:1, hh * P:(hh + 1) * P]),
            waits=[(aS, a_ks)], inc=sKs, by=16)

    cc_waits = [(sKs, s_ks), (sKa, sKa.n), (sKb, sKb.n)]

    colq5 = [arena[:, 4096 * i:4096 * (i + 1)] for i in range(4)] + [qex[:, :]]
    qsems = [sWa, sQb, sBa, sBb, sQc]
    t_qcol = []
    ev_q = {}
    g_qcol = []
    v_phi_q = []
    for j in range(NCH):
        sW = qsems[j % 5]
        if j < 2:
            waits = [(tP, t_kmm[3])]
        elif j < 4:
            waits = [(tP, t_vmm[3])]
        elif j == 4:
            waits = [(tP, t_ks), (tP, t_kvh[3][0])]
        else:
            waits = [(tP, t_qcol[j - 5])]
        g_qcol.append(pr.op("GP", lambda e, j=j: e.dma_start(
            out=colq5[j % 5], in_=d_wq[j]), waits=waits, inc=sW, by=16))
        if j == 5:
            g_eye = pr.op("GP", lambda e: e.dma_start(
                out=eyeTb[:, :], in_=d_eyeT),
                waits=[(tP, t_kvh[3][0])], inc=sEy, by=16)
        bank0, bank1 = (j % 2) * 2, (j % 2) * 2 + 1
        bw = [(sW, g_qcol[j])]
        for k in (bank0, bank1):
            bw += ev_q.get(k, [])
        tq = None
        for c in range(NCH):
            pr.op("TE", lambda e, j=j, c=c, bank0=bank0: e.matmul(
                ps[bank0][:, :], colq5[j % 5][:, c * P:(c + 1) * P],
                xn[:, c * NT: c * NT + LH],
                start=(c == 0), stop=(c == NCH - 1)),
                waits=bw if c == 0 else [])
            tq = pr.op("TE", lambda e, j=j, c=c, bank1=bank1: e.matmul(
                ps[bank1][:, :], colq5[j % 5][:, c * P:(c + 1) * P],
                xn[:, c * NT + LH: c * NT + NT],
                start=(c == 0), stop=(c == NCH - 1)),
                inc=(tP if c == NCH - 1 else None))
        t_qcol.append(tq)
        for h in range(2):
            bank = bank0 if h == 0 else bank1
            u = h
            s_e = qsc[:, (2 * u) * LH:(2 * u + 1) * LH]
            s_r = qsc[:, (2 * u + 1) * LH:(2 * u + 2) * LH]
            ew = [(tP, t_qcol[j]), (sP, s_pre)]
            if j >= 1:
                ew.append((aV, v_phi_q[2 * (j - 1) + h]))
            a_e = pr.op("SC", lambda e, bank=bank, j=j, s_e=s_e: e.activation(
                s_e, ps[bank][:, :], AF.Exp, bias=bqcol[:, j:j + 1]),
                waits=ew, inc=aS)
            a_r = pr.op("SC", lambda e, bank=bank, j=j, s_r=s_r: e.activation(
                s_r, ps[bank][:, :], AF.Relu, bias=bqcol[:, j:j + 1]), inc=aS)
            dst = qb[:, j * NT + h * LH: j * NT + (h + 1) * LH]
            vp = pr.op("VE", lambda e, dst=dst, s_e=s_e, s_r=s_r:
                       e.scalar_tensor_tensor(
                           dst, s_e, 1.0, s_r, ALU.min, ALU.add),
                       waits=[(aS, a_r)], inc=aV)
            v_phi_q.append(vp)
            ev_q[bank] = [(aS, a_r)]

        if j == 4:
            c_cc = pr.op("GP", lambda e: e.collective_compute(
                "AllReduce", ALU.add,
                replica_groups=[list(range(N_CORES))],
                ins=[cc_in.ap()], outs=[cc_out.ap()]),
                waits=cc_waits, inc=cC, by=1)

        if j == 16:
            g_kvtb = pr.op("GP", lambda e: e.dma_start(
                out=kvtb[:, :],
                in_=cc_out.ap()[0:DK, :].rearrange("(h d) v -> d h v", d=P)),
                waits=[(cC, c_cc)], inc=sX, by=16)
            s_ksb = pr.op("SY", lambda e: e.dma_start(
                out=ksb[:, :],
                in_=cc_out.ap()[DK:DK + NE, :].rearrange("h v -> v h"),
                allow_slow_non_contiguous=True),
                waits=[(cC, c_cc)], inc=sKR, by=16)
            v_k1 = pr.op("VE", lambda e: e.tensor_single_scalar(
                ksb_r[:, :], ksb[:, :], 1e-10, ALU.add),
                waits=[(sKR, s_ksb)], inc=aV)
            pr.op("VE", lambda e: e.reciprocal(ksb_r[:, :], ksb_r[:, :]),
                  waits=[(aV, v_k1)], inc=aV)
            v_ksr = aV.n

        if j == 20:
            t_rt = []
            for h in range(2):
                tr = None
                for c in range(NCH):
                    tr = pr.op("TE", lambda e, h=h, c=c: e.matmul(
                        ps[6][0:8, :] if h == 0 else ps[7][0:8, :],
                        gwT[:, c * NE:(c + 1) * NE],
                        xn[:, c * NT + h * LH: c * NT + h * LH + LH],
                        start=(c == 0), stop=(c == NCH - 1)),
                        waits=([(sP, s_pre), (aS, a_ks), (cC, c_cc)]
                               if c == 0 else []),
                        inc=(tP if c == NCH - 1 else None))
                t_rt.append(tr)
        if j == 21:
            a_exp = []
            for h in range(2):
                a_exp.append(pr.op("SC", lambda e, h=h: e.activation(
                    expL[0:8, h * LH:(h + 1) * LH], ps[6 + h][0:8, :], AF.Exp),
                    waits=[(tP, t_rt[h])], inc=aS))
        if j == 22:
            t_z = []
            for h in range(2):
                t_z.append(pr.op("TE", lambda e, h=h: e.matmul(
                    ps[6 + h][0:1, :], ones_c[0:8, :],
                    expL[0:8, h * LH:(h + 1) * LH],
                    start=True, stop=True), waits=[(aS, a_exp[h])], inc=tP))
        if j == 23:
            a_z = None
            for h in range(2):
                a_z = pr.op("SC", lambda e, h=h: e.activation(
                    r_ex2[0:1, h * LH:(h + 1) * LH], ps[6 + h][0:1, :],
                    AF.Copy), waits=[(tP, t_z[h])], inc=aS)
            a_zl = pr.op("SC", lambda e: e.activation(
                r_mean[0:1, :], r_ex2[0:1, :], AF.Ln),
                waits=[(aS, a_z)], inc=aS)
            a_zi = pr.op("SC", lambda e: e.activation(
                r_mean[0:1, :], r_mean[0:1, :], AF.Exp, scale=-1.0),
                waits=[(aS, a_zl)], inc=aS)
        if j == 24:
            t_zb = []
            for h in range(2):
                t_zb.append(pr.op("TE", lambda e, h=h: e.matmul(
                    ps[6 + h][0:8, :], onesf[:, 1:9],
                    r_mean[0:1, h * LH:(h + 1) * LH],
                    start=True, stop=True), waits=[(aS, a_zi)], inc=tP))
            v_rout8 = None
            for h in range(2):
                v_rout8 = pr.op("VE", lambda e, h=h: e.tensor_tensor(
                    routb[0:8, h * LH:(h + 1) * LH],
                    expL[0:8, h * LH:(h + 1) * LH],
                    ps[6 + h][0:8, :], ALU.mult),
                    waits=[(tP, t_zb[h])], inc=aV)
        if j == 25:
            t_rf = []
            for h in range(2):
                t_rf.append(pr.op("TE", lambda e, h=h: e.matmul(
                    ps[6 + h][:, :], eyeTb[:, :],
                    routb[0:8, h * LH:(h + 1) * LH],
                    start=True, stop=True),
                    waits=[(aV, v_rout8), (sEy, g_eye)], inc=tP))
            v_rout = None
            for h in range(2):
                v_rout = pr.op("VE", lambda e, h=h: e.tensor_copy(
                    routf[:, h * LH:(h + 1) * LH], ps[6 + h][:, :]),
                    waits=[(tP, t_rf[h])], inc=aV)
    v_q_done = aV.n

    bank_ev = {b: list(ev_q.get(b, [])) for b in range(6)}
    bank_ev[6] = [(aV, v_rout)]
    bank_ev[7] = [(aV, v_rout)]

    t_attn_done = None
    for qh in range(NQH):
        kvh = qh // 4
        for h in range(2):
            u = 2 * qh + h
            bank = u % 6
            bw = [(sX, g_kvtb), (aV, v_phi_q[u])] + bank_ev.get(bank, [])
            t_a = pr.op("TE", lambda e, qh=qh, h=h, kvh=kvh, bank=bank: e.matmul(
                ps[bank][:, :], kvtb[:, kvh * P:(kvh + 1) * P],
                qb[:, qh * NT + h * LH: qh * NT + (h + 1) * LH],
                start=True, stop=True), waits=bw, inc=tP)
            t_attn_done = t_a
            dst = qb[:, qh * NT + h * LH: qh * NT + (h + 1) * LH]
            if u % 2 == 0:
                v_a = pr.op("VE", lambda e, dst=dst, bank=bank, kvh=kvh:
                            e.tensor_single_scalar(
                                dst, ps[bank][:, :], ksb_r[:, kvh:kvh + 1],
                                ALU.mult),
                            waits=[(tP, t_a), (aV, v_ksr)], inc=aV)
                bank_ev[bank] = [(aV, v_a)]
            else:
                s_a = pr.op("SC", lambda e, dst=dst, bank=bank, kvh=kvh:
                            e.activation(
                                dst, ps[bank][:, :], AF.Copy,
                                scale=ksb_r[:, kvh:kvh + 1]),
                            waits=[(tP, t_a), (aV, v_ksr)], inc=aS)
                bank_ev[bank] = [(aS, s_a)]
    attn_v_done = aV.n
    attn_s_done = aS.n

    g_ict = []
    for j in range(8):
        g_ict.append(pr.op("GP", lambda e, j=j: e.dma_start(
            out=icst[:, (j % 8) * P:(j % 8 + 1) * P],
            in_=d_ics[:, j * P:(j + 1) * P]), inc=sIq[j % 8], by=16))

    colw = [arena[:, 0:4096], arena[:, 4096:8192]]
    colb = [arena[:, 8192:12288], arena[:, 12288:16384]]
    g_wo = []
    g_wb = []
    g_oct = []
    g_res = []
    a_sgn = []
    for j in range(2):
        sW = sWa if j % 2 == 0 else sWb
        sB = sBa if j % 2 == 0 else sBb
        sR = sRa if j % 2 == 0 else sRb
        sO = sOa if j % 2 == 0 else sOb
        g_oct.append(pr.op("GP", lambda e, j=j: e.dma_start(
            out=ocst[:, (j % 2) * P:(j % 2 + 1) * P],
            in_=d_ocs[:, j * P:(j + 1) * P]),
            waits=[(aV, v_rout)], inc=sO, by=16))
        g_wo.append(pr.op("GP", lambda e, j=j: e.dma_start(
            out=colw[j % 2], in_=d_wo[j]),
            waits=[(tP, t_qcol[-1])], inc=sW, by=16))
        g_wb.append(pr.op("GP", lambda e, j=j: e.dma_start(
            out=colb[j % 2], in_=d_wb[j]),
            waits=[(tP, t_qcol[-1])], inc=sB, by=16))
        a_sgn.append(pr.op("VE", lambda e, j=j: e.tensor_scalar(
            colb[j % 2][:, :], colb[j % 2][:, :], 0.0, 0.5,
            ALU.is_ge, ALU.subtract),
            waits=[(sB, g_wb[j])], inc=aV))
    for j in range(2):
        g_res.append(pr.op("GP", lambda e, j=j: e.dma_start(
            out=residb[:, (j % 2) * NT:(j % 2 + 1) * NT],
            in_=d_xT[:, j * NT:(j + 1) * NT]),
            waits=[(tP, t_attn_done)], inc=(sRa if j % 2 == 0 else sRb), by=16))

    pre_thunks = []
    for h in range(2):
        u = h % 2
        bS = 2 if u == 0 else 5
        bO = 4 if u == 0 else 7
        bw0 = [(sWa, g_wo[0]), (aV, attn_v_done), (aS, attn_s_done)]
        bw0 += bank_ev[bS]
        for c in range(NCH):
            pre_thunks.append((
                "S", h,
                lambda e, h=h, c=c, bS=bS: e.matmul(
                    ps[bS][:, :], colw[0][:, c * P:(c + 1) * P],
                    qb[:, c * NT + h * LH: c * NT + (h + 1) * LH],
                    start=(c == 0), stop=(c == NCH - 1)),
                bw0 if c == 0 else [], c == NCH - 1))
        bwo = [(sOa, g_oct[0]), (aV, v_rout)] + bank_ev[bO]
        pre_thunks.append((
            "O", h,
            lambda e, h=h, bO=bO: e.matmul(
                ps[bO][:, :], ocst[:, 0:P],
                routf[:, h * LH:(h + 1) * LH],
                start=True, stop=True), bwo, True))

    pre_tS = {}
    pre_tO = {}
    pre_iter = iter(pre_thunks)

    def drain_pre(n):
        for _ in range(n):
            item = next(pre_iter, None)
            if item is None:
                return
            kind, h, fn, w, is_end = item
            v = pr.op("TE", fn, waits=w, inc=(tP if is_end else None))
            if is_end:
                (pre_tS if kind == "S" else pre_tO)[h] = v

    xs_banks = [6, 0, 1, 3]
    t_ic = []
    for j in range(NCH):
        sI = sIq[j % 8]
        if j >= 8:
            g_ict.append(pr.op("GP", lambda e, j=j: e.dma_start(
                out=icst[:, (j % 8) * P:(j % 8 + 1) * P],
                in_=d_ics[:, j * P:(j + 1) * P]),
                waits=[(tP, t_ic[j - 8])], inc=sI, by=16))
        for h in range(2):
            bank = xs_banks[(2 * j + h) % 4]
            bw = [(sI, g_ict[j]), (aV, v_rout), (tP, t_qcol[-1])]
            bw += bank_ev.get(bank, [])
            t_i = pr.op("TE", lambda e, j=j, h=h, bank=bank: e.matmul(
                ps[bank][:, :], icst[:, (j % 8) * P:(j % 8 + 1) * P],
                routf[:, h * LH:(h + 1) * LH],
                start=True, stop=True), waits=bw, inc=tP)
            sl = slice(j * NT + h * LH, j * NT + (h + 1) * LH)
            vx = pr.op("VE", lambda e, sl=sl, bank=bank:
                       e.scalar_tensor_tensor(
                           xn[:, sl], ps[bank][:, :], 1.0 / 16, xn[:, sl],
                           ALU.mult, ALU.mult),
                       waits=[(tP, t_i)], inc=aV)
            bank_ev[bank] = [(aV, vx)]
            if j >= 6:
                drain_pre(2)
        t_ic.append(t_i)
    drain_pre(len(pre_thunks))
    xs_v_done = aV.n

    t_scol = []
    t_mcol = []
    t_oc = []
    s_store = []
    ev_fin = {
        ("M", 0): bank_ev[3][0],
        ("M", 1): bank_ev[6][0],
    }
    v_out = {}
    for j in range(NCH):
        sW = sWa if j % 2 == 0 else sWb
        sB = sBa if j % 2 == 0 else sBb
        sR = sRa if j % 2 == 0 else sRb
        sO = sOa if j % 2 == 0 else sOb
        sS = sSa if j % 2 == 0 else sSb
        if j >= 2:
            g_wo.append(pr.op("GP", lambda e, j=j: e.dma_start(
                out=colw[j % 2], in_=d_wo[j]),
                waits=[(tP, t_scol[j - 2])], inc=sW, by=16))
            g_wb.append(pr.op("GP", lambda e, j=j: e.dma_start(
                out=colb[j % 2], in_=d_wb[j]),
                waits=[(tP, t_mcol[j - 2])], inc=sB, by=16))
            a_sgn.append(pr.op("VE", lambda e, j=j: e.tensor_scalar(
                colb[j % 2][:, :], colb[j % 2][:, :], 0.0, 0.5,
                ALU.is_ge, ALU.subtract),
                waits=[(sB, g_wb[j])], inc=aV))
            g_res.append(pr.op("GP", lambda e, j=j: e.dma_start(
                out=residb[:, (j % 2) * NT:(j % 2 + 1) * NT],
                in_=d_xT[:, j * NT:(j + 1) * NT]),
                waits=[(aV, v_out[(j - 2, 1)])], inc=sR, by=16))
            g_oct.append(pr.op("GP", lambda e, j=j: e.dma_start(
                out=ocst[:, (j % 2) * P:(j % 2 + 1) * P],
                in_=d_ocs[:, j * P:(j + 1) * P]),
                waits=[(tP, t_oc[j - 2])], inc=sO, by=16))
        for h in range(2):
            u = (2 * j + h) % 2
            bS, bM, bO = (2, 3, 4) if u == 0 else (5, 6, 7)
            s_os = scr[:, (3 * u) * LH:(3 * u + 1) * LH]
            s_P = scr[:, (3 * u + 1) * LH:(3 * u + 2) * LH]
            s_g = scr[:, (3 * u + 2) * LH:(3 * u + 3) * LH]
            if j == 0:
                tS = pre_tS[h]
                tO = pre_tO[h]
            else:
                bw = [(sW, g_wo[j]), (aV, attn_v_done), (aS, attn_s_done)]
                if ("S", u) in ev_fin:
                    bw.append(ev_fin[("S", u)])
                tS = None
                for c in range(NCH):
                    tS = pr.op("TE", lambda e, j=j, h=h, c=c, bS=bS: e.matmul(
                        ps[bS][:, :], colw[j % 2][:, c * P:(c + 1) * P],
                        qb[:, c * NT + h * LH: c * NT + (h + 1) * LH],
                        start=(c == 0), stop=(c == NCH - 1)),
                        waits=bw if c == 0 else [],
                        inc=(tP if c == NCH - 1 else None))
                bw = [(sO, g_oct[j]), (aV, v_rout)]
                if ("O", u) in ev_fin:
                    bw.append(ev_fin[("O", u)])
                tO = pr.op("TE", lambda e, j=j, h=h, bO=bO: e.matmul(
                    ps[bO][:, :], ocst[:, (j % 2) * P:(j % 2 + 1) * P],
                    routf[:, h * LH:(h + 1) * LH],
                    start=True, stop=True), waits=bw, inc=tP)
            if h == 1:
                t_scol.append(tS)
                t_oc.append(tO)
            bw = [(aV, a_sgn[j]), (aV, xs_v_done)]
            if ("M", u) in ev_fin:
                bw.append(ev_fin[("M", u)])
            tM = None
            for c in range(NCH):
                tM = pr.op("TE", lambda e, j=j, h=h, c=c, bM=bM: e.matmul(
                    ps[bM][:, :], colb[j % 2][:, c * P:(c + 1) * P],
                    xn[:, c * NT + h * LH: c * NT + (h + 1) * LH],
                    start=(c == 0), stop=(c == NCH - 1)),
                    waits=bw if c == 0 else [],
                    inc=(tP if c == NCH - 1 else None))
            if h == 1:
                t_mcol.append(tM)
            a_os = pr.op("SC", lambda e, bO=bO, s_os=s_os: e.activation(
                s_os, ps[bO][:, :], AF.Copy, scale=1.0 / 16),
                waits=[(tP, tO)], inc=aS)
            v_P = pr.op("VE", lambda e, bM=bM, s_P=s_P, s_os=s_os: e.tensor_tensor(
                s_P, ps[bM][:, :], s_os, ALU.mult),
                waits=[(tP, tM), (aS, a_os)], inc=aV)
            a_g = pr.op("SC", lambda e, j=j, s_P=s_P, s_g=s_g: e.activation(
                s_g, s_P, AF.Silu, bias=bfcol[:, j:j + 1], scale=2.0),
                waits=[(aV, v_P), (sP, s_pre)], inc=aS)
            v_sg = pr.op("VE", lambda e, j=j, bS=bS, s_os=s_os, s_g=s_g:
                         e.scalar_tensor_tensor(
                             s_os, ps[bS][:, :], bocol[:, j:j + 1], s_g,
                             ALU.add, ALU.mult),
                         waits=[(tP, tS), (aS, a_g), (sP, s_pre)], inc=aV)
            ow = [(aV, v_sg), (sR, g_res[j])]
            if j >= 2:
                ow.append((sS, s_store[j - 2]))
            vo = pr.op("VE", lambda e, j=j, h=h, s_os=s_os: e.tensor_tensor(
                outst[:, (j % 2) * NT + h * LH:(j % 2) * NT + (h + 1) * LH],
                s_os, residb[:, (j % 2) * NT + h * LH:(j % 2) * NT + (h + 1) * LH],
                ALU.add), waits=ow, inc=aV)
            v_out[(j, h)] = vo
            ev_fin[("S", u)] = (aV, v_sg)
            ev_fin[("M", u)] = (aV, v_P)
            ev_fin[("O", u)] = (aS, a_os)
        s_store.append(pr.op("SY", lambda e, j=j: e.dma_start(
            out=d_out[j], in_=outst[:, (j % 2) * NT:(j % 2 + 1) * NT]),
            waits=[(aV, v_out[(j, 1)])], inc=sS, by=16))

    pr.op("SY", lambda e: e.nop(), waits=[(sSa, sSa.n), (sSb, sSb.n)])

    from contextlib import ExitStack
    with ExitStack() as ctx:
        for s in sems:
            s.h = ctx.enter_context(nc.semaphore(s.name))
        block = ctx.enter_context(nc.Block())

        def runner(key):
            def cb(eng):
                hw = {}
                for (k, fn, waits, inc, by) in pr.ops:
                    if k != key:
                        continue
                    for (s, v) in waits:
                        if hw.get(s.name, 0) < v:
                            eng.wait_ge(s.h, v)
                            hw[s.name] = v
                    inst = fn(eng)
                    if inc is not None and inst is not None:
                        if by is None:
                            inst.then_inc(inc.h)
                        else:
                            inst.then_inc(inc.h, by)
            return cb

        block.sync(runner("SY"))
        block.scalar(runner("SC"))
        block.vector(runner("VE"))
        block.tensor(runner("TE"))
        block.gpsimd(runner("GP"))

    return nc


_NC_CACHE = []


def kernel(**inputs):
    f = np.float32

    def g(name):
        return np.asarray(inputs[name], dtype=f)

    x = g("L0_attn_hidden_states")[0]
    Wq, bq = g("L0_attn_Wq"), g("L0_attn_bq")
    Wk, bk = g("L0_attn_Wk"), g("L0_attn_bk")
    Wv, bv = g("L0_attn_Wv"), g("L0_attn_bv")
    Wo, bo = g("L0_attn_Wo"), g("L0_attn_bo")
    Wf, bf = g("L0_ffn_weight"), g("L0_ffn_bias")
    gw = g("L0_ffn_gate_w")
    ics = g("L0_ffn_in_channel_scale")
    ocs = g("L0_ffn_out_channel_scale")

    def colt(W):
        O = W.shape[0]
        return np.ascontiguousarray(
            W.T.reshape(NCH, P, O // P, P).transpose(2, 1, 0, 3)
        ).reshape(O // P, P, NCH * P)

    wq_t = colt(Wq)
    wo_t = colt(Wo)
    wb_t = colt(Wf)

    def blkt(W):
        return np.ascontiguousarray(
            W.T.reshape(NCH, P, 4, 256).transpose(2, 1, 0, 3)
        ).reshape(4, P, NCH * 256)

    wk_t = blkt(Wk)
    wv_t = blkt(Wv)
    gw_t = np.ascontiguousarray(
        gw.T.reshape(NCH, P, NE).transpose(1, 0, 2)).reshape(P, NCH * NE)
    bqc = np.ascontiguousarray(bq.reshape(NCH, P).T)
    boc = np.ascontiguousarray(bo.reshape(NCH, P).T)
    bfc = np.ascontiguousarray(bf.reshape(NCH, P).T)
    bk2 = np.ascontiguousarray(bk.reshape(1, DK))
    bv2 = np.ascontiguousarray(bv.reshape(1, DK))
    ics = np.ascontiguousarray(np.tile(ics, (16, 1)))
    ocs = np.ascontiguousarray(np.tile(ocs, (16, 1)))
    eyeT = np.zeros((NE, P), dtype=f)
    for e_ in range(NE):
        eyeT[e_, e_::NE] = 1.0

    xT = np.ascontiguousarray(x.T)
    in_maps = []
    for c in range(N_CORES):
        xc = np.ascontiguousarray(
            xT[:, c * NT:(c + 1) * NT].reshape(NCH, P, NT)
            .transpose(1, 0, 2)).reshape(P, NCH * NT)
        in_maps.append({
            "xT": xc, "wq": wq_t, "wo": wo_t, "wb": wb_t,
            "wk": wk_t, "wv": wv_t, "gw": gw_t, "ics": ics, "ocs": ocs,
            "bqc": bqc, "boc": boc, "bfc": bfc, "bk": bk2, "bv": bv2,
            "eyeT": eyeT,
        })

    if not _NC_CACHE:
        _NC_CACHE.append(build_program())
    nc = _NC_CACHE[0]
    res = run_bass_kernel_spmd(nc, in_maps, core_ids=list(range(N_CORES)))

    out = np.empty((8192, D), dtype=f)
    for c in range(N_CORES):
        oc = np.asarray(res.results[c]["outT"], dtype=f)
        out[c * NT:(c + 1) * NT, :] = oc.reshape(D, NT).T
    return out.reshape(1, 8192, D)
